# version 17
# baseline (speedup 1.0000x reference)
import copy
import numpy as np
import ml_dtypes
import concourse.bass as bass
import concourse.mybir as mybir
from concourse.bass_utils import run_bass_kernel_spmd
from concourse.alu_op_type import AluOpType as OP
from contextlib import ExitStack

F32 = mybir.dt.float32
BF16 = mybir.dt.bfloat16
I32 = mybir.dt.int32
AF = mybir.ActivationFunctionType
AXL = mybir.AxisListType.X
ENG = mybir.EngineType

B, S, D, H, DK = 4, 4096, 1024, 16, 64
NE, NI = 8, 2048
NOWN = 2048
NG = 8
QCS = [[0, 512, 3072, 3584], [1024, 1536, 2048, 2560]]
OWN_RANGES = [[(0, 1024), (3072, 4096)], [(1024, 3072)]]
ARENA_SZ = 207872


THIN = True


def build_nc(limit=None, debug=False):
    nc = bass.Bass("TRN2", target_bir_lowering=False)

    xb = nc.dram_tensor("xb", [S, D], F32, kind="ExternalInput").ap()
    role_t = nc.dram_tensor("role", [1, 1], I32, kind="ExternalInput").ap()
    wq2 = nc.dram_tensor("wq2", [D, H * DK], F32, kind="ExternalInput").ap()
    wk2 = nc.dram_tensor("wk2", [D, H * DK], F32, kind="ExternalInput").ap()
    wv2 = nc.dram_tensor("wv2", [D, H * DK], F32, kind="ExternalInput").ap()
    wo_h = nc.dram_tensor("wo", [D, D], F32, kind="ExternalInput").ap()
    w1_h = nc.dram_tensor("w1", [NE * D, NI], F32, kind="ExternalInput").ap()
    w2_h = nc.dram_tensor("w2", [NE * NI, D], F32, kind="ExternalInput").ap()
    wg_h = nc.dram_tensor("wg", [D, NE], F32, kind="ExternalInput").ap()
    bqr_h = nc.dram_tensor("bqr", [1, 8, 128], F32, kind="ExternalInput").ap()
    bkr_h = nc.dram_tensor("bkr", [1, 8, 128], F32, kind="ExternalInput").ap()
    bvr_h = nc.dram_tensor("bvr", [1, 4, 256], F32, kind="ExternalInput").ap()
    bot_h = nc.dram_tensor("bot", [128, 8], F32, kind="ExternalInput").ap()
    b1t_h = nc.dram_tensor("b1t", [128, 8, 16], F32, kind="ExternalInput").ap()
    b2t_h = nc.dram_tensor("b2t", [128, 8, 8], F32, kind="ExternalInput").ap()
    cmask_h = nc.dram_tensor("cmask", [128, 4, 512], BF16, kind="ExternalInput").ap()
    i128_h = nc.dram_tensor("i128", [128, 128], BF16, kind="ExternalInput").ap()
    eyeg_h = nc.dram_tensor("eyeg", [8, 8, 128], BF16, kind="ExternalInput").ap()
    yout = nc.dram_tensor("yout", [NOWN, D], F32, kind="ExternalOutput").ap()
    dbg = nc.dram_tensor("dbg", [128, 40960], F32, kind="ExternalOutput").ap() if debug else None
    x16 = nc.dram_tensor("x16", [S, D], BF16).ap()
    x2f16 = nc.dram_tensor("x2f16", [NG, 128, NOWN], BF16).ap()

    ctx = ExitStack()
    E = ctx.enter_context
    arena = E(nc.sbuf_tensor("arena", [128, ARENA_SZ], mybir.dt.uint8))
    base_addr = nc.lookup_mloc(arena).addr

    def place(name, shape, dtype, off):
        nbytes = int(np.prod(shape[1:])) * mybir.dt.size(dtype)
        assert off % 32 == 0 and off + nbytes <= ARENA_SZ, (name, off, nbytes)
        h = nc.virtual_sbuf_tensor(name, shape, dtype)
        nc._tensor(h.name, shape, dtype, type="SB")
        mloc = nc.lookup_mloc(h)
        mloc.allocated = True
        mloc.addr = base_addr + off
        return h

    R_XLN1, R_BIG, R_CTX, R_WOW2, R_MISC = 0, 65536, 114944, 147712, 180480
    xln1 = place("xln1", [128, NG, S], BF16, R_XLN1)
    x1 = place("x1", [128, NG, NOWN], F32, R_XLN1)
    xfm = place("xfm", [128, 4, NG, 512], BF16, R_CTX)
    k4h = place("k4h", [128, 2, S], BF16, R_BIG)
    xown = place("xown", [128, NG, NOWN], BF16, R_BIG)
    xln2 = place("xln2", [128, NG, NOWN], BF16, R_BIG)
    vt4 = place("vt4", [128, 32, 4, 65], BF16, R_BIG + 16384)
    q4h = place("q4h", [128, 4, NOWN], BF16, R_BIG + 33024)
    h1 = place("h1", [128, 16, 512], BF16, R_BIG + 32768)
    probs = None
    ctxsb = place("ctxsb", [128, 8, NOWN], BF16, R_CTX)
    w1sb = place("w1sb", [128, 8, NI], BF16, R_CTX)
    x2b = place("x2b", [128, NG, NOWN], BF16, R_CTX)
    wosb = place("wosb", [128, 8, 1024], BF16, R_WOW2)
    wqkv = place("wqkv", [128, 3, 8, 256], BF16, R_WOW2 + 16384)
    w2sb = place("w2sb", [128, 16, 1024], BF16, R_WOW2)

    off = [R_MISC]

    def P(name, shape, dtype, at=None):
        o = off[0] if at is None else at
        t = place(name, shape, dtype, o)
        if at is None:
            nbytes = int(np.prod(shape[1:])) * mybir.dt.size(dtype)
            off[0] += (nbytes + 31) // 32 * 32
        return t

    cmask = place("cmask", [128, 4, 512], BF16, R_WOW2 + 28672)
    x2tok = P("x2tok", [128, 2, 1024], BF16)
    probs = place("probs", [128, 2, 1024], BF16, R_MISC)
    abuf = off[0]
    bqr = P("bqr", [1, 8, 128], BF16, at=abuf)
    bkr = P("bkr", [1, 8, 128], BF16, at=abuf + 2048)
    bvr = P("bvr", [1, 4, 256], BF16, at=abuf + 4096)
    reciprow = P("reciprow", [1, 512], BF16, at=abuf + 6144)
    recipb = P("recipb", [64, 512], BF16, at=abuf + 7168)
    colmax = P("colmax", [128, 4], BF16, at=abuf + 8192)
    ytmp = P("ytmp", [128, 2, 512], F32, at=abuf)
    wbsb = P("wbsb", [128, 2, 512], F32, at=abuf + 4096)
    off[0] = abuf + 8224
    i128b = P("i128b", [128, 128], BF16)
    eyeg = P("eyeg", [8, 8, 128], BF16)
    rbuf = off[0]
    rowsf = P("rowsf", [1, 3, 512], F32, at=rbuf)
    rowsb = P("rowsb", [1, 2, 512], BF16, at=rbuf + 6144)
    w_T = P("w_T", [8, NOWN], BF16, at=rbuf)
    off[0] = rbuf + 8192
    ones_row = P("ones_row", [1, 512], BF16)
    ones1 = P("ones1", [1, 128], BF16)
    onesc = P("onesc", [128, 1], BF16)
    onescf = P("onescf", [128, 1], F32)
    negB = P("negB", [128, 1], F32)
    sc1 = P("sc1", [1, 8], F32)
    negbrow = P("negbrow", [1, 1], BF16)
    epsr = P("epsr", [1, 1], F32)
    wgsb = P("wgsb", [128, 8, 8], BF16)
    sbuf0 = off[0]
    sqtmp = P("sqtmp", [128, 2, 512], BF16, at=sbuf0)
    logits = P("logits", [128, 16, 8], F32, at=sbuf0)
    gtE = P("gtE", [128, 16, 8], F32, at=sbuf0 + 512)
    gtK = P("gtK", [128, 16, 8], F32, at=sbuf0 + 1024)
    gcolM = P("gcolM", [128, 16], F32, at=sbuf0 + 1536)
    gcolN = P("gcolN", [128, 16], F32, at=sbuf0 + 1600)
    gcolD = P("gcolD", [128, 16], F32, at=sbuf0 + 1664)
    wtokb = P("wtokb", [128, 16, 8], BF16, at=sbuf0 + 1728)
    off[0] = sbuf0 + 2048
    botsb = P("botsb", [128, 8], F32)
    b1tsb = P("b1tsb", [128, 8, 16], F32)
    b2tsb = P("b2tsb", [128, 8, 8], F32)
    rsb = P("rsb", [1, 1], I32)
    assert off[0] <= ARENA_SZ, off[0]

    psA = E(nc.psum_tensor("psA", [128, 1024], F32, unsafe=True))
    psB = E(nc.psum_tensor("psB", [128, 1024], F32, unsafe=True))
    psC = E(nc.psum_tensor("psC", [128, 512], F32))
    psD = E(nc.psum_tensor("psD", [128, 512], F32))
    psE = E(nc.psum_tensor("psE", [128, 512], F32))
    psF = E(nc.psum_tensor("psF", [128, 512], F32))

    sems = {s: E(nc.semaphore(s)) for s in ["pe", "act", "dve", "pool",
            "g_role", "g_small", "g_wo", "g_wqkv",
            "s_const", "s_xo"]
        + [f"g_x{i}" for i in range(8)] + ["s_xt0", "s_xt1", "s_xt2", "s_xt3"]
        + [f"g_w1q{i}" for i in range(4)] + ["g_w2h0", "g_w2h1"]
        + [f"s_x2s{i}" for i in range(4)] + ["s_ot0", "s_ot1", "g_out0", "g_out1"]}
    TEN, SCA, VEC, GPS, SYN = nc.tensor, nc.scalar, nc.vector, nc.gpsimd, nc.sync
    blk = E(nc.Block())

    ST = {
        "C": {s: 0 for s in sems},
        "ev_ps": {"C": 0, "D": 0},
        "ps_par": 0,
        "scfree": [0, 0],
        "pbfree": [0, 0],
        "u": 0,
        "ctxread": [0, 0],
        "bcread": [0, 0],
        "recip_pe": 0,
        "u_norm": 0,
        "bmach_pe": 0,
        "bmach_ev1": 0,
    }


    def _finish(dumps=()):
        if debug and dumps:
            col = 0
            for ap in dumps:
                n = int(np.prod(ap.shape[1:]))
                GPS.wait_ge(sems["pe"], ST["C"]["pe"])
                GPS.wait_ge(sems["dve"], ST["C"]["dve"])
                GPS.wait_ge(sems["act"], ST["C"]["act"])
                pat = {2: None, 3: "p a b -> p (a b)", 4: "p a b c -> p (a b c)"}[len(ap.shape)]
                flat = ap if pat is None else ap.rearrange(pat)
                I(GPS.dma_start(dbg[0:ap.shape[0], col:col + n], flat), "g_out0", 16)
                col += n
        for sname, cnt in ST["C"].items():
            if sname.startswith("g_") and cnt:
                GPS.wait_ge(sems[sname], cnt)
            if sname.startswith("s_") and cnt:
                SYN.wait_ge(sems[sname], cnt)
        ctx.close()
        return nc
    def I(instr, s, n=1):
        ST["C"][s] += n
        instr.then_inc(sems[s], n)
        return ST["C"][s]

    def W(eng, s, v):
        if v > 0:
            eng.wait_ge(sems[s], v)

    def VF(instr):
        ev = I(instr, "dve", 1)
        VEC.wait_ge(sems["dve"], ev)
        return ev

    def proj_bank():
        b = "C" if ST["ps_par"] % 2 == 0 else "D"
        ST["ps_par"] += 1
        return (psC if b == "C" else psD), b

    def snapshot():
        return copy.deepcopy(ST)

    def restore(s):
        st2 = copy.deepcopy(s)
        ST.clear()
        ST.update(st2)

    ev_role = I(GPS.dma_start(rsb[:, :], role_t), "g_role", 16)
    I(GPS.dma_start(bqr[:, :, :], bqr_h), "g_small", 16)
    I(GPS.dma_start(bkr[:, :, :], bkr_h), "g_small", 16)
    I(GPS.dma_start(bvr[:, :, :], bvr_h), "g_small", 16)
    ev_gsmall = I(GPS.dma_start(wgsb[:, :, :], wg_h.rearrange("(g p) e -> p g e", p=128)), "g_small", 16)

    ev_wqkv = {}

    def load_wqkv(grp):
        cs, ce = grp * 256, (grp + 1) * 256
        I(GPS.dma_start(wqkv[:, 0, :, :], wq2[:, cs:ce].rearrange("(g p) n -> p g n", p=128)), "g_wqkv", 16)
        I(GPS.dma_start(wqkv[:, 1, :, :], wk2[:, cs:ce].rearrange("(g p) n -> p g n", p=128)), "g_wqkv", 16)
        ev_wqkv[grp] = I(GPS.dma_start(wqkv[:, 2, :, :], wv2[:, cs:ce].rearrange("(g p) n -> p g n", p=128)), "g_wqkv", 16)

    load_wqkv(0)
    ev_xc = []
    for tc in range(8):
        ev_xc.append(I(GPS.dma_start(x16[tc * 512:(tc + 1) * 512, :], xb[tc * 512:(tc + 1) * 512, :]), f"g_x{tc}", 16))
    ev_wo = I(GPS.dma_start(wosb[:, :, :], wo_h.rearrange("(P p) n -> p P n", p=128)), "g_wo", 16)

    I(SYN.dma_start(cmask[:, :, :], cmask_h), "s_const", 16)
    I(SYN.dma_start(i128b[:, :], i128_h), "s_const", 16)
    I(SYN.dma_start(eyeg[:, :, :], eyeg_h), "s_const", 16)
    I(SYN.dma_start(botsb[:, :], bot_h), "s_const", 16)
    I(SYN.dma_start(b1tsb[:, :, :], b1t_h), "s_const", 16)
    ev_consts = I(SYN.dma_start(b2tsb[:, :, :], b2t_h), "s_const", 16)

    VEC.memset(ones_row[:, :], 1.0)
    VEC.memset(ones1[:, :], 1.0)
    VEC.memset(onesc[:, :], 1.0)
    VEC.memset(onescf[:, :], 1.0)
    VEC.memset(epsr[:, :], 1e-5)
    VEC.memset(q4h[:, :, :], 0.0)
    ev_vtones = I(VEC.memset(vt4[:, :, :, 64:65], 1.0), "dve", 1)

    for eng in (TEN, SCA, VEC, SYN):
        W(eng, "g_role", ev_role)
    val_t = nc.values_load(rsb[:1, :1], engines=[ENG.PE], skip_runtime_bounds_check=True)
    val_s = nc.values_load(rsb[:1, :1], engines=[ENG.Activation], skip_runtime_bounds_check=True)
    val_v = nc.values_load(rsb[:1, :1], engines=[ENG.DVE], skip_runtime_bounds_check=True)
    val_sp = nc.values_load(rsb[:1, :1], engines=[ENG.SP], skip_runtime_bounds_check=True)

    def ln_chunk(sum_lhsT, sum_rhs_g, sq_src_g, norm_src_g, out_g,
                 pre_mm_wait=(), pre_sq_wait=(), prev=(0, 0), mid_emit=None):
        pre_mm_wait = list(pre_mm_wait) + [("dve", prev[0])]
        pre_sq_wait = list(pre_sq_wait) + [("dve", prev[1])]
        mm_evs = []
        for g in range(NG):
            if g == 0:
                for s, v in pre_sq_wait:
                    W(SCA, s, v)
            if g >= 2:
                W(SCA, "pe", mm_evs[g - 2])
            sq_ev = I(SCA.activation(sqtmp[:, g % 2, :], sq_src_g(g), AF.Square), "act", 1)
            if g == 0:
                for s, v in pre_mm_wait:
                    W(TEN, s, v)
            TEN.matmul(psA[0:1, 0:512], sum_lhsT, sum_rhs_g(g), start=(g == 0), stop=(g == NG - 1))
            W(TEN, "act", sq_ev)
            mm_evs.append(I(TEN.matmul(psA[0:1, 512:1024], onesc[:, :], sqtmp[:, g % 2, :],
                                       start=(g == 0), stop=(g == NG - 1)), "pe", 1))
        ev_mm = mm_evs[-1]
        if mid_emit is not None:
            mid_emit()
        W(VEC, "pe", ev_mm)
        VF(VEC.tensor_scalar(rowsf[:, 0, :], psA[0:1, 0:512], 1.0 / 1024.0, None, OP.mult))
        VF(VEC.tensor_tensor(rowsf[:, 1, :], rowsf[:, 0, :], rowsf[:, 0, :], OP.mult))
        ev_var = VF(VEC.scalar_tensor_tensor(rowsf[:, 2, :], psA[0:1, 512:1024], 1.0 / 1024.0,
                                             rowsf[:, 1, :], OP.mult, OP.subtract))
        W(SCA, "dve", ev_var)
        ev_sd = I(SCA.activation(rowsf[:, 1, :], rowsf[:, 2, :], AF.Sqrt, bias=epsr[:, :]), "act", 1)
        W(VEC, "act", ev_sd)
        VF(VEC.reciprocal(rowsf[:, 2, :], rowsf[:, 1, :]))
        with nc.allow_low_precision(reason="bf16 LN mean/rstd rows feeding broadcast matmul (values computed in fp32)"):
            VEC.tensor_copy(rowsb[:, 0, :], rowsf[:, 0, :])
            ev_rows = I(VEC.tensor_copy(rowsb[:, 1, :], rowsf[:, 2, :]), "dve", 1)
        W(TEN, "dve", ev_rows)
        TEN.matmul(psB[:, 0:512], ones1[:, :], rowsb[:, 0, :], start=True, stop=True)
        ev_bc = I(TEN.matmul(psB[:, 512:1024], ones1[:, :], rowsb[:, 1, :], start=True, stop=True), "pe", 1)
        W(VEC, "pe", ev_bc)
        ev_nrm = None
        for g in range(NG):
            VF(VEC.tensor_tensor(sqtmp[:, g % 2, :], norm_src_g(g), psB[:, 0:512], OP.subtract))
            ev_nrm = I(VEC.tensor_tensor(out_g(g), sqtmp[:, g % 2, :], psB[:, 512:1024], OP.mult), "dve", 1)
        return ev_nrm, ev_var

    def kv_proj_chunk(grp, sc):
        for li in range(2):
            pglob = grp * 2 + li
            ps, bn = proj_bank()
            W(TEN, "dve", ST["ev_ps"][bn])
            if sc == 0 and li == 0:
                W(TEN, "g_wqkv", ev_wqkv[grp])
                W(TEN, "g_small", ev_gsmall)
            for g in range(NG):
                if g == 0 and grp == 0:
                    W(TEN, "dve", ev_ln1[sc])
                TEN.matmul(ps[:, :], wqkv[:, 1, g, li * 128:(li + 1) * 128],
                           xln1[:, g, sc * 512:(sc + 1) * 512], start=(g == 0), stop=False)
            ev = I(TEN.matmul(ps[:, :], bkr[:, pglob, :], ones_row[:, :], start=False, stop=True), "pe", 1)
            W(VEC, "pe", ev)
            ST["ev_ps"][bn] = I(VEC.tensor_copy(k4h[:, li, sc * 512:(sc + 1) * 512], ps[:, :]), "dve", 1)
        for s4 in range(4):
            sc128 = 4 * sc + s4
            ps, bn = proj_bank()
            W(TEN, "dve", ST["ev_ps"][bn])
            for g in range(NG):
                TEN.matmul(ps[:, 0:256], xln1[:, g, sc128 * 128:(sc128 + 1) * 128], wqkv[:, 2, g, :],
                           start=(g == 0), stop=False)
            ev = I(TEN.matmul(ps[:, 0:256], ones1[:, :], bvr[:, grp, :], start=False, stop=True), "pe", 1)
            W(VEC, "pe", ev)
            if sc128 == 0 and grp == 0:
                W(VEC, "dve", ev_vtones)
            ST["ev_ps"][bn] = I(VEC.tensor_copy(vt4[:, sc128, :, 0:64],
                                                ps[:, 0:256].rearrange("p (a b) -> p a b", a=4)), "dve", 1)

    def kv_proj(grp):
        for sc in range(8):
            kv_proj_chunk(grp, sc)
        bmach_k_reduces(grp)

    def q_proj(grp):
        snap = snapshot()
        final0 = None
        for arm in nc.Switch(engines=[TEN, VEC], index=[val_t, val_v], n=2):
            if arm == 1:
                restore(snap)
            qcs = QCS[arm]
            for li in range(2):
                pglob = grp * 2 + li
                for j in range(4):
                    ps, bn = proj_bank()
                    W(TEN, "dve", ST["ev_ps"][bn])
                    for g in range(NG):
                        TEN.matmul(ps[:, :], wqkv[:, 0, g, li * 128:(li + 1) * 128],
                                   xln1[:, g, qcs[j]:qcs[j] + 512], start=(g == 0), stop=False)
                    ev = I(TEN.matmul(ps[:, :], bqr[:, pglob, :], ones_row[:, :],
                                      start=False, stop=True), "pe", 1)
                    W(VEC, "pe", ev)
                    VEC.tensor_scalar(q4h[0:64, 2 * li, j * 512:(j + 1) * 512],
                                      ps[0:64, :], 0.125, None, OP.mult)
                    ST["ev_ps"][bn] = I(VEC.tensor_scalar(q4h[64:128, 2 * li + 1, j * 512:(j + 1) * 512],
                                                          ps[64:128, :], 0.125, None, OP.mult), "dve", 1)
                    bmach_q_step(li, j, li == 0 and j == 0)
            if arm == 0:
                final0 = snapshot()
            else:
                assert ST == final0, ("qproj", grp)

    def bmach_k_reduces(grp):
        W(VEC, "pe", ST.get("bmach_pe", 0))
        VEC.tensor_reduce(colmax[:, 0:1], k4h[:, 0, :], AXL, OP.max, apply_absolute_value=True)
        VF(VEC.tensor_reduce(colmax[:, 1:2], k4h[:, 1, :], AXL, OP.max, apply_absolute_value=True))
        VF(VEC.tensor_tensor(colmax[:, 0:1], colmax[:, 0:1], colmax[:, 1:2], OP.max))

    def bmach_q_step(li, j, first):
        VEC.tensor_reduce(colmax[:, 1:2], q4h[:, 2 * li, j * 512:(j + 1) * 512], AXL, OP.max, apply_absolute_value=True)
        VF(VEC.tensor_reduce(colmax[:, 3:4], q4h[:, 2 * li + 1, j * 512:(j + 1) * 512], AXL, OP.max, apply_absolute_value=True))
        VF(VEC.tensor_tensor(colmax[:, 3:4], colmax[:, 3:4], colmax[:, 1:2], OP.max))
        if first:
            ST["bmach_ev1"] = VF(VEC.tensor_copy(colmax[:, 2:3], colmax[:, 3:4]))
        else:
            ST["bmach_ev1"] = VF(VEC.tensor_tensor(colmax[:, 2:3], colmax[:, 2:3], colmax[:, 3:4], OP.max))

    def b_machinery(grp, act_guard):
        ev1 = ST["bmach_ev1"]
        W(TEN, "dve", ev1)
        W(TEN, "dve", max(ST["ctxread"][0], ST["bcread"][0]))
        TEN.matmul(psE[0:1, 0:128], colmax[:, 0:1], i128b[:, :], start=True, stop=True)
        ev2 = I(TEN.matmul(psE[0:1, 128:256], colmax[:, 2:3], i128b[:, :], start=True, stop=True), "pe", 1)
        ST["bmach_pe"] = ev2
        W(VEC, "pe", ev2)
        VEC.tensor_reduce(sc1[:, 0:1], psE[0:1, 0:128], AXL, OP.max)
        VF(VEC.tensor_reduce(sc1[:, 1:2], psE[0:1, 128:256], AXL, OP.max))
        VF(VEC.tensor_tensor(sc1[:, 2:3], sc1[:, 0:1], sc1[:, 1:2], OP.mult))
        with nc.allow_low_precision(reason="bf16 stability-shift scalar feeding broadcast matmul"):
            ev3 = VF(VEC.tensor_scalar(negbrow[:, :], sc1[:, 2:3], -64.0, 50.0, OP.mult, OP.add))
        W(TEN, "dve", ev3)
        ev4 = I(TEN.matmul(psE[:, 256:257], ones1[:, :], negbrow[:, :], start=True, stop=True), "pe", 1)
        W(VEC, "pe", ev4)
        W(VEC, "act", act_guard)
        ev5 = I(VEC.tensor_copy(negB[:, :], psE[:, 256:257]), "dve", 1)
        ST["ctxread"][0] = max(ST["ctxread"][0], ev5)
        ST["bcread"][0] = max(ST["bcread"][0], ev5)
        return ev5

    def emit_pv(p, pb, ev_exp, mask_ev, hl, npair, ctxps, cb, lo=(0, 0)):
        W(TEN, "act", ev_exp)
        if mask_ev:
            W(TEN, "dve", mask_ev)
        if p == 0:
            W(TEN, "dve", ST["bcread"][cb])
        TEN.matmul(ctxps[0:65, lo[0]:512], vt4[:, 2 * p, hl, :], probs[:, pb, lo[0]:512],
                   start=(p == 0), stop=False)
        ev_pv = I(TEN.matmul(ctxps[0:65, lo[1]:512], vt4[:, 2 * p + 1, hl, :],
                             probs[:, pb, 512 + lo[1]:1024],
                             start=False, stop=(p == npair - 1)), "pe", 1)
        ST["pbfree"][pb] = ev_pv
        return ev_pv

    def emit_norm(cb, ctxps, pglob, half, j, ev_rc):
        bname = "C" if ST["u_norm"] % 2 == 0 else "D"
        bcps = psC if bname == "C" else psD
        ST["u_norm"] += 1
        W(TEN, "dve", ev_rc)
        W(TEN, "dve", ST["ev_ps"][bname])
        ev_bcmm = I(TEN.matmul(bcps[0:64, :], ones1[0:1, 0:64], reciprow[:, :],
                               start=True, stop=True), "pe", 1)
        ST["recip_pe"] = ev_bcmm
        W(VEC, "pe", ev_bcmm)
        ev_bcv = VF(VEC.tensor_copy(recipb[:, :], bcps[0:64, :]))
        ST["ev_ps"][bname] = ev_bcv
        ev_cx = I(VEC.tensor_tensor(ctxsb[64 * half:64 * half + 64, pglob, j * 512:(j + 1) * 512],
                                    ctxps[0:64, :], recipb[:, :], OP.mult), "dve", 1)
        ST["ctxread"][cb] = ev_cx
        ST["bcread"][cb] = ev_cx

    def attn_group(grp, ev_negB):
        snap = snapshot()
        final0 = None
        for arm in nc.Switch(engines=[TEN, SCA, VEC], index=[val_t, val_s, val_v], n=2):
            if arm == 1:
                restore(snap)
            qcs = QCS[arm]
            norm_pending = None
            for hl in range(4):
                li, half = hl // 2, hl % 2
                pglob = grp * 2 + li
                for j in range(4):
                    qb = qcs[j]
                    npair = (qb // 128 + 4) // 2
                    cb = ST["u"] % 2
                    ctxps = psE if cb == 0 else psF
                    pv_pending = None
                    last_pv = 0
                    for p in range(npair):
                        lo = (256, 384) if (p == npair - 1 and THIN) else (0, 0)
                        sbuf = (psA, psB)[p % 2]
                        W(TEN, "act", ST["scfree"][p % 2])
                        TEN.matmul(sbuf[:, lo[0]:512],
                                   k4h[:, li, (2 * p) * 128:(2 * p + 1) * 128],
                                   q4h[:, hl, j * 512 + lo[0]:(j + 1) * 512],
                                   start=True, stop=True)
                        ev_sc = I(TEN.matmul(sbuf[:, 512 + lo[1]:1024],
                                             k4h[:, li, (2 * p + 1) * 128:(2 * p + 2) * 128],
                                             q4h[:, hl, j * 512 + lo[1]:(j + 1) * 512],
                                             start=True, stop=True), "pe", 1)
                        if pv_pending is not None:
                            last_pv = emit_pv(*pv_pending)
                        if p == min(2, npair - 1) and norm_pending is not None:
                            emit_norm(*norm_pending)
                            norm_pending = None
                        pb = p % 2
                        W(SCA, "pe", ev_sc)
                        W(SCA, "pe", ST["pbfree"][pb])
                        if p == 0 and hl == 0 and j == 0:
                            W(SCA, "dve", ev_negB)
                        if lo[0]:
                            SCA.activation(probs[:, pb, lo[0]:512], sbuf[:, lo[0]:512], AF.Exp,
                                           bias=negB[:, :], scale=1.0)
                            ev_exp = I(SCA.activation(probs[:, pb, 512 + lo[1]:1024], sbuf[:, 512 + lo[1]:1024],
                                                      AF.Exp, bias=negB[:, :], scale=1.0), "act", 1)
                        else:
                            ev_exp = I(SCA.activation(probs[:, pb, :], sbuf[:, :], AF.Exp,
                                                      bias=negB[:, :], scale=1.0), "act", 1)
                        ST["scfree"][p % 2] = ev_exp
                        mask_ev = 0
                        if p >= npair - 2:
                            mi0 = (p - (npair - 2)) * 2
                            W(VEC, "act", ev_exp)
                            VEC.tensor_tensor(probs[:, pb, lo[0]:512], probs[:, pb, lo[0]:512],
                                              cmask[:, mi0, lo[0]:512], OP.mult)
                            mask_ev = I(VEC.tensor_tensor(probs[:, pb, 512 + lo[1]:1024],
                                                          probs[:, pb, 512 + lo[1]:1024],
                                                          cmask[:, mi0 + 1, lo[1]:512], OP.mult), "dve", 1)
                        pv_pending = (p, pb, ev_exp, mask_ev, hl, npair, ctxps, cb, lo)
                    last_pv = emit_pv(*pv_pending)
                    W(VEC, "pe", last_pv)
                    W(VEC, "pe", ST["recip_pe"])
                    with nc.allow_low_precision(reason="bf16 softmax 1/sum row feeding broadcast matmul"):
                        ev_rc = VF(VEC.reciprocal(reciprow[:, :], ctxps[64:65, :]))
                    norm_pending = (cb, ctxps, pglob, half, j, ev_rc)
                    ST["u"] += 1
            if norm_pending is not None:
                emit_norm(*norm_pending)
                norm_pending = None
            if arm == 0:
                final0 = snapshot()
            else:
                assert ST == final0, ("attn", grp)

    prev_norm = [0, 0, 0, 0]
    ev_ln1 = []
    ln_prev = (0, 0)
    for tc in range(8):
        buf = tc % 4
        W(SYN, f"g_x{tc}", ev_xc[tc])
        W(SYN, "dve", prev_norm[buf])
        evt = None
        for g in range(NG):
            evt = I(SYN.dma_start(xfm[:, buf, g, :],
                                  x16[tc * 512:(tc + 1) * 512, g * 128:(g + 1) * 128],
                                  transpose=True), f"s_xt{tc % 4}", 16)
        src = (lambda g, b=buf: xfm[:, b, g, :])
        pw = [(f"s_xt{tc % 4}", evt)] + ([("s_const", ev_consts), ("dve", ev_vtones)] if tc == 0 else [])
        env, evv = ln_chunk(onesc[:, :], src, src, src,
                            (lambda g, t=tc: xln1[:, g, t * 512:(t + 1) * 512]),
                            pre_mm_wait=pw, pre_sq_wait=[(f"s_xt{tc % 4}", evt)], prev=ln_prev,
                            mid_emit=(lambda t=tc: kv_proj_chunk(0, t - 1)) if tc >= 1 else None)
        ln_prev = (evv, env)
        prev_norm[buf] = env
        ev_ln1.append(env)
    if limit == "ln1":
        return _finish([xln1[:, :, :]])

    act_guard = 0
    kv_proj_chunk(0, 7)
    bmach_k_reduces(0)
    for grp in range(4):
        if grp > 0:
            kv_proj(grp)
        q_proj(grp)
        if grp < 3:
            W(GPS, "pe", ST["C"]["pe"])
            load_wqkv(grp + 1)
        if limit == "proj0":
            return _finish([k4h[:, :, :], q4h[:, :, :], vt4[:, :, :, :]])
        ev_negB = b_machinery(grp, act_guard)
        if limit == "bmach0":
            return _finish([negB[:, :]])
        attn_group(grp, ev_negB)
        act_guard = ST["C"]["act"]
        if limit == "attn0":
            return _finish([ctxsb[:, :, :]])

    if limit == "attn":
        return _finish([ctxsb[:, :, :]])
    ev_attn_pe_end = ST["C"]["pe"]
    ev_attn_act_end = ST["C"]["act"]
    ev_attn_dve_end = ST["C"]["dve"]
    ev_ctx_done = max(max(ST["ctxread"]), max(ST["bcread"]))

    W(SYN, "pe", ev_attn_pe_end)
    snap = snapshot()
    final0 = None
    for arm in nc.Switch(engines=[SYN], index=[val_sp], n=2):
        if arm == 1:
            restore(snap)
        loc = 0
        parts = []
        for (a, b) in OWN_RANGES[arm]:
            if b - a > 1024:
                m = (a + b) // 2
                parts += [(a, m), (m, b)]
            else:
                parts += [(a, b)]
        for (a, b) in parts:
            for g in range(NG):
                I(SYN.dma_start(xown[:, g, loc:loc + (b - a)],
                                x16[a:b, g * 128:(g + 1) * 128], transpose=True), "s_xo", 16)
            loc += b - a
        if arm == 0:
            final0 = snapshot()
        else:
            assert ST == final0, "xown"
    ev_xown = ST["C"]["s_xo"]

    ev_wo_tc = [None] * 4
    ev_ln2 = [None] * 4
    first_wo = [True]

    def wo_chunk(tc):
        ev_evac = None
        for dc in range(8):
            ps, bn = proj_bank()
            W(TEN, "dve", ST["ev_ps"][bn])
            mm = None
            for pg in range(8):
                if first_wo[0] and pg == 0:
                    W(TEN, "g_wo", ev_wo)
                    W(TEN, "dve", ev_ctx_done)
                    first_wo[0] = False
                mm = TEN.matmul(ps[:, :], wosb[:, pg, dc * 128:(dc + 1) * 128],
                                ctxsb[:, pg, tc * 512:(tc + 1) * 512],
                                start=(pg == 0), stop=(pg == 7))
            ev = I(mm, "pe", 1)
            W(VEC, "pe", ev)
            if tc == 0 and dc == 0:
                W(VEC, "s_xo", ev_xown)
            ev_evac = I(VEC.scalar_tensor_tensor(x1[:, dc, tc * 512:(tc + 1) * 512], ps[:, :],
                                                 botsb[:, dc:dc + 1],
                                                 xown[:, dc, tc * 512:(tc + 1) * 512],
                                                 OP.add, OP.add), "dve", 1)
            ST["ev_ps"][bn] = ev_evac
        ev_wo_tc[tc] = ev_evac

    def gating_logits(tjs):
        for tj in tjs:
            ps, bn = proj_bank()
            W(TEN, "dve", ST["ev_ps"][bn])
            if tj == 0:
                W(TEN, "g_small", ev_gsmall)
            W(TEN, "dve", ev_ln2[tj // 4])
            mm = None
            for g in range(NG):
                mm = TEN.matmul(ps[:, 0:8], xln2[:, g, tj * 128:(tj + 1) * 128], wgsb[:, g, :],
                                start=(g == 0), stop=(g == 7))
            ev = I(mm, "pe", 1)
            W(VEC, "pe", ev)
            ev_lg = I(VEC.tensor_copy(logits[:, tj, :], ps[:, 0:8]), "dve", 1)
            ST["ev_ps"][bn] = ev_lg

    wo_chunk(0)
    for tc in range(4):
        pw = [("dve", ev_wo_tc[tc])]
        if tc == 0:
            pw.append(("act", ev_attn_act_end))
        src_f32 = (lambda g, t=tc: x1[:, g, t * 512:(t + 1) * 512])
        mid = (lambda t=tc: wo_chunk(t + 1)) if tc < 3 else None
        env, evv = ln_chunk(onescf[:, :], src_f32, src_f32, src_f32,
                            (lambda g, t=tc: xln2[:, g, t * 512:(t + 1) * 512]),
                            pre_mm_wait=pw, pre_sq_wait=[("dve", ev_wo_tc[tc])], prev=ln_prev,
                            mid_emit=mid)
        ln_prev = (evv, env)
        ev_ln2[tc] = env

    if limit == "wo":
        return _finish([x1[:, :, :]])

    gating_logits(range(0, 16))
    W(VEC, "dve", max(ST["ev_ps"]["C"], ST["ev_ps"]["D"]))
    def stage_fence(instr):
        ev = I(instr, "dve", 1)
        VEC.wait_ge(sems["dve"], ev)
        return ev
    for tj in range(15):
        VEC.tensor_reduce(gcolM[:, tj:tj + 1], logits[:, tj, :], AXL, OP.max)
    stage_fence(VEC.tensor_reduce(gcolM[:, 15:16], logits[:, 15, :], AXL, OP.max))
    ev_m1 = stage_fence(VEC.tensor_scalar(gcolN[:, :], gcolM[:, :], -1.0, None, OP.mult))
    W(SCA, "dve", ev_m1)
    ev_e = None
    for tj in range(16):
        ev_e = I(SCA.activation(gtE[:, tj, :], logits[:, tj, :], AF.Exp, bias=gcolN[:, tj:tj + 1]), "act", 1)
    for tj in range(15):
        VEC.tensor_scalar(gtK[:, tj, :], logits[:, tj, :], gcolM[:, tj:tj + 1], None, OP.is_ge)
    stage_fence(VEC.tensor_scalar(gtK[:, 15, :], logits[:, 15, :], gcolM[:, 15:16], None, OP.is_ge))
    for tj in range(15):
        VEC.scalar_tensor_tensor(gtK[:, tj, :], gtK[:, tj, :], -1e9, logits[:, tj, :], OP.mult, OP.add)
    stage_fence(VEC.scalar_tensor_tensor(gtK[:, 15, :], gtK[:, 15, :], -1e9, logits[:, 15, :], OP.mult, OP.add))
    for tj in range(15):
        VEC.tensor_reduce(gcolM[:, tj:tj + 1], gtK[:, tj, :], AXL, OP.max)
    stage_fence(VEC.tensor_reduce(gcolM[:, 15:16], gtK[:, 15, :], AXL, OP.max))
    for tj in range(15):
        VEC.tensor_scalar(gtK[:, tj, :], logits[:, tj, :], gcolM[:, tj:tj + 1], None, OP.is_ge)
    stage_fence(VEC.tensor_scalar(gtK[:, 15, :], logits[:, 15, :], gcolM[:, 15:16], None, OP.is_ge))
    W(VEC, "act", ev_e)
    for tj in range(15):
        VEC.tensor_tensor(gtE[:, tj, :], gtE[:, tj, :], gtK[:, tj, :], OP.mult)
    stage_fence(VEC.tensor_tensor(gtE[:, 15, :], gtE[:, 15, :], gtK[:, 15, :], OP.mult))
    for tj in range(15):
        VEC.tensor_reduce(gcolD[:, tj:tj + 1], gtE[:, tj, :], AXL, OP.add)
    stage_fence(VEC.tensor_reduce(gcolD[:, 15:16], gtE[:, 15, :], AXL, OP.add))
    stage_fence(VEC.reciprocal(gcolN[:, :], gcolD[:, :]))
    ev_wtok = []
    with nc.allow_low_precision(reason="bf16 gating weights for broadcast matmul (top-2 selection done in fp32)"):
        for tj in range(16):
            ev_wtok.append(I(VEC.tensor_scalar(wtokb[:, tj, :], gtE[:, tj, :], gcolN[:, tj:tj + 1], None, OP.mult), "dve", 1))

    ev_wT = None
    for tj in range(16):
        ps, bn = proj_bank()
        W(TEN, "dve", ST["ev_ps"][bn])
        W(TEN, "dve", ev_wtok[tj])
        ev = I(TEN.matmul(ps[0:8, 0:128], wtokb[:, tj, :], i128b[:, :], start=True, stop=True), "pe", 1)
        W(VEC, "pe", ev)
        ev_wT = I(VEC.tensor_copy(w_T[:, tj * 128:(tj + 1) * 128], ps[0:8, 0:128]), "dve", 1)
        ST["ev_ps"][bn] = ev_wT

    if limit == "ln2":
        return _finish([xln2[:, :, :], w_T[:, :]])
    ev_wo_mm_done = ST["C"]["pe"]
    ev_w1l = {}
    ev_w2l = {}

    def load_w1(e, quarters=(0, 1, 2, 3), wait_pe=0):
        W(GPS, "pe", wait_pe)
        evs = []
        for q in quarters:
            evs.append(I(GPS.dma_start(
                w1sb[:, :, q * 512:(q + 1) * 512],
                w1_h[e * D:(e + 1) * D, q * 512:(q + 1) * 512].rearrange("(r p) n -> p r n", p=128)), f"g_w1q{q}", 16))
        ev_w1l[e] = evs

    def load_w2(e, wait_pe=0):
        W(GPS, "pe", wait_pe)
        evs = []
        for hh in range(2):
            evs.append(I(GPS.dma_start(
                w2sb[:, hh * 8:(hh + 1) * 8, :],
                w2_h[e * NI + hh * 1024:e * NI + (hh + 1) * 1024, :].rearrange("(r p) n -> p r n", p=128)), f"g_w2h{hh}", 16))
        ev_w2l[e] = evs

    load_w1(0, wait_pe=ev_wo_mm_done)
    load_w2(0, wait_pe=ev_wo_mm_done)

    H1B = [(psA, 0), (psA, 512), (psB, 0), (psB, 512)]
    h1_gelu_ev = [0, 0, 0, 0]
    h1s = 0
    ev_y_tc_prev = 0
    ev_pool = [0, 0]
    ev_wb = [0, 0]
    wb_par = 0
    ev_gelu_tc = 0
    ev_w1_lastmm = 0
    ev_w2_lastmm = 0
    ev_castout = [0, 0]
    ev_pool_tc = [0, 0, 0, 0]
    ln2_act_end = ST["C"]["act"]

    n_experts = 1 if limit == "ffn1" else NE
    for e in range(n_experts):
        for tc in range(4):
            first_gelu = True
            for ic in range(16):
                ps, colo = H1B[h1s % 4]
                W(TEN, "act", h1_gelu_ev[h1s % 4])
                if e == 0 and tc == 0 and ic == 0:
                    W(TEN, "act", ln2_act_end)
                    W(TEN, "dve", ev_ln2[3])
                mm = None
                for r in range(NG):
                    if ic in (0, 8) and r == 0:
                        W(TEN, f"g_w1q{0 if ic < 8 else 2}", ev_w1l[e][0 if ic < 8 else 2])
                    if ic in (4, 12) and r == 0:
                        W(TEN, f"g_w1q{1 if ic < 8 else 3}", ev_w1l[e][1 if ic < 8 else 3])
                    mm = TEN.matmul(ps[:, colo:colo + 512], w1sb[:, r, ic * 128:(ic + 1) * 128],
                                    xln2[:, r, tc * 512:(tc + 1) * 512],
                                    start=(r == 0), stop=(r == 7))
                ev_mm = I(mm, "pe", 1)
                W(SCA, "pe", ev_mm)
                if first_gelu:
                    W(SCA, "pe", ev_y_tc_prev)
                    first_gelu = False
                h1_gelu_ev[h1s % 4] = I(SCA.activation(h1[:, ic, :], ps[:, colo:colo + 512], AF.Gelu,
                                                       bias=b1tsb[:, e, ic:ic + 1]), "act", 1)
                h1s += 1
            ev_gelu_tc = h1_gelu_ev[(h1s - 1) % 4]
            if tc == 3:
                ev_w1_lastmm = ST["C"]["pe"]
                if e < NE - 1:
                    load_w1(e + 1, wait_pe=ev_w1_lastmm)
            wps = (psE, psF)[wb_par % 2]
            W(TEN, "dve", ev_wb[wb_par % 2])
            if e == 0 and tc == 0:
                W(TEN, "dve", max(ST["ctxread"][0], ST["ctxread"][1], ST["bcread"][0], ST["bcread"][1], ev_wT))
            else:
                W(TEN, "dve", ev_wT)
            ev = I(TEN.matmul(wps[:, :], eyeg[:, e, :], w_T[:, tc * 512:(tc + 1) * 512],
                              start=True, stop=True), "pe", 1)
            W(VEC, "pe", ev)
            ev_wb[wb_par % 2] = VF(VEC.tensor_copy(wbsb[:, tc % 2, :], wps[:, :]))
            ev_wb_cur = ev_wb[wb_par % 2]
            wb_par += 1
            for dc in range(8):
                ps, bn = proj_bank()
                W(TEN, "dve", ST["ev_ps"][bn])
                if dc == 0:
                    W(TEN, "act", ev_gelu_tc)
                    W(TEN, "g_w2h1", ev_w2l[e][1])
                    W(TEN, "g_w2h0", ev_w2l[e][0])
                mm = None
                for r in range(16):
                    mm = TEN.matmul(ps[:, :], w2sb[:, r, dc * 128:(dc + 1) * 128], h1[:, r, :],
                                    start=(r == 0), stop=(r == 15))
                ev_y = I(mm, "pe", 1)
                yb = dc % 2
                W(VEC, "pe", ev_y)
                W(VEC, "pool", ev_pool[yb])
                ev_stt = I(VEC.scalar_tensor_tensor(ytmp[:, yb, :], ps[:, :], b2tsb[:, e, dc:dc + 1],
                                                    wbsb[:, tc % 2, :], OP.add, OP.mult), "dve", 1)
                ST["ev_ps"][bn] = ev_stt
                W(GPS, "dve", ev_stt)
                ev_pool[yb] = I(GPS.tensor_tensor(x1[:, dc, tc * 512:(tc + 1) * 512],
                                                  x1[:, dc, tc * 512:(tc + 1) * 512],
                                                  ytmp[:, yb, :], OP.add), "pool", 1)
            ev_y_tc_prev = ST["C"]["pe"]
            if tc == 3:
                ev_w2_lastmm = ST["C"]["pe"]
                if e < NE - 1:
                    load_w2(e + 1, wait_pe=ev_w2_lastmm)
            if e == NE - 1:
                ev_pool_tc[tc] = max(ev_pool)

    if limit in ("ffn", "ffn1"):
        return _finish([x1[:, :, :], h1[:, :, :], wbsb[:, :, :]])

    for tc in range(4):
        W(VEC, "pool", ev_pool_tc[tc])
        if tc == 0:
            W(VEC, "pe", ev_w1_lastmm)
        ev_x2b = I(VEC.tensor_copy(x2b[:, :, tc * 512:(tc + 1) * 512],
                                   x1[:, :, tc * 512:(tc + 1) * 512]), "dve", 1)
        W(SYN, "dve", ev_x2b)
        if tc == 0:
            W(SYN, "dve", ev_attn_dve_end)
            W(SYN, "pe", ev_attn_pe_end)
        ev_sp1 = I(SYN.dma_start(
            x2f16.rearrange("g p t -> p g t")[:, :, tc * 512:(tc + 1) * 512],
            x2b[:, :, tc * 512:(tc + 1) * 512]), f"s_x2s{tc}", 16)
        for tj4 in range(4):
            tj = tc * 4 + tj4
            ob = tj % 2
            W(SYN, f"s_x2s{tc}", ev_sp1)
            W(SYN, f"g_out{ob}", ev_castout[ob])
            ev_tr = I(SYN.dma_start(
                x2tok[:, ob, :],
                x2f16.rearrange("g p t -> (g p) t")[:, tj * 128:(tj + 1) * 128],
                transpose=True), f"s_ot{ob}", 16)
            W(GPS, f"s_ot{ob}", ev_tr)
            ev_castout[ob] = I(GPS.dma_start(yout[tj * 128:(tj + 1) * 128, :],
                                             x2tok[:, ob, :]), f"g_out{ob}", 16)
    for sname, cnt in ST["C"].items():
        if sname.startswith("g_") and cnt:
            GPS.wait_ge(sems[sname], cnt)
        if sname.startswith("s_") and cnt:
            SYN.wait_ge(sems[sname], cnt)

    ctx.close()
    return nc


_NC_CACHE = {}


def _get_nc():
    if "nc" not in _NC_CACHE:
        _NC_CACHE["nc"] = build_nc()
    return _NC_CACHE["nc"]


def kernel(L0_attn_embeddings, L0_attn_Wq, L0_attn_bq, L0_attn_Wk, L0_attn_bk,
           L0_attn_Wv, L0_attn_bv, L0_attn_Wo, L0_attn_bo,
           L0_ffn_Wg, L0_ffn_W1, L0_ffn_b1, L0_ffn_W2, L0_ffn_b2):
    f32 = np.float32
    x = np.ascontiguousarray(np.asarray(L0_attn_embeddings, dtype=f32))
    Wq = np.asarray(L0_attn_Wq, dtype=f32)
    Wk = np.asarray(L0_attn_Wk, dtype=f32)
    Wv = np.asarray(L0_attn_Wv, dtype=f32)
    bq = np.asarray(L0_attn_bq, dtype=f32)
    bk = np.asarray(L0_attn_bk, dtype=f32)
    bv = np.asarray(L0_attn_bv, dtype=f32)
    Wo = np.ascontiguousarray(np.asarray(L0_attn_Wo, dtype=f32))
    bo = np.asarray(L0_attn_bo, dtype=f32)
    Wg = np.ascontiguousarray(np.asarray(L0_ffn_Wg, dtype=f32))
    W1 = np.asarray(L0_ffn_W1, dtype=f32)
    W2 = np.asarray(L0_ffn_W2, dtype=f32)
    b1 = np.asarray(L0_ffn_b1, dtype=f32)
    b2 = np.asarray(L0_ffn_b2, dtype=f32)

    wq2 = np.ascontiguousarray(Wq.transpose(1, 0, 2).reshape(D, H * DK))
    wk2 = np.ascontiguousarray(Wk.transpose(1, 0, 2).reshape(D, H * DK))
    wv2 = np.ascontiguousarray(Wv.transpose(1, 0, 2).reshape(D, H * DK))
    w1f = np.ascontiguousarray(W1.reshape(NE * D, NI))
    w2f = np.ascontiguousarray(W2.reshape(NE * NI, D))
    bqr = np.ascontiguousarray(bq.reshape(1, 8, 128))
    bkr = np.ascontiguousarray(bk.reshape(1, 8, 128))
    bvr = np.ascontiguousarray(bv.reshape(1, 4, 256))
    bot = np.ascontiguousarray(bo.reshape(8, 128).T)
    b1t = np.ascontiguousarray(b1.reshape(8, 16, 128).transpose(2, 0, 1))
    b2t = np.ascontiguousarray(b2.reshape(8, 8, 128).transpose(2, 0, 1))

    s_idx = np.arange(128)[:, None]
    q_idx = np.arange(512)[None, :]
    cmask = np.stack([(128 * i + s_idx <= q_idx) for i in range(4)], axis=1).astype(ml_dtypes.bfloat16)
    cmask = np.ascontiguousarray(cmask)
    i128 = np.eye(128, dtype=ml_dtypes.bfloat16)
    eyeg = np.zeros((8, 8, 128), dtype=ml_dtypes.bfloat16)
    for e in range(8):
        eyeg[e, e, :] = 1
    shared = dict(wq2=wq2, wk2=wk2, wv2=wv2, wo=Wo, w1=w1f, w2=w2f, wg=Wg,
                  bqr=bqr, bkr=bkr, bvr=bvr, bot=bot, b1t=b1t, b2t=b2t,
                  cmask=cmask, i128=i128, eyeg=eyeg)

    nc = _get_nc()
    in_maps = []
    for c in range(8):
        m = dict(shared)
        m["xb"] = np.ascontiguousarray(x[c // 2])
        m["role"] = np.array([[c % 2]], dtype=np.int32)
        in_maps.append(m)
    res = run_bass_kernel_spmd(nc, in_maps, core_ids=list(range(8)))

    out = np.empty((B, S, D), dtype=f32)
    for c in range(8):
        y = res.results[c]["yout"]
        b, role = c // 2, c % 2
        loc = 0
        for (a, e2) in OWN_RANGES[role]:
            out[b, a:e2, :] = y[loc:loc + (e2 - a), :]
            loc += e2 - a
    return out


# version 18
# speedup vs baseline: 1.1717x; 1.1717x over previous
import copy
import numpy as np
import ml_dtypes
import concourse.bass as bass
import concourse.mybir as mybir
from concourse.bass_utils import run_bass_kernel_spmd
from concourse.alu_op_type import AluOpType as OP
from contextlib import ExitStack

F32 = mybir.dt.float32
BF16 = mybir.dt.bfloat16
I32 = mybir.dt.int32
AF = mybir.ActivationFunctionType
AXL = mybir.AxisListType.X
ENG = mybir.EngineType

B, S, D, H, DK = 4, 4096, 1024, 16, 64
NE, NI = 8, 2048
NOWN = 2048
NG = 8
QCS = [[0, 512, 3072, 3584], [1024, 1536, 2048, 2560]]
OWN_RANGES = [[(0, 1024), (3072, 4096)], [(1024, 3072)]]
ARENA_SZ = 207872


THIN = True


def build_nc(limit=None, debug=False):
    nc = bass.Bass("TRN2", target_bir_lowering=False)

    xb = nc.dram_tensor("xb", [S, D], F32, kind="ExternalInput").ap()
    role_t = nc.dram_tensor("role", [1, 1], I32, kind="ExternalInput").ap()
    wq2 = nc.dram_tensor("wq2", [D, H * DK], F32, kind="ExternalInput").ap()
    wk2 = nc.dram_tensor("wk2", [D, H * DK], F32, kind="ExternalInput").ap()
    wv2 = nc.dram_tensor("wv2", [D, H * DK], F32, kind="ExternalInput").ap()
    wo_h = nc.dram_tensor("wo", [D, D], F32, kind="ExternalInput").ap()
    w1_h = nc.dram_tensor("w1", [NE * D, NI], F32, kind="ExternalInput").ap()
    w2_h = nc.dram_tensor("w2", [NE * NI, D], F32, kind="ExternalInput").ap()
    wg_h = nc.dram_tensor("wg", [D, NE], F32, kind="ExternalInput").ap()
    bqr_h = nc.dram_tensor("bqr", [1, 8, 128], F32, kind="ExternalInput").ap()
    bkr_h = nc.dram_tensor("bkr", [1, 8, 128], F32, kind="ExternalInput").ap()
    bvr_h = nc.dram_tensor("bvr", [1, 4, 256], F32, kind="ExternalInput").ap()
    bot_h = nc.dram_tensor("bot", [128, 8], F32, kind="ExternalInput").ap()
    b1t_h = nc.dram_tensor("b1t", [128, 8, 16], F32, kind="ExternalInput").ap()
    b2t_h = nc.dram_tensor("b2t", [128, 8, 8], F32, kind="ExternalInput").ap()
    cmask_h = nc.dram_tensor("cmask", [128, 4, 512], BF16, kind="ExternalInput").ap()
    i128_h = nc.dram_tensor("i128", [128, 128], BF16, kind="ExternalInput").ap()
    eyeg_h = nc.dram_tensor("eyeg", [8, 8, 128], BF16, kind="ExternalInput").ap()
    yout = nc.dram_tensor("yout", [NOWN, D], F32, kind="ExternalOutput").ap()
    dbg = nc.dram_tensor("dbg", [128, 40960], F32, kind="ExternalOutput").ap() if debug else None
    x16 = nc.dram_tensor("x16", [S, D], BF16).ap()
    x2f16 = nc.dram_tensor("x2f16", [NG, 128, NOWN], BF16).ap()

    ctx = ExitStack()
    E = ctx.enter_context
    arena = E(nc.sbuf_tensor("arena", [128, ARENA_SZ], mybir.dt.uint8))
    base_addr = nc.lookup_mloc(arena).addr

    def place(name, shape, dtype, off):
        nbytes = int(np.prod(shape[1:])) * mybir.dt.size(dtype)
        assert off % 32 == 0 and off + nbytes <= ARENA_SZ, (name, off, nbytes)
        h = nc.virtual_sbuf_tensor(name, shape, dtype)
        nc._tensor(h.name, shape, dtype, type="SB")
        mloc = nc.lookup_mloc(h)
        mloc.allocated = True
        mloc.addr = base_addr + off
        return h

    R_XLN1, R_BIG, R_CTX, R_WOW2, R_MISC = 0, 65536, 114944, 147712, 180480
    xln1 = place("xln1", [128, NG, S], BF16, R_XLN1)
    x1 = place("x1", [128, NG, NOWN], F32, R_XLN1)
    xfm = place("xfm", [128, 4, NG, 512], BF16, R_CTX)
    k4h = place("k4h", [128, 2, S], BF16, R_BIG)
    xown = place("xown", [128, NG, NOWN], BF16, R_BIG)
    xln2 = place("xln2", [128, NG, NOWN], BF16, R_BIG)
    vt4 = place("vt4", [128, 32, 4, 65], BF16, R_BIG + 16384)
    q4h = place("q4h", [128, 4, NOWN], BF16, R_BIG + 33024)
    h1 = place("h1", [128, 16, 512], BF16, R_BIG + 32768)
    probs = None
    ctxsb = place("ctxsb", [128, 8, NOWN], BF16, R_CTX)
    w1sb = place("w1sb", [128, 8, NI], BF16, R_CTX)
    x2b = place("x2b", [128, NG, NOWN], BF16, R_CTX)
    wosb = place("wosb", [128, 8, 1024], BF16, R_WOW2)
    wqkv = place("wqkv", [128, 3, 8, 256], BF16, R_WOW2 + 16384)
    w2sb = place("w2sb", [128, 16, 1024], BF16, R_WOW2)

    off = [R_MISC]

    def P(name, shape, dtype, at=None):
        o = off[0] if at is None else at
        t = place(name, shape, dtype, o)
        if at is None:
            nbytes = int(np.prod(shape[1:])) * mybir.dt.size(dtype)
            off[0] += (nbytes + 31) // 32 * 32
        return t

    cmask = place("cmask", [128, 4, 512], BF16, R_WOW2 + 28672)
    x2tok = P("x2tok", [128, 2, 1024], BF16)
    probs = place("probs", [128, 2, 1024], BF16, R_MISC)
    abuf = off[0]
    bqr = P("bqr", [1, 8, 128], BF16, at=abuf)
    bkr = P("bkr", [1, 8, 128], BF16, at=abuf + 2048)
    bvr = P("bvr", [1, 4, 256], BF16, at=abuf + 4096)
    reciprow = P("reciprow", [1, 512], BF16, at=abuf + 6144)
    recipb = P("recipb", [64, 512], BF16, at=abuf + 7168)
    colmax = P("colmax", [128, 4], BF16, at=abuf + 8192)
    ytmp = P("ytmp", [128, 2, 512], F32, at=abuf)
    wbsb = P("wbsb", [128, 2, 512], F32, at=abuf + 4096)
    off[0] = abuf + 8224
    i128b = P("i128b", [128, 128], BF16)
    eyeg = P("eyeg", [8, 8, 128], BF16)
    rbuf = off[0]
    rowsf = P("rowsf", [1, 3, 512], F32, at=rbuf)
    rowsb = P("rowsb", [1, 2, 512], BF16, at=rbuf + 6144)
    w_T = P("w_T", [8, NOWN], BF16, at=rbuf)
    off[0] = rbuf + 8192
    ones_row = P("ones_row", [1, 512], BF16)
    ones1 = P("ones1", [1, 128], BF16)
    onesc = P("onesc", [128, 1], BF16)
    onescf = P("onescf", [128, 1], F32)
    negB = P("negB", [128, 1], F32)
    sc1 = P("sc1", [1, 8], F32)
    negbrow = P("negbrow", [1, 1], BF16)
    epsr = P("epsr", [1, 1], F32)
    wgsb = P("wgsb", [128, 8, 8], BF16)
    sbuf0 = off[0]
    sqtmp = P("sqtmp", [128, 2, 512], BF16, at=sbuf0)
    logits = P("logits", [128, 16, 8], F32, at=sbuf0)
    gtE = P("gtE", [128, 16, 8], F32, at=sbuf0 + 512)
    gtK = P("gtK", [128, 16, 8], F32, at=sbuf0 + 1024)
    gcolM = P("gcolM", [128, 16], F32, at=sbuf0 + 1536)
    gcolN = P("gcolN", [128, 16], F32, at=sbuf0 + 1600)
    gcolD = P("gcolD", [128, 16], F32, at=sbuf0 + 1664)
    wtokb = P("wtokb", [128, 16, 8], BF16, at=sbuf0 + 1728)
    off[0] = sbuf0 + 2048
    botsb = P("botsb", [128, 8], F32)
    b1tsb = P("b1tsb", [128, 8, 16], F32)
    b2tsb = P("b2tsb", [128, 8, 8], F32)
    rsb = P("rsb", [1, 1], I32)
    assert off[0] <= ARENA_SZ, off[0]

    psA = E(nc.psum_tensor("psA", [128, 1024], F32, unsafe=True))
    psB = E(nc.psum_tensor("psB", [128, 1024], F32, unsafe=True))
    psC = E(nc.psum_tensor("psC", [128, 512], F32))
    psD = E(nc.psum_tensor("psD", [128, 512], F32))
    psE = E(nc.psum_tensor("psE", [128, 512], F32))
    psF = E(nc.psum_tensor("psF", [128, 512], F32))

    sems = {s: E(nc.semaphore(s)) for s in ["pe", "act", "dve", "pool",
            "g_role", "g_small", "g_wo", "g_wqkv",
            "s_const", "s_xo"]
        + [f"g_x{i}" for i in range(8)] + ["s_xt0", "s_xt1", "s_xt2", "s_xt3"]
        + [f"g_w1q{i}" for i in range(4)] + ["g_w2h0", "g_w2h1"]
        + [f"s_x2s{i}" for i in range(4)] + ["s_ot0", "s_ot1", "g_out0", "g_out1"]}
    TEN, SCA, VEC, GPS, SYN = nc.tensor, nc.scalar, nc.vector, nc.gpsimd, nc.sync
    blk = E(nc.Block())

    ST = {
        "C": {s: 0 for s in sems},
        "ev_ps": {"C": 0, "D": 0},
        "ps_par": 0,
        "scfree": [0, 0],
        "pbfree": [0, 0],
        "u": 0,
        "ctxread": [0, 0],
        "bcread": [0, 0],
        "recip_pe": 0,
        "u_norm": 0,
        "bmach_pe": 0,
        "bmach_ev1": 0,
    }


    def _finish(dumps=()):
        if debug and dumps:
            col = 0
            for ap in dumps:
                n = int(np.prod(ap.shape[1:]))
                GPS.wait_ge(sems["pe"], ST["C"]["pe"])
                GPS.wait_ge(sems["dve"], ST["C"]["dve"])
                GPS.wait_ge(sems["act"], ST["C"]["act"])
                pat = {2: None, 3: "p a b -> p (a b)", 4: "p a b c -> p (a b c)"}[len(ap.shape)]
                flat = ap if pat is None else ap.rearrange(pat)
                I(GPS.dma_start(dbg[0:ap.shape[0], col:col + n], flat), "g_out0", 16)
                col += n
        for sname, cnt in ST["C"].items():
            if sname.startswith("g_") and cnt:
                GPS.wait_ge(sems[sname], cnt)
            if sname.startswith("s_") and cnt:
                SYN.wait_ge(sems[sname], cnt)
        ctx.close()
        return nc
    def I(instr, s, n=1):
        ST["C"][s] += n
        instr.then_inc(sems[s], n)
        return ST["C"][s]

    def W(eng, s, v):
        if v > 0:
            eng.wait_ge(sems[s], v)

    def VF(instr):
        ev = I(instr, "dve", 1)
        VEC.wait_ge(sems["dve"], ev)
        return ev

    def proj_bank():
        b = "C" if ST["ps_par"] % 2 == 0 else "D"
        ST["ps_par"] += 1
        return (psC if b == "C" else psD), b

    def snapshot():
        return copy.deepcopy(ST)

    def restore(s):
        st2 = copy.deepcopy(s)
        ST.clear()
        ST.update(st2)

    ev_role = I(GPS.dma_start(rsb[:, :], role_t), "g_role", 16)
    I(GPS.dma_start(bqr[:, :, :], bqr_h), "g_small", 16)
    I(GPS.dma_start(bkr[:, :, :], bkr_h), "g_small", 16)
    I(GPS.dma_start(bvr[:, :, :], bvr_h), "g_small", 16)
    ev_gsmall = I(GPS.dma_start(wgsb[:, :, :], wg_h.rearrange("(g p) e -> p g e", p=128)), "g_small", 16)

    ev_wqkv = {}

    def load_wqkv(grp):
        cs, ce = grp * 256, (grp + 1) * 256
        I(GPS.dma_start(wqkv[:, 0, :, :], wq2[:, cs:ce].rearrange("(g p) n -> p g n", p=128)), "g_wqkv", 16)
        I(GPS.dma_start(wqkv[:, 1, :, :], wk2[:, cs:ce].rearrange("(g p) n -> p g n", p=128)), "g_wqkv", 16)
        ev_wqkv[grp] = I(GPS.dma_start(wqkv[:, 2, :, :], wv2[:, cs:ce].rearrange("(g p) n -> p g n", p=128)), "g_wqkv", 16)

    load_wqkv(0)
    ev_xc = []
    for tc in range(8):
        ev_xc.append(I(GPS.dma_start(x16[tc * 512:(tc + 1) * 512, :], xb[tc * 512:(tc + 1) * 512, :]), f"g_x{tc}", 16))
    ev_wo = I(GPS.dma_start(wosb[:, :, :], wo_h.rearrange("(P p) n -> p P n", p=128)), "g_wo", 16)

    I(SYN.dma_start(cmask[:, :, :], cmask_h), "s_const", 16)
    I(SYN.dma_start(i128b[:, :], i128_h), "s_const", 16)
    I(SYN.dma_start(eyeg[:, :, :], eyeg_h), "s_const", 16)
    I(SYN.dma_start(botsb[:, :], bot_h), "s_const", 16)
    I(SYN.dma_start(b1tsb[:, :, :], b1t_h), "s_const", 16)
    ev_consts = I(SYN.dma_start(b2tsb[:, :, :], b2t_h), "s_const", 16)

    VEC.memset(ones_row[:, :], 1.0)
    VEC.memset(ones1[:, :], 1.0)
    VEC.memset(onesc[:, :], 1.0)
    VEC.memset(onescf[:, :], 1.0)
    VEC.memset(epsr[:, :], 1e-5)
    VEC.memset(q4h[:, :, :], 0.0)
    ev_vtones = I(VEC.memset(vt4[:, :, :, 64:65], 1.0), "dve", 1)

    for eng in (TEN, SCA, VEC, SYN):
        W(eng, "g_role", ev_role)
    val_t = nc.values_load(rsb[:1, :1], engines=[ENG.PE], skip_runtime_bounds_check=True)
    val_s = nc.values_load(rsb[:1, :1], engines=[ENG.Activation], skip_runtime_bounds_check=True)
    val_v = nc.values_load(rsb[:1, :1], engines=[ENG.DVE], skip_runtime_bounds_check=True)
    val_sp = nc.values_load(rsb[:1, :1], engines=[ENG.SP], skip_runtime_bounds_check=True)

    def ln_chunk(sum_lhsT, sum_rhs_g, sq_src_g, norm_src_g, out_g,
                 pre_mm_wait=(), pre_sq_wait=(), prev=(0, 0), mid_emit=None):
        pre_mm_wait = list(pre_mm_wait) + [("dve", prev[0])]
        pre_sq_wait = list(pre_sq_wait) + [("dve", prev[1])]
        mm_evs = []
        for g in range(NG):
            if g == 0:
                for s, v in pre_sq_wait:
                    W(SCA, s, v)
            if g >= 2:
                W(SCA, "pe", mm_evs[g - 2])
            sq_ev = I(SCA.activation(sqtmp[:, g % 2, :], sq_src_g(g), AF.Square), "act", 1)
            if g == 0:
                for s, v in pre_mm_wait:
                    W(TEN, s, v)
            TEN.matmul(psA[0:1, 0:512], sum_lhsT, sum_rhs_g(g), start=(g == 0), stop=(g == NG - 1))
            W(TEN, "act", sq_ev)
            mm_evs.append(I(TEN.matmul(psA[0:1, 512:1024], onesc[:, :], sqtmp[:, g % 2, :],
                                       start=(g == 0), stop=(g == NG - 1)), "pe", 1))
        ev_mm = mm_evs[-1]
        if mid_emit is not None:
            mid_emit()
        W(VEC, "pe", ev_mm)
        VF(VEC.tensor_scalar(rowsf[:, 0, :], psA[0:1, 0:512], 1.0 / 1024.0, None, OP.mult))
        VF(VEC.tensor_tensor(rowsf[:, 1, :], rowsf[:, 0, :], rowsf[:, 0, :], OP.mult))
        ev_var = VF(VEC.scalar_tensor_tensor(rowsf[:, 2, :], psA[0:1, 512:1024], 1.0 / 1024.0,
                                             rowsf[:, 1, :], OP.mult, OP.subtract))
        W(SCA, "dve", ev_var)
        ev_sd = I(SCA.activation(rowsf[:, 1, :], rowsf[:, 2, :], AF.Sqrt, bias=epsr[:, :]), "act", 1)
        W(VEC, "act", ev_sd)
        VF(VEC.reciprocal(rowsf[:, 2, :], rowsf[:, 1, :]))
        with nc.allow_low_precision(reason="bf16 LN mean/rstd rows feeding broadcast matmul (values computed in fp32)"):
            VEC.tensor_copy(rowsb[:, 0, :], rowsf[:, 0, :])
            ev_rows = I(VEC.tensor_copy(rowsb[:, 1, :], rowsf[:, 2, :]), "dve", 1)
        W(TEN, "dve", ev_rows)
        TEN.matmul(psB[:, 0:512], ones1[:, :], rowsb[:, 0, :], start=True, stop=True)
        ev_bc = I(TEN.matmul(psB[:, 512:1024], ones1[:, :], rowsb[:, 1, :], start=True, stop=True), "pe", 1)
        W(VEC, "pe", ev_bc)
        ev_nrm = None
        for g in range(NG):
            VF(VEC.tensor_tensor(sqtmp[:, g % 2, :], norm_src_g(g), psB[:, 0:512], OP.subtract))
            ev_nrm = I(VEC.tensor_tensor(out_g(g), sqtmp[:, g % 2, :], psB[:, 512:1024], OP.mult), "dve", 1)
        return ev_nrm, ev_var

    def kv_proj_chunk(grp, sc):
        for li in range(2):
            pglob = grp * 2 + li
            ps, bn = proj_bank()
            W(TEN, "dve", ST["ev_ps"][bn])
            if sc == 0 and li == 0:
                W(TEN, "g_wqkv", ev_wqkv[grp])
                W(TEN, "g_small", ev_gsmall)
            for g in range(NG):
                if g == 0 and grp == 0:
                    W(TEN, "dve", ev_ln1[sc])
                TEN.matmul(ps[:, :], wqkv[:, 1, g, li * 128:(li + 1) * 128],
                           xln1[:, g, sc * 512:(sc + 1) * 512], start=(g == 0), stop=False)
            ev = I(TEN.matmul(ps[:, :], bkr[:, pglob, :], ones_row[:, :], start=False, stop=True), "pe", 1)
            W(VEC, "pe", ev)
            ST["ev_ps"][bn] = I(VEC.tensor_copy(k4h[:, li, sc * 512:(sc + 1) * 512], ps[:, :]), "dve", 1)
        for s4 in range(4):
            sc128 = 4 * sc + s4
            ps, bn = proj_bank()
            W(TEN, "dve", ST["ev_ps"][bn])
            for g in range(NG):
                TEN.matmul(ps[:, 0:256], xln1[:, g, sc128 * 128:(sc128 + 1) * 128], wqkv[:, 2, g, :],
                           start=(g == 0), stop=False)
            ev = I(TEN.matmul(ps[:, 0:256], ones1[:, :], bvr[:, grp, :], start=False, stop=True), "pe", 1)
            W(VEC, "pe", ev)
            if sc128 == 0 and grp == 0:
                W(VEC, "dve", ev_vtones)
            ST["ev_ps"][bn] = I(VEC.tensor_copy(vt4[:, sc128, :, 0:64],
                                                ps[:, 0:256].rearrange("p (a b) -> p a b", a=4)), "dve", 1)

    def kv_proj(grp):
        for sc in range(8):
            kv_proj_chunk(grp, sc)
        bmach_k_reduces(grp)

    def q_proj(grp):
        snap = snapshot()
        final0 = None
        for arm in nc.Switch(engines=[TEN, VEC], index=[val_t, val_v], n=2):
            if arm == 1:
                restore(snap)
            qcs = QCS[arm]
            for li in range(2):
                pglob = grp * 2 + li
                for j in range(4):
                    ps, bn = proj_bank()
                    W(TEN, "dve", ST["ev_ps"][bn])
                    for g in range(NG):
                        TEN.matmul(ps[:, :], wqkv[:, 0, g, li * 128:(li + 1) * 128],
                                   xln1[:, g, qcs[j]:qcs[j] + 512], start=(g == 0), stop=False)
                    ev = I(TEN.matmul(ps[:, :], bqr[:, pglob, :], ones_row[:, :],
                                      start=False, stop=True), "pe", 1)
                    W(VEC, "pe", ev)
                    VEC.tensor_scalar(q4h[0:64, 2 * li, j * 512:(j + 1) * 512],
                                      ps[0:64, :], 0.125, None, OP.mult)
                    ST["ev_ps"][bn] = I(VEC.tensor_scalar(q4h[64:128, 2 * li + 1, j * 512:(j + 1) * 512],
                                                          ps[64:128, :], 0.125, None, OP.mult), "dve", 1)
                    bmach_q_step(li, j, li == 0 and j == 0)
            if arm == 0:
                final0 = snapshot()
            else:
                assert ST == final0, ("qproj", grp)

    def bmach_k_reduces(grp):
        W(VEC, "pe", ST.get("bmach_pe", 0))
        VEC.tensor_reduce(colmax[:, 0:1], k4h[:, 0, :], AXL, OP.max, apply_absolute_value=True)
        VF(VEC.tensor_reduce(colmax[:, 1:2], k4h[:, 1, :], AXL, OP.max, apply_absolute_value=True))
        VF(VEC.tensor_tensor(colmax[:, 0:1], colmax[:, 0:1], colmax[:, 1:2], OP.max))

    def bmach_q_step(li, j, first):
        VEC.tensor_reduce(colmax[:, 1:2], q4h[:, 2 * li, j * 512:(j + 1) * 512], AXL, OP.max, apply_absolute_value=True)
        VF(VEC.tensor_reduce(colmax[:, 3:4], q4h[:, 2 * li + 1, j * 512:(j + 1) * 512], AXL, OP.max, apply_absolute_value=True))
        VF(VEC.tensor_tensor(colmax[:, 3:4], colmax[:, 3:4], colmax[:, 1:2], OP.max))
        if first:
            ST["bmach_ev1"] = VF(VEC.tensor_copy(colmax[:, 2:3], colmax[:, 3:4]))
        else:
            ST["bmach_ev1"] = VF(VEC.tensor_tensor(colmax[:, 2:3], colmax[:, 2:3], colmax[:, 3:4], OP.max))

    def b_machinery(grp, act_guard):
        ev1 = ST["bmach_ev1"]
        W(TEN, "dve", ev1)
        W(TEN, "dve", max(ST["ctxread"][0], ST["bcread"][0]))
        TEN.matmul(psE[0:1, 0:128], colmax[:, 0:1], i128b[:, :], start=True, stop=True)
        ev2 = I(TEN.matmul(psE[0:1, 128:256], colmax[:, 2:3], i128b[:, :], start=True, stop=True), "pe", 1)
        ST["bmach_pe"] = ev2
        W(VEC, "pe", ev2)
        VEC.tensor_reduce(sc1[:, 0:1], psE[0:1, 0:128], AXL, OP.max)
        VF(VEC.tensor_reduce(sc1[:, 1:2], psE[0:1, 128:256], AXL, OP.max))
        VF(VEC.tensor_tensor(sc1[:, 2:3], sc1[:, 0:1], sc1[:, 1:2], OP.mult))
        with nc.allow_low_precision(reason="bf16 stability-shift scalar feeding broadcast matmul"):
            ev3 = VF(VEC.tensor_scalar(negbrow[:, :], sc1[:, 2:3], -64.0, 50.0, OP.mult, OP.add))
        W(TEN, "dve", ev3)
        ev4 = I(TEN.matmul(psE[:, 256:257], ones1[:, :], negbrow[:, :], start=True, stop=True), "pe", 1)
        W(VEC, "pe", ev4)
        W(VEC, "act", act_guard)
        ev5 = I(VEC.tensor_copy(negB[:, :], psE[:, 256:257]), "dve", 1)
        ST["ctxread"][0] = max(ST["ctxread"][0], ev5)
        ST["bcread"][0] = max(ST["bcread"][0], ev5)
        return ev5

    def emit_pv(p, pb, ev_exp_a, ev_exp, mask_ev_a, mask_ev, hl, npair, ctxps, cb, lo=(0, 0)):
        W(TEN, "act", ev_exp_a)
        if mask_ev_a:
            W(TEN, "dve", mask_ev_a)
        if p == 0:
            W(TEN, "dve", ST["bcread"][cb])
        TEN.matmul(ctxps[0:65, lo[0]:512], vt4[:, 2 * p, hl, :], probs[:, pb, lo[0]:512],
                   start=(p == 0), stop=False)
        W(TEN, "act", ev_exp)
        if mask_ev:
            W(TEN, "dve", mask_ev)
        ev_pv = I(TEN.matmul(ctxps[0:65, lo[1]:512], vt4[:, 2 * p + 1, hl, :],
                             probs[:, pb, 512 + lo[1]:1024],
                             start=False, stop=(p == npair - 1)), "pe", 1)
        ST["pbfree"][pb] = ev_pv
        return ev_pv

    def emit_norm(cb, ctxps, pglob, half, j, ev_rc):
        bname = "C" if ST["u_norm"] % 2 == 0 else "D"
        bcps = psC if bname == "C" else psD
        ST["u_norm"] += 1
        W(TEN, "dve", ev_rc)
        W(TEN, "dve", ST["ev_ps"][bname])
        ev_bcmm = I(TEN.matmul(bcps[0:64, :], ones1[0:1, 0:64], reciprow[:, :],
                               start=True, stop=True), "pe", 1)
        ST["recip_pe"] = ev_bcmm
        W(VEC, "pe", ev_bcmm)
        ev_bcv = VF(VEC.tensor_copy(recipb[:, :], bcps[0:64, :]))
        ST["ev_ps"][bname] = ev_bcv
        ev_cx = I(VEC.tensor_tensor(ctxsb[64 * half:64 * half + 64, pglob, j * 512:(j + 1) * 512],
                                    ctxps[0:64, :], recipb[:, :], OP.mult), "dve", 1)
        ST["ctxread"][cb] = ev_cx
        ST["bcread"][cb] = ev_cx

    def attn_group(grp, ev_negB):
        snap = snapshot()
        final0 = None
        for arm in nc.Switch(engines=[TEN, SCA, VEC], index=[val_t, val_s, val_v], n=2):
            if arm == 1:
                restore(snap)
            qcs = QCS[arm]
            norm_pending = None
            for hl in range(4):
                li, half = hl // 2, hl % 2
                pglob = grp * 2 + li
                for j in range(4):
                    qb = qcs[j]
                    npair = (qb // 128 + 4) // 2
                    cb = ST["u"] % 2
                    ctxps = psE if cb == 0 else psF
                    pv_pending = None
                    last_pv = 0
                    for p in range(npair):
                        lo = (256, 384) if (p == npair - 1 and THIN) else (0, 0)
                        sbuf = (psA, psB)[p % 2]
                        W(TEN, "act", ST["scfree"][p % 2])
                        ev_sc1 = I(TEN.matmul(sbuf[:, lo[0]:512],
                                              k4h[:, li, (2 * p) * 128:(2 * p + 1) * 128],
                                              q4h[:, hl, j * 512 + lo[0]:(j + 1) * 512],
                                              start=True, stop=True), "pe", 1)
                        ev_sc = I(TEN.matmul(sbuf[:, 512 + lo[1]:1024],
                                             k4h[:, li, (2 * p + 1) * 128:(2 * p + 2) * 128],
                                             q4h[:, hl, j * 512 + lo[1]:(j + 1) * 512],
                                             start=True, stop=True), "pe", 1)
                        if p == 0 and norm_pending is not None:
                            emit_norm(*norm_pending)
                            norm_pending = None
                        if pv_pending is not None:
                            last_pv = emit_pv(*pv_pending)
                        pb = p % 2
                        W(SCA, "pe", ev_sc1)
                        W(SCA, "pe", ST["pbfree"][pb])
                        if p == 0 and hl == 0 and j == 0:
                            W(SCA, "dve", ev_negB)
                        ev_exp_a = I(SCA.activation(probs[:, pb, lo[0]:512], sbuf[:, lo[0]:512], AF.Exp,
                                                    bias=negB[:, :], scale=1.0), "act", 1)
                        W(SCA, "pe", ev_sc)
                        ev_exp = I(SCA.activation(probs[:, pb, 512 + lo[1]:1024], sbuf[:, 512 + lo[1]:1024],
                                                  AF.Exp, bias=negB[:, :], scale=1.0), "act", 1)
                        ST["scfree"][p % 2] = ev_exp
                        mask_ev_a = mask_ev = 0
                        if p >= npair - 2:
                            mi0 = (p - (npair - 2)) * 2
                            W(VEC, "act", ev_exp_a)
                            mask_ev_a = I(VEC.tensor_tensor(probs[:, pb, lo[0]:512], probs[:, pb, lo[0]:512],
                                                            cmask[:, mi0, lo[0]:512], OP.mult), "dve", 1)
                            W(VEC, "act", ev_exp)
                            mask_ev = I(VEC.tensor_tensor(probs[:, pb, 512 + lo[1]:1024],
                                                          probs[:, pb, 512 + lo[1]:1024],
                                                          cmask[:, mi0 + 1, lo[1]:512], OP.mult), "dve", 1)
                        pv_pending = (p, pb, ev_exp_a, ev_exp, mask_ev_a, mask_ev, hl, npair, ctxps, cb, lo)
                    last_pv = emit_pv(*pv_pending)
                    W(VEC, "pe", last_pv)
                    W(VEC, "pe", ST["recip_pe"])
                    with nc.allow_low_precision(reason="bf16 softmax 1/sum row feeding broadcast matmul"):
                        ev_rc = VF(VEC.reciprocal(reciprow[:, :], ctxps[64:65, :]))
                    norm_pending = (cb, ctxps, pglob, half, j, ev_rc)
                    ST["u"] += 1
            if norm_pending is not None:
                emit_norm(*norm_pending)
                norm_pending = None
            if arm == 0:
                final0 = snapshot()
            else:
                assert ST == final0, ("attn", grp)

    prev_norm = [0, 0, 0, 0]
    ev_ln1 = []
    ln_prev = (0, 0)
    for tc in range(8):
        buf = tc % 4
        W(SYN, f"g_x{tc}", ev_xc[tc])
        W(SYN, "dve", prev_norm[buf])
        evt = None
        for g in range(NG):
            evt = I(SYN.dma_start(xfm[:, buf, g, :],
                                  x16[tc * 512:(tc + 1) * 512, g * 128:(g + 1) * 128],
                                  transpose=True), f"s_xt{tc % 4}", 16)
        src = (lambda g, b=buf: xfm[:, b, g, :])
        pw = [(f"s_xt{tc % 4}", evt)] + ([("s_const", ev_consts), ("dve", ev_vtones)] if tc == 0 else [])
        env, evv = ln_chunk(onesc[:, :], src, src, src,
                            (lambda g, t=tc: xln1[:, g, t * 512:(t + 1) * 512]),
                            pre_mm_wait=pw, pre_sq_wait=[(f"s_xt{tc % 4}", evt)], prev=ln_prev,
                            mid_emit=(lambda t=tc: kv_proj_chunk(0, t - 1)) if tc >= 1 else None)
        ln_prev = (evv, env)
        prev_norm[buf] = env
        ev_ln1.append(env)
    if limit == "ln1":
        return _finish([xln1[:, :, :]])

    act_guard = 0
    kv_proj_chunk(0, 7)
    bmach_k_reduces(0)
    for grp in range(4):
        if grp > 0:
            kv_proj(grp)
        q_proj(grp)
        if grp < 3:
            W(GPS, "pe", ST["C"]["pe"])
            load_wqkv(grp + 1)
        if limit == "proj0":
            return _finish([k4h[:, :, :], q4h[:, :, :], vt4[:, :, :, :]])
        ev_negB = b_machinery(grp, act_guard)
        if limit == "bmach0":
            return _finish([negB[:, :]])
        attn_group(grp, ev_negB)
        act_guard = ST["C"]["act"]
        if limit == "attn0":
            return _finish([ctxsb[:, :, :]])

    if limit == "attn":
        return _finish([ctxsb[:, :, :]])
    ev_attn_pe_end = ST["C"]["pe"]
    ev_attn_act_end = ST["C"]["act"]
    ev_attn_dve_end = ST["C"]["dve"]
    ev_ctx_done = max(max(ST["ctxread"]), max(ST["bcread"]))

    W(SYN, "pe", ev_attn_pe_end)
    snap = snapshot()
    final0 = None
    for arm in nc.Switch(engines=[SYN], index=[val_sp], n=2):
        if arm == 1:
            restore(snap)
        loc = 0
        parts = []
        for (a, b) in OWN_RANGES[arm]:
            if b - a > 1024:
                m = (a + b) // 2
                parts += [(a, m), (m, b)]
            else:
                parts += [(a, b)]
        for (a, b) in parts:
            for g in range(NG):
                I(SYN.dma_start(xown[:, g, loc:loc + (b - a)],
                                x16[a:b, g * 128:(g + 1) * 128], transpose=True), "s_xo", 16)
            loc += b - a
        if arm == 0:
            final0 = snapshot()
        else:
            assert ST == final0, "xown"
    ev_xown = ST["C"]["s_xo"]

    ev_wo_tc = [None] * 4
    ev_ln2 = [None] * 4
    first_wo = [True]

    def wo_chunk(tc):
        ev_evac = None
        for dc in range(8):
            ps, bn = proj_bank()
            W(TEN, "dve", ST["ev_ps"][bn])
            mm = None
            for pg in range(8):
                if first_wo[0] and pg == 0:
                    W(TEN, "g_wo", ev_wo)
                    W(TEN, "dve", ev_ctx_done)
                    first_wo[0] = False
                mm = TEN.matmul(ps[:, :], wosb[:, pg, dc * 128:(dc + 1) * 128],
                                ctxsb[:, pg, tc * 512:(tc + 1) * 512],
                                start=(pg == 0), stop=(pg == 7))
            ev = I(mm, "pe", 1)
            W(VEC, "pe", ev)
            if tc == 0 and dc == 0:
                W(VEC, "s_xo", ev_xown)
            ev_evac = I(VEC.scalar_tensor_tensor(x1[:, dc, tc * 512:(tc + 1) * 512], ps[:, :],
                                                 botsb[:, dc:dc + 1],
                                                 xown[:, dc, tc * 512:(tc + 1) * 512],
                                                 OP.add, OP.add), "dve", 1)
            ST["ev_ps"][bn] = ev_evac
        ev_wo_tc[tc] = ev_evac

    def gating_logits(tjs):
        for tj in tjs:
            ps, bn = proj_bank()
            W(TEN, "dve", ST["ev_ps"][bn])
            if tj == 0:
                W(TEN, "g_small", ev_gsmall)
            W(TEN, "dve", ev_ln2[tj // 4])
            mm = None
            for g in range(NG):
                mm = TEN.matmul(ps[:, 0:8], xln2[:, g, tj * 128:(tj + 1) * 128], wgsb[:, g, :],
                                start=(g == 0), stop=(g == 7))
            ev = I(mm, "pe", 1)
            W(VEC, "pe", ev)
            ev_lg = I(VEC.tensor_copy(logits[:, tj, :], ps[:, 0:8]), "dve", 1)
            ST["ev_ps"][bn] = ev_lg

    wo_chunk(0)
    for tc in range(4):
        pw = [("dve", ev_wo_tc[tc])]
        if tc == 0:
            pw.append(("act", ev_attn_act_end))
        src_f32 = (lambda g, t=tc: x1[:, g, t * 512:(t + 1) * 512])
        mid = (lambda t=tc: wo_chunk(t + 1)) if tc < 3 else None
        env, evv = ln_chunk(onescf[:, :], src_f32, src_f32, src_f32,
                            (lambda g, t=tc: xln2[:, g, t * 512:(t + 1) * 512]),
                            pre_mm_wait=pw, pre_sq_wait=[("dve", ev_wo_tc[tc])], prev=ln_prev,
                            mid_emit=mid)
        ln_prev = (evv, env)
        ev_ln2[tc] = env

    if limit == "wo":
        return _finish([x1[:, :, :]])

    gating_logits(range(0, 16))
    W(VEC, "dve", max(ST["ev_ps"]["C"], ST["ev_ps"]["D"]))
    def stage_fence(instr):
        ev = I(instr, "dve", 1)
        VEC.wait_ge(sems["dve"], ev)
        return ev
    for tj in range(15):
        VEC.tensor_reduce(gcolM[:, tj:tj + 1], logits[:, tj, :], AXL, OP.max)
    stage_fence(VEC.tensor_reduce(gcolM[:, 15:16], logits[:, 15, :], AXL, OP.max))
    ev_m1 = stage_fence(VEC.tensor_scalar(gcolN[:, :], gcolM[:, :], -1.0, None, OP.mult))
    W(SCA, "dve", ev_m1)
    ev_e = None
    for tj in range(16):
        ev_e = I(SCA.activation(gtE[:, tj, :], logits[:, tj, :], AF.Exp, bias=gcolN[:, tj:tj + 1]), "act", 1)
    for tj in range(15):
        VEC.tensor_scalar(gtK[:, tj, :], logits[:, tj, :], gcolM[:, tj:tj + 1], None, OP.is_ge)
    stage_fence(VEC.tensor_scalar(gtK[:, 15, :], logits[:, 15, :], gcolM[:, 15:16], None, OP.is_ge))
    for tj in range(15):
        VEC.scalar_tensor_tensor(gtK[:, tj, :], gtK[:, tj, :], -1e9, logits[:, tj, :], OP.mult, OP.add)
    stage_fence(VEC.scalar_tensor_tensor(gtK[:, 15, :], gtK[:, 15, :], -1e9, logits[:, 15, :], OP.mult, OP.add))
    for tj in range(15):
        VEC.tensor_reduce(gcolM[:, tj:tj + 1], gtK[:, tj, :], AXL, OP.max)
    stage_fence(VEC.tensor_reduce(gcolM[:, 15:16], gtK[:, 15, :], AXL, OP.max))
    for tj in range(15):
        VEC.tensor_scalar(gtK[:, tj, :], logits[:, tj, :], gcolM[:, tj:tj + 1], None, OP.is_ge)
    stage_fence(VEC.tensor_scalar(gtK[:, 15, :], logits[:, 15, :], gcolM[:, 15:16], None, OP.is_ge))
    W(VEC, "act", ev_e)
    for tj in range(15):
        VEC.tensor_tensor(gtE[:, tj, :], gtE[:, tj, :], gtK[:, tj, :], OP.mult)
    stage_fence(VEC.tensor_tensor(gtE[:, 15, :], gtE[:, 15, :], gtK[:, 15, :], OP.mult))
    for tj in range(15):
        VEC.tensor_reduce(gcolD[:, tj:tj + 1], gtE[:, tj, :], AXL, OP.add)
    stage_fence(VEC.tensor_reduce(gcolD[:, 15:16], gtE[:, 15, :], AXL, OP.add))
    stage_fence(VEC.reciprocal(gcolN[:, :], gcolD[:, :]))
    ev_wtok = []
    with nc.allow_low_precision(reason="bf16 gating weights for broadcast matmul (top-2 selection done in fp32)"):
        for tj in range(16):
            ev_wtok.append(I(VEC.tensor_scalar(wtokb[:, tj, :], gtE[:, tj, :], gcolN[:, tj:tj + 1], None, OP.mult), "dve", 1))

    ev_wT = None
    for tj in range(16):
        ps, bn = proj_bank()
        W(TEN, "dve", ST["ev_ps"][bn])
        W(TEN, "dve", ev_wtok[tj])
        ev = I(TEN.matmul(ps[0:8, 0:128], wtokb[:, tj, :], i128b[:, :], start=True, stop=True), "pe", 1)
        W(VEC, "pe", ev)
        ev_wT = I(VEC.tensor_copy(w_T[:, tj * 128:(tj + 1) * 128], ps[0:8, 0:128]), "dve", 1)
        ST["ev_ps"][bn] = ev_wT

    if limit == "ln2":
        return _finish([xln2[:, :, :], w_T[:, :]])
    ev_wo_mm_done = ST["C"]["pe"]
    ev_w1l = {}
    ev_w2l = {}

    def load_w1(e, quarters=(0, 1, 2, 3), wait_pe=0):
        W(GPS, "pe", wait_pe)
        evs = []
        for q in quarters:
            evs.append(I(GPS.dma_start(
                w1sb[:, :, q * 512:(q + 1) * 512],
                w1_h[e * D:(e + 1) * D, q * 512:(q + 1) * 512].rearrange("(r p) n -> p r n", p=128)), f"g_w1q{q}", 16))
        ev_w1l[e] = evs

    def load_w2(e, wait_pe=0):
        W(GPS, "pe", wait_pe)
        evs = []
        for hh in range(2):
            evs.append(I(GPS.dma_start(
                w2sb[:, hh * 8:(hh + 1) * 8, :],
                w2_h[e * NI + hh * 1024:e * NI + (hh + 1) * 1024, :].rearrange("(r p) n -> p r n", p=128)), f"g_w2h{hh}", 16))
        ev_w2l[e] = evs

    load_w1(0, wait_pe=ev_wo_mm_done)
    load_w2(0, wait_pe=ev_wo_mm_done)

    H1B = [(psA, 0), (psA, 512), (psB, 0), (psB, 512)]
    h1_gelu_ev = [0, 0, 0, 0]
    h1s = 0
    ev_y_tc_prev = 0
    ev_pool = [0, 0]
    ev_wb = [0, 0]
    wb_par = 0
    ev_gelu_tc = 0
    ev_w1_lastmm = 0
    ev_w2_lastmm = 0
    ev_castout = [0, 0]
    ev_pool_tc = [0, 0, 0, 0]
    ln2_act_end = ST["C"]["act"]

    n_experts = 1 if limit == "ffn1" else NE
    for e in range(n_experts):
        for tc in range(4):
            first_gelu = True
            for ic in range(16):
                ps, colo = H1B[h1s % 4]
                W(TEN, "act", h1_gelu_ev[h1s % 4])
                if e == 0 and tc == 0 and ic == 0:
                    W(TEN, "act", ln2_act_end)
                    W(TEN, "dve", ev_ln2[3])
                mm = None
                for r in range(NG):
                    if ic in (0, 8) and r == 0:
                        W(TEN, f"g_w1q{0 if ic < 8 else 2}", ev_w1l[e][0 if ic < 8 else 2])
                    if ic in (4, 12) and r == 0:
                        W(TEN, f"g_w1q{1 if ic < 8 else 3}", ev_w1l[e][1 if ic < 8 else 3])
                    mm = TEN.matmul(ps[:, colo:colo + 512], w1sb[:, r, ic * 128:(ic + 1) * 128],
                                    xln2[:, r, tc * 512:(tc + 1) * 512],
                                    start=(r == 0), stop=(r == 7))
                ev_mm = I(mm, "pe", 1)
                W(SCA, "pe", ev_mm)
                if first_gelu:
                    W(SCA, "pe", ev_y_tc_prev)
                    first_gelu = False
                h1_gelu_ev[h1s % 4] = I(SCA.activation(h1[:, ic, :], ps[:, colo:colo + 512], AF.Gelu,
                                                       bias=b1tsb[:, e, ic:ic + 1]), "act", 1)
                h1s += 1
            ev_gelu_tc = h1_gelu_ev[(h1s - 1) % 4]
            if tc == 3:
                ev_w1_lastmm = ST["C"]["pe"]
                if e < NE - 1:
                    load_w1(e + 1, wait_pe=ev_w1_lastmm)
            wps = (psE, psF)[wb_par % 2]
            W(TEN, "dve", ev_wb[wb_par % 2])
            if e == 0 and tc == 0:
                W(TEN, "dve", max(ST["ctxread"][0], ST["ctxread"][1], ST["bcread"][0], ST["bcread"][1], ev_wT))
            else:
                W(TEN, "dve", ev_wT)
            ev = I(TEN.matmul(wps[:, :], eyeg[:, e, :], w_T[:, tc * 512:(tc + 1) * 512],
                              start=True, stop=True), "pe", 1)
            W(VEC, "pe", ev)
            ev_wb[wb_par % 2] = VF(VEC.tensor_copy(wbsb[:, tc % 2, :], wps[:, :]))
            ev_wb_cur = ev_wb[wb_par % 2]
            wb_par += 1
            for dc in range(8):
                ps, bn = proj_bank()
                W(TEN, "dve", ST["ev_ps"][bn])
                if dc == 0:
                    W(TEN, "act", ev_gelu_tc)
                    W(TEN, "g_w2h1", ev_w2l[e][1])
                    W(TEN, "g_w2h0", ev_w2l[e][0])
                mm = None
                for r in range(16):
                    mm = TEN.matmul(ps[:, :], w2sb[:, r, dc * 128:(dc + 1) * 128], h1[:, r, :],
                                    start=(r == 0), stop=(r == 15))
                ev_y = I(mm, "pe", 1)
                yb = dc % 2
                W(VEC, "pe", ev_y)
                W(VEC, "pool", ev_pool[yb])
                ev_stt = I(VEC.scalar_tensor_tensor(ytmp[:, yb, :], ps[:, :], b2tsb[:, e, dc:dc + 1],
                                                    wbsb[:, tc % 2, :], OP.add, OP.mult), "dve", 1)
                ST["ev_ps"][bn] = ev_stt
                W(GPS, "dve", ev_stt)
                ev_pool[yb] = I(GPS.tensor_tensor(x1[:, dc, tc * 512:(tc + 1) * 512],
                                                  x1[:, dc, tc * 512:(tc + 1) * 512],
                                                  ytmp[:, yb, :], OP.add), "pool", 1)
            ev_y_tc_prev = ST["C"]["pe"]
            if tc == 3:
                ev_w2_lastmm = ST["C"]["pe"]
                if e < NE - 1:
                    load_w2(e + 1, wait_pe=ev_w2_lastmm)
            if e == NE - 1:
                ev_pool_tc[tc] = max(ev_pool)

    if limit in ("ffn", "ffn1"):
        return _finish([x1[:, :, :], h1[:, :, :], wbsb[:, :, :]])

    for tc in range(4):
        W(VEC, "pool", ev_pool_tc[tc])
        if tc == 0:
            W(VEC, "pe", ev_w1_lastmm)
        ev_x2b = I(VEC.tensor_copy(x2b[:, :, tc * 512:(tc + 1) * 512],
                                   x1[:, :, tc * 512:(tc + 1) * 512]), "dve", 1)
        W(SYN, "dve", ev_x2b)
        if tc == 0:
            W(SYN, "dve", ev_attn_dve_end)
            W(SYN, "pe", ev_attn_pe_end)
        ev_sp1 = I(SYN.dma_start(
            x2f16.rearrange("g p t -> p g t")[:, :, tc * 512:(tc + 1) * 512],
            x2b[:, :, tc * 512:(tc + 1) * 512]), f"s_x2s{tc}", 16)
        for tj4 in range(4):
            tj = tc * 4 + tj4
            ob = tj % 2
            W(SYN, f"s_x2s{tc}", ev_sp1)
            W(SYN, f"g_out{ob}", ev_castout[ob])
            ev_tr = I(SYN.dma_start(
                x2tok[:, ob, :],
                x2f16.rearrange("g p t -> (g p) t")[:, tj * 128:(tj + 1) * 128],
                transpose=True), f"s_ot{ob}", 16)
            W(GPS, f"s_ot{ob}", ev_tr)
            ev_castout[ob] = I(GPS.dma_start(yout[tj * 128:(tj + 1) * 128, :],
                                             x2tok[:, ob, :]), f"g_out{ob}", 16)
    for sname, cnt in ST["C"].items():
        if sname.startswith("g_") and cnt:
            GPS.wait_ge(sems[sname], cnt)
        if sname.startswith("s_") and cnt:
            SYN.wait_ge(sems[sname], cnt)

    ctx.close()
    return nc


_NC_CACHE = {}


def _get_nc():
    if "nc" not in _NC_CACHE:
        _NC_CACHE["nc"] = build_nc()
    return _NC_CACHE["nc"]


def kernel(L0_attn_embeddings, L0_attn_Wq, L0_attn_bq, L0_attn_Wk, L0_attn_bk,
           L0_attn_Wv, L0_attn_bv, L0_attn_Wo, L0_attn_bo,
           L0_ffn_Wg, L0_ffn_W1, L0_ffn_b1, L0_ffn_W2, L0_ffn_b2):
    f32 = np.float32
    x = np.ascontiguousarray(np.asarray(L0_attn_embeddings, dtype=f32))
    Wq = np.asarray(L0_attn_Wq, dtype=f32)
    Wk = np.asarray(L0_attn_Wk, dtype=f32)
    Wv = np.asarray(L0_attn_Wv, dtype=f32)
    bq = np.asarray(L0_attn_bq, dtype=f32)
    bk = np.asarray(L0_attn_bk, dtype=f32)
    bv = np.asarray(L0_attn_bv, dtype=f32)
    Wo = np.ascontiguousarray(np.asarray(L0_attn_Wo, dtype=f32))
    bo = np.asarray(L0_attn_bo, dtype=f32)
    Wg = np.ascontiguousarray(np.asarray(L0_ffn_Wg, dtype=f32))
    W1 = np.asarray(L0_ffn_W1, dtype=f32)
    W2 = np.asarray(L0_ffn_W2, dtype=f32)
    b1 = np.asarray(L0_ffn_b1, dtype=f32)
    b2 = np.asarray(L0_ffn_b2, dtype=f32)

    wq2 = np.ascontiguousarray(Wq.transpose(1, 0, 2).reshape(D, H * DK))
    wk2 = np.ascontiguousarray(Wk.transpose(1, 0, 2).reshape(D, H * DK))
    wv2 = np.ascontiguousarray(Wv.transpose(1, 0, 2).reshape(D, H * DK))
    w1f = np.ascontiguousarray(W1.reshape(NE * D, NI))
    w2f = np.ascontiguousarray(W2.reshape(NE * NI, D))
    bqr = np.ascontiguousarray(bq.reshape(1, 8, 128))
    bkr = np.ascontiguousarray(bk.reshape(1, 8, 128))
    bvr = np.ascontiguousarray(bv.reshape(1, 4, 256))
    bot = np.ascontiguousarray(bo.reshape(8, 128).T)
    b1t = np.ascontiguousarray(b1.reshape(8, 16, 128).transpose(2, 0, 1))
    b2t = np.ascontiguousarray(b2.reshape(8, 8, 128).transpose(2, 0, 1))

    s_idx = np.arange(128)[:, None]
    q_idx = np.arange(512)[None, :]
    cmask = np.stack([(128 * i + s_idx <= q_idx) for i in range(4)], axis=1).astype(ml_dtypes.bfloat16)
    cmask = np.ascontiguousarray(cmask)
    i128 = np.eye(128, dtype=ml_dtypes.bfloat16)
    eyeg = np.zeros((8, 8, 128), dtype=ml_dtypes.bfloat16)
    for e in range(8):
        eyeg[e, e, :] = 1
    shared = dict(wq2=wq2, wk2=wk2, wv2=wv2, wo=Wo, w1=w1f, w2=w2f, wg=Wg,
                  bqr=bqr, bkr=bkr, bvr=bvr, bot=bot, b1t=b1t, b2t=b2t,
                  cmask=cmask, i128=i128, eyeg=eyeg)

    nc = _get_nc()
    in_maps = []
    for c in range(8):
        m = dict(shared)
        m["xb"] = np.ascontiguousarray(x[c // 2])
        m["role"] = np.array([[c % 2]], dtype=np.int32)
        in_maps.append(m)
    res = run_bass_kernel_spmd(nc, in_maps, core_ids=list(range(8)))

    out = np.empty((B, S, D), dtype=f32)
    for c in range(8):
        y = res.results[c]["yout"]
        b, role = c // 2, c % 2
        loc = 0
        for (a, e2) in OWN_RANGES[role]:
            out[b, a:e2, :] = y[loc:loc + (e2 - a), :]
            loc += e2 - a
    return out


# version 21
# speedup vs baseline: 1.2061x; 1.0294x over previous
import copy
import numpy as np
import ml_dtypes
import concourse.bass as bass
import concourse.mybir as mybir
from concourse.bass_utils import run_bass_kernel_spmd
from concourse.alu_op_type import AluOpType as OP
from contextlib import ExitStack

F32 = mybir.dt.float32
BF16 = mybir.dt.bfloat16
I32 = mybir.dt.int32
AF = mybir.ActivationFunctionType
AXL = mybir.AxisListType.X
ENG = mybir.EngineType

B, S, D, H, DK = 4, 4096, 1024, 16, 64
NE, NI = 8, 2048
NOWN = 2048
NG = 8
QCS = [[0, 512, 3072, 3584], [1024, 1536, 2048, 2560]]
OWN_RANGES = [[(0, 1024), (3072, 4096)], [(1024, 3072)]]
ARENA_SZ = 207872


THIN = True


def build_nc(limit=None, debug=False):
    nc = bass.Bass("TRN2", target_bir_lowering=False)

    xb = nc.dram_tensor("xb", [S, D], F32, kind="ExternalInput").ap()
    role_t = nc.dram_tensor("role", [1, 1], I32, kind="ExternalInput").ap()
    wq2 = nc.dram_tensor("wq2", [D, H * DK], F32, kind="ExternalInput").ap()
    wk2 = nc.dram_tensor("wk2", [D, H * DK], F32, kind="ExternalInput").ap()
    wv2 = nc.dram_tensor("wv2", [D, H * DK], F32, kind="ExternalInput").ap()
    wo_h = nc.dram_tensor("wo", [D, D], F32, kind="ExternalInput").ap()
    w1_h = nc.dram_tensor("w1", [NE * D, NI], F32, kind="ExternalInput").ap()
    w2_h = nc.dram_tensor("w2", [NE * NI, D], F32, kind="ExternalInput").ap()
    wg_h = nc.dram_tensor("wg", [D, NE], F32, kind="ExternalInput").ap()
    bqr_h = nc.dram_tensor("bqr", [1, 8, 128], F32, kind="ExternalInput").ap()
    bkr_h = nc.dram_tensor("bkr", [1, 8, 128], F32, kind="ExternalInput").ap()
    bvr_h = nc.dram_tensor("bvr", [1, 4, 256], F32, kind="ExternalInput").ap()
    bot_h = nc.dram_tensor("bot", [128, 8], F32, kind="ExternalInput").ap()
    b1t_h = nc.dram_tensor("b1t", [128, 8, 16], F32, kind="ExternalInput").ap()
    b2t_h = nc.dram_tensor("b2t", [128, 8, 8], F32, kind="ExternalInput").ap()
    cmask_h = nc.dram_tensor("cmask", [128, 4, 512], BF16, kind="ExternalInput").ap()
    i128_h = nc.dram_tensor("i128", [128, 128], BF16, kind="ExternalInput").ap()
    eyeg_h = nc.dram_tensor("eyeg", [8, 8, 128], BF16, kind="ExternalInput").ap()
    yout = nc.dram_tensor("yout", [NOWN, D], F32, kind="ExternalOutput").ap()
    dbg = nc.dram_tensor("dbg", [128, 40960], F32, kind="ExternalOutput").ap() if debug else None
    x16 = nc.dram_tensor("x16", [S, D], BF16).ap()
    x2f16 = nc.dram_tensor("x2f16", [NG, 128, NOWN], BF16).ap()

    ctx = ExitStack()
    E = ctx.enter_context
    arena = E(nc.sbuf_tensor("arena", [128, ARENA_SZ], mybir.dt.uint8))
    base_addr = nc.lookup_mloc(arena).addr

    def place(name, shape, dtype, off):
        nbytes = int(np.prod(shape[1:])) * mybir.dt.size(dtype)
        assert off % 32 == 0 and off + nbytes <= ARENA_SZ, (name, off, nbytes)
        h = nc.virtual_sbuf_tensor(name, shape, dtype)
        nc._tensor(h.name, shape, dtype, type="SB")
        mloc = nc.lookup_mloc(h)
        mloc.allocated = True
        mloc.addr = base_addr + off
        return h

    R_XLN1, R_BIG, R_CTX, R_WOW2, R_MISC = 0, 65536, 114944, 147712, 180480
    xln1 = place("xln1", [128, NG, S], BF16, R_XLN1)
    x1 = place("x1", [128, NG, NOWN], F32, R_XLN1)
    xfm = place("xfm", [128, 4, NG, 512], BF16, R_CTX)
    k4h = place("k4h", [128, 2, S], BF16, R_BIG)
    xown = place("xown", [128, NG, NOWN], BF16, R_BIG)
    xln2 = place("xln2", [128, NG, NOWN], BF16, R_BIG)
    vt4 = place("vt4", [128, 32, 4, 65], BF16, R_BIG + 16384)
    q4h = place("q4h", [128, 4, NOWN], BF16, R_BIG + 33024)
    h1 = place("h1", [128, 16, 512], BF16, R_BIG + 32768)
    probs = None
    ctxsb = place("ctxsb", [128, 8, NOWN], BF16, R_CTX)
    w1sb = place("w1sb", [128, 8, NI], BF16, R_CTX)
    x2b = place("x2b", [128, NG, NOWN], BF16, R_CTX)
    wosb = place("wosb", [128, 8, 1024], BF16, R_WOW2)
    wqkv = place("wqkv", [128, 3, 8, 256], BF16, R_WOW2 + 16384)
    w2sb = place("w2sb", [128, 16, 1024], BF16, R_WOW2)

    off = [R_MISC]

    def P(name, shape, dtype, at=None):
        o = off[0] if at is None else at
        t = place(name, shape, dtype, o)
        if at is None:
            nbytes = int(np.prod(shape[1:])) * mybir.dt.size(dtype)
            off[0] += (nbytes + 31) // 32 * 32
        return t

    cmask = place("cmask", [128, 4, 512], BF16, R_WOW2 + 28672)
    x2tok = P("x2tok", [128, 2, 1024], BF16)
    probs = place("probs", [128, 2, 1024], BF16, R_MISC)
    abuf = off[0]
    bqr = P("bqr", [1, 8, 128], BF16, at=abuf)
    bkr = P("bkr", [1, 8, 128], BF16, at=abuf + 2048)
    bvr = P("bvr", [1, 4, 256], BF16, at=abuf + 4096)
    reciprow = P("reciprow", [1, 512], BF16, at=abuf + 6144)
    recipb = P("recipb", [64, 512], BF16, at=abuf + 7168)
    colmax = P("colmax", [128, 4], BF16, at=abuf + 8192)
    ytmp = P("ytmp", [128, 2, 512], F32, at=abuf)
    wbsb = P("wbsb", [128, 2, 512], F32, at=abuf + 4096)
    off[0] = abuf + 8224
    i128b = P("i128b", [128, 128], BF16)
    eyeg = P("eyeg", [8, 8, 128], BF16)
    rbuf = off[0]
    rowsf = P("rowsf", [1, 3, 512], F32, at=rbuf)
    rowsb = P("rowsb", [1, 2, 512], BF16, at=rbuf + 6144)
    w_T = P("w_T", [8, NOWN], BF16, at=rbuf)
    off[0] = rbuf + 8192
    ones_row = P("ones_row", [1, 512], BF16)
    ones1 = P("ones1", [1, 128], BF16)
    onesc = P("onesc", [128, 1], BF16)
    onescf = P("onescf", [128, 1], F32)
    negB = P("negB", [128, 1], F32)
    sc1 = P("sc1", [1, 8], F32)
    negbrow = P("negbrow", [1, 1], BF16)
    epsr = P("epsr", [1, 1], F32)
    wgsb = P("wgsb", [128, 8, 8], BF16)
    sbuf0 = off[0]
    sqtmp = P("sqtmp", [128, 2, 512], BF16, at=sbuf0)
    logits = P("logits", [128, 16, 8], F32, at=sbuf0)
    gtE = P("gtE", [128, 16, 8], F32, at=sbuf0 + 512)
    gtK = P("gtK", [128, 16, 8], F32, at=sbuf0 + 1024)
    gcolM = P("gcolM", [128, 16], F32, at=sbuf0 + 1536)
    gcolN = P("gcolN", [128, 16], F32, at=sbuf0 + 1600)
    gcolD = P("gcolD", [128, 16], F32, at=sbuf0 + 1664)
    wtokb = P("wtokb", [128, 16, 8], BF16, at=sbuf0 + 1728)
    off[0] = sbuf0 + 2048
    botsb = P("botsb", [128, 8], F32)
    b1tsb = P("b1tsb", [128, 8, 16], F32)
    b2tsb = P("b2tsb", [128, 8, 8], F32)
    rsb = P("rsb", [1, 1], I32)
    assert off[0] <= ARENA_SZ, off[0]

    psA = E(nc.psum_tensor("psA", [128, 1024], F32, unsafe=True))
    psB = E(nc.psum_tensor("psB", [128, 1024], F32, unsafe=True))
    psC = E(nc.psum_tensor("psC", [128, 512], F32))
    psD = E(nc.psum_tensor("psD", [128, 512], F32))
    psE = E(nc.psum_tensor("psE", [128, 512], F32))
    psF = E(nc.psum_tensor("psF", [128, 512], F32))

    sems = {s: E(nc.semaphore(s)) for s in ["pe", "act", "dve", "pool",
            "g_role", "g_small", "g_wo", "g_wqkv",
            "s_const", "s_xo"]
        + [f"g_x{i}" for i in range(8)] + ["s_xt0", "s_xt1", "s_xt2", "s_xt3"]
        + [f"g_w1q{i}" for i in range(4)] + ["g_w2h0", "g_w2h1"]
        + [f"s_x2s{i}" for i in range(4)] + ["s_ot0", "s_ot1", "g_out0", "g_out1"]}
    TEN, SCA, VEC, GPS, SYN = nc.tensor, nc.scalar, nc.vector, nc.gpsimd, nc.sync
    blk = E(nc.Block())

    ST = {
        "C": {s: 0 for s in sems},
        "ev_ps": {"C": 0, "D": 0},
        "ps_par": 0,
        "scfree": [0, 0],
        "pbfree": [0, 0],
        "u": 0,
        "ctxread": [0, 0],
        "bcread": [0, 0],
        "rcread": [0, 0],
        "recip_pe": 0,
        "u_norm": 0,
        "bmach_pe": 0,
        "bmach_ev1": 0,
    }


    def _finish(dumps=()):
        if debug and dumps:
            col = 0
            for ap in dumps:
                n = int(np.prod(ap.shape[1:]))
                GPS.wait_ge(sems["pe"], ST["C"]["pe"])
                GPS.wait_ge(sems["dve"], ST["C"]["dve"])
                GPS.wait_ge(sems["act"], ST["C"]["act"])
                pat = {2: None, 3: "p a b -> p (a b)", 4: "p a b c -> p (a b c)"}[len(ap.shape)]
                flat = ap if pat is None else ap.rearrange(pat)
                I(GPS.dma_start(dbg[0:ap.shape[0], col:col + n], flat), "g_out0", 16)
                col += n
        for sname, cnt in ST["C"].items():
            if sname.startswith("g_") and cnt:
                GPS.wait_ge(sems[sname], cnt)
            if sname.startswith("s_") and cnt:
                SYN.wait_ge(sems[sname], cnt)
        ctx.close()
        return nc
    def I(instr, s, n=1):
        ST["C"][s] += n
        instr.then_inc(sems[s], n)
        return ST["C"][s]

    def W(eng, s, v):
        if v > 0:
            eng.wait_ge(sems[s], v)

    def VF(instr):
        ev = I(instr, "dve", 1)
        VEC.wait_ge(sems["dve"], ev)
        return ev

    def proj_bank():
        b = "C" if ST["ps_par"] % 2 == 0 else "D"
        ST["ps_par"] += 1
        return (psC if b == "C" else psD), b

    def snapshot():
        return copy.deepcopy(ST)

    def restore(s):
        st2 = copy.deepcopy(s)
        ST.clear()
        ST.update(st2)

    ev_role = I(GPS.dma_start(rsb[:, :], role_t), "g_role", 16)
    I(GPS.dma_start(bqr[:, :, :], bqr_h), "g_small", 16)
    I(GPS.dma_start(bkr[:, :, :], bkr_h), "g_small", 16)
    I(GPS.dma_start(bvr[:, :, :], bvr_h), "g_small", 16)
    ev_gsmall = I(GPS.dma_start(wgsb[:, :, :], wg_h.rearrange("(g p) e -> p g e", p=128)), "g_small", 16)

    ev_wqkv = {}

    def load_wqkv(grp):
        cs, ce = grp * 256, (grp + 1) * 256
        I(GPS.dma_start(wqkv[:, 0, :, :], wq2[:, cs:ce].rearrange("(g p) n -> p g n", p=128)), "g_wqkv", 16)
        I(GPS.dma_start(wqkv[:, 1, :, :], wk2[:, cs:ce].rearrange("(g p) n -> p g n", p=128)), "g_wqkv", 16)
        ev_wqkv[grp] = I(GPS.dma_start(wqkv[:, 2, :, :], wv2[:, cs:ce].rearrange("(g p) n -> p g n", p=128)), "g_wqkv", 16)

    load_wqkv(0)
    ev_xc = []
    for tc in range(8):
        ev_xc.append(I(GPS.dma_start(x16[tc * 512:(tc + 1) * 512, :], xb[tc * 512:(tc + 1) * 512, :]), f"g_x{tc}", 16))
    ev_wo = I(GPS.dma_start(wosb[:, :, :], wo_h.rearrange("(P p) n -> p P n", p=128)), "g_wo", 16)

    I(SYN.dma_start(cmask[:, :, :], cmask_h), "s_const", 16)
    I(SYN.dma_start(i128b[:, :], i128_h), "s_const", 16)
    I(SYN.dma_start(eyeg[:, :, :], eyeg_h), "s_const", 16)
    I(SYN.dma_start(botsb[:, :], bot_h), "s_const", 16)
    I(SYN.dma_start(b1tsb[:, :, :], b1t_h), "s_const", 16)
    ev_consts = I(SYN.dma_start(b2tsb[:, :, :], b2t_h), "s_const", 16)

    VEC.memset(ones_row[:, :], 1.0)
    VEC.memset(ones1[:, :], 1.0)
    VEC.memset(onesc[:, :], 1.0)
    VEC.memset(onescf[:, :], 1.0)
    VEC.memset(epsr[:, :], 1e-5)
    VEC.memset(q4h[:, :, :], 0.0)
    ev_vtones = I(VEC.memset(vt4[:, :, :, 64:65], 1.0), "dve", 1)

    for eng in (TEN, SCA, VEC, SYN):
        W(eng, "g_role", ev_role)
    val_t = nc.values_load(rsb[:1, :1], engines=[ENG.PE], skip_runtime_bounds_check=True)
    val_s = nc.values_load(rsb[:1, :1], engines=[ENG.Activation], skip_runtime_bounds_check=True)
    val_v = nc.values_load(rsb[:1, :1], engines=[ENG.DVE], skip_runtime_bounds_check=True)
    val_sp = nc.values_load(rsb[:1, :1], engines=[ENG.SP], skip_runtime_bounds_check=True)

    def ln_chunk(sum_lhsT, sum_rhs_g, sq_src_g, norm_src_g, out_g,
                 pre_mm_wait=(), pre_sq_wait=(), prev=(0, 0), mid_emit=None):
        pre_mm_wait = list(pre_mm_wait) + [("dve", prev[0])]
        pre_sq_wait = list(pre_sq_wait) + [("dve", prev[1])]
        mm_evs = []
        for g in range(NG):
            if g == 0:
                for s, v in pre_sq_wait:
                    W(SCA, s, v)
            if g >= 2:
                W(SCA, "pe", mm_evs[g - 2])
            sq_ev = I(SCA.activation(sqtmp[:, g % 2, :], sq_src_g(g), AF.Square), "act", 1)
            if g == 0:
                for s, v in pre_mm_wait:
                    W(TEN, s, v)
            TEN.matmul(psA[0:1, 0:512], sum_lhsT, sum_rhs_g(g), start=(g == 0), stop=(g == NG - 1))
            W(TEN, "act", sq_ev)
            mm_evs.append(I(TEN.matmul(psA[0:1, 512:1024], onesc[:, :], sqtmp[:, g % 2, :],
                                       start=(g == 0), stop=(g == NG - 1)), "pe", 1))
        ev_mm = mm_evs[-1]
        if mid_emit is not None:
            mid_emit()
        W(VEC, "pe", ev_mm)
        VF(VEC.tensor_scalar(rowsf[:, 0, :], psA[0:1, 0:512], 1.0 / 1024.0, None, OP.mult))
        VF(VEC.tensor_tensor(rowsf[:, 1, :], rowsf[:, 0, :], rowsf[:, 0, :], OP.mult))
        ev_var = VF(VEC.scalar_tensor_tensor(rowsf[:, 2, :], psA[0:1, 512:1024], 1.0 / 1024.0,
                                             rowsf[:, 1, :], OP.mult, OP.subtract))
        W(SCA, "dve", ev_var)
        ev_sd = I(SCA.activation(rowsf[:, 1, :], rowsf[:, 2, :], AF.Sqrt, bias=epsr[:, :]), "act", 1)
        W(VEC, "act", ev_sd)
        VF(VEC.reciprocal(rowsf[:, 2, :], rowsf[:, 1, :]))
        with nc.allow_low_precision(reason="bf16 LN mean/rstd rows feeding broadcast matmul (values computed in fp32)"):
            VEC.tensor_copy(rowsb[:, 0, :], rowsf[:, 0, :])
            ev_rows = I(VEC.tensor_copy(rowsb[:, 1, :], rowsf[:, 2, :]), "dve", 1)
        W(TEN, "dve", ev_rows)
        TEN.matmul(psB[:, 0:512], ones1[:, :], rowsb[:, 0, :], start=True, stop=True)
        ev_bc = I(TEN.matmul(psB[:, 512:1024], ones1[:, :], rowsb[:, 1, :], start=True, stop=True), "pe", 1)
        W(VEC, "pe", ev_bc)
        ev_nrm = None
        for g in range(NG):
            VF(VEC.tensor_tensor(sqtmp[:, g % 2, :], norm_src_g(g), psB[:, 0:512], OP.subtract))
            ev_nrm = I(VEC.tensor_tensor(out_g(g), sqtmp[:, g % 2, :], psB[:, 512:1024], OP.mult), "dve", 1)
        return ev_nrm, ev_var

    def kv_proj_chunk(grp, sc):
        for li in range(2):
            pglob = grp * 2 + li
            ps, bn = proj_bank()
            W(TEN, "dve", ST["ev_ps"][bn])
            if sc == 0 and li == 0:
                W(TEN, "g_wqkv", ev_wqkv[grp])
                W(TEN, "g_small", ev_gsmall)
            for g in range(NG):
                if g == 0 and grp == 0:
                    W(TEN, "dve", ev_ln1[sc])
                TEN.matmul(ps[:, :], wqkv[:, 1, g, li * 128:(li + 1) * 128],
                           xln1[:, g, sc * 512:(sc + 1) * 512], start=(g == 0), stop=False)
            ev = I(TEN.matmul(ps[:, :], bkr[:, pglob, :], ones_row[:, :], start=False, stop=True), "pe", 1)
            W(VEC, "pe", ev)
            ST["ev_ps"][bn] = I(VEC.tensor_copy(k4h[:, li, sc * 512:(sc + 1) * 512], ps[:, :]), "dve", 1)
        for s4 in range(4):
            sc128 = 4 * sc + s4
            ps, bn = proj_bank()
            W(TEN, "dve", ST["ev_ps"][bn])
            for g in range(NG):
                TEN.matmul(ps[:, 0:256], xln1[:, g, sc128 * 128:(sc128 + 1) * 128], wqkv[:, 2, g, :],
                           start=(g == 0), stop=False)
            ev = I(TEN.matmul(ps[:, 0:256], ones1[:, :], bvr[:, grp, :], start=False, stop=True), "pe", 1)
            W(VEC, "pe", ev)
            if sc128 == 0 and grp == 0:
                W(VEC, "dve", ev_vtones)
            ST["ev_ps"][bn] = I(VEC.tensor_copy(vt4[:, sc128, :, 0:64],
                                                ps[:, 0:256].rearrange("p (a b) -> p a b", a=4)), "dve", 1)

    def kv_proj(grp):
        for sc in range(8):
            kv_proj_chunk(grp, sc)
        bmach_k_reduces(grp)

    def q_proj(grp):
        snap = snapshot()
        final0 = None
        for arm in nc.Switch(engines=[TEN, VEC], index=[val_t, val_v], n=2):
            if arm == 1:
                restore(snap)
            qcs = QCS[arm]
            for li in range(2):
                pglob = grp * 2 + li
                for j in range(4):
                    ps, bn = proj_bank()
                    W(TEN, "dve", ST["ev_ps"][bn])
                    for g in range(NG):
                        TEN.matmul(ps[:, :], wqkv[:, 0, g, li * 128:(li + 1) * 128],
                                   xln1[:, g, qcs[j]:qcs[j] + 512], start=(g == 0), stop=False)
                    ev = I(TEN.matmul(ps[:, :], bqr[:, pglob, :], ones_row[:, :],
                                      start=False, stop=True), "pe", 1)
                    W(VEC, "pe", ev)
                    VEC.tensor_scalar(q4h[0:64, 2 * li, j * 512:(j + 1) * 512],
                                      ps[0:64, :], 0.125, None, OP.mult)
                    ST["ev_ps"][bn] = I(VEC.tensor_scalar(q4h[64:128, 2 * li + 1, j * 512:(j + 1) * 512],
                                                          ps[64:128, :], 0.125, None, OP.mult), "dve", 1)
                    bmach_q_step(li, j, li == 0 and j == 0)
            if arm == 0:
                final0 = snapshot()
            else:
                assert ST == final0, ("qproj", grp)

    def bmach_k_reduces(grp):
        W(VEC, "pe", ST.get("bmach_pe", 0))
        VEC.tensor_reduce(colmax[:, 0:1], k4h[:, 0, :], AXL, OP.max, apply_absolute_value=True)
        VF(VEC.tensor_reduce(colmax[:, 1:2], k4h[:, 1, :], AXL, OP.max, apply_absolute_value=True))
        VF(VEC.tensor_tensor(colmax[:, 0:1], colmax[:, 0:1], colmax[:, 1:2], OP.max))

    def bmach_q_step(li, j, first):
        VEC.tensor_reduce(colmax[:, 1:2], q4h[:, 2 * li, j * 512:(j + 1) * 512], AXL, OP.max, apply_absolute_value=True)
        VF(VEC.tensor_reduce(colmax[:, 3:4], q4h[:, 2 * li + 1, j * 512:(j + 1) * 512], AXL, OP.max, apply_absolute_value=True))
        VF(VEC.tensor_tensor(colmax[:, 3:4], colmax[:, 3:4], colmax[:, 1:2], OP.max))
        if first:
            ST["bmach_ev1"] = VF(VEC.tensor_copy(colmax[:, 2:3], colmax[:, 3:4]))
        else:
            ST["bmach_ev1"] = VF(VEC.tensor_tensor(colmax[:, 2:3], colmax[:, 2:3], colmax[:, 3:4], OP.max))

    def b_machinery(grp, act_guard):
        ev1 = ST["bmach_ev1"]
        W(TEN, "dve", ev1)
        W(TEN, "dve", max(ST["ctxread"][0], ST["bcread"][0]))
        W(TEN, "act", ST["rcread"][0])
        TEN.matmul(psE[0:1, 0:128], colmax[:, 0:1], i128b[:, :], start=True, stop=True)
        ev2 = I(TEN.matmul(psE[0:1, 128:256], colmax[:, 2:3], i128b[:, :], start=True, stop=True), "pe", 1)
        ST["bmach_pe"] = ev2
        W(VEC, "pe", ev2)
        VEC.tensor_reduce(sc1[:, 0:1], psE[0:1, 0:128], AXL, OP.max)
        VF(VEC.tensor_reduce(sc1[:, 1:2], psE[0:1, 128:256], AXL, OP.max))
        VF(VEC.tensor_tensor(sc1[:, 2:3], sc1[:, 0:1], sc1[:, 1:2], OP.mult))
        with nc.allow_low_precision(reason="bf16 stability-shift scalar feeding broadcast matmul"):
            ev3 = VF(VEC.tensor_scalar(negbrow[:, :], sc1[:, 2:3], -64.0, 50.0, OP.mult, OP.add))
        W(TEN, "dve", ev3)
        ev4 = I(TEN.matmul(psE[:, 256:257], ones1[:, :], negbrow[:, :], start=True, stop=True), "pe", 1)
        W(VEC, "pe", ev4)
        W(VEC, "act", act_guard)
        ev5 = I(VEC.tensor_copy(negB[:, :], psE[:, 256:257]), "dve", 1)
        ST["ctxread"][0] = max(ST["ctxread"][0], ev5)
        ST["bcread"][0] = max(ST["bcread"][0], ev5)
        return ev5

    def emit_pv(p, pb, ev_exp_a, ev_exp, mask_ev_a, mask_ev, hl, npair, ctxps, cb, lo=(0, 0)):
        W(TEN, "act", ev_exp_a)
        if mask_ev_a:
            W(TEN, "dve", mask_ev_a)
        if p == 0:
            W(TEN, "dve", ST["bcread"][cb])
            W(TEN, "act", ST["rcread"][cb])
        TEN.matmul(ctxps[0:65, lo[0]:512], vt4[:, 2 * p, hl, :], probs[:, pb, lo[0]:512],
                   start=(p == 0), stop=False)
        W(TEN, "act", ev_exp)
        if mask_ev:
            W(TEN, "dve", mask_ev)
        ev_pv = I(TEN.matmul(ctxps[0:65, lo[1]:512], vt4[:, 2 * p + 1, hl, :],
                             probs[:, pb, 512 + lo[1]:1024],
                             start=False, stop=(p == npair - 1)), "pe", 1)
        ST["pbfree"][pb] = ev_pv
        return ev_pv

    def emit_norm(cb, ctxps, pglob, half, j, ev_rc):
        bname = "C" if ST["u_norm"] % 2 == 0 else "D"
        bcps = psC if bname == "C" else psD
        ST["u_norm"] += 1
        W(TEN, "act", ev_rc)
        W(TEN, "dve", ST["ev_ps"][bname])
        ev_bcmm = I(TEN.matmul(bcps[0:64, :], ones1[0:1, 0:64], reciprow[:, :],
                               start=True, stop=True), "pe", 1)
        ST["recip_pe"] = ev_bcmm
        W(VEC, "pe", ev_bcmm)
        with nc.allow_low_precision(reason="bf16 softmax 1/sum feeding ctx normalization"):
            ev_bcv = VF(VEC.reciprocal(recipb[:, :], bcps[0:64, :]))
        ST["ev_ps"][bname] = ev_bcv
        ev_cx = I(VEC.tensor_tensor(ctxsb[64 * half:64 * half + 64, pglob, j * 512:(j + 1) * 512],
                                    ctxps[0:64, :], recipb[:, :], OP.mult), "dve", 1)
        ST["ctxread"][cb] = ev_cx
        ST["bcread"][cb] = ev_cx

    def attn_group(grp, ev_negB):
        snap = snapshot()
        final0 = None
        for arm in nc.Switch(engines=[TEN, SCA, VEC], index=[val_t, val_s, val_v], n=2):
            if arm == 1:
                restore(snap)
            qcs = QCS[arm]
            norm_pending = None
            for hl in range(4):
                li, half = hl // 2, hl % 2
                pglob = grp * 2 + li
                for j in range(4):
                    qb = qcs[j]
                    npair = (qb // 128 + 4) // 2
                    cb = ST["u"] % 2
                    ctxps = psE if cb == 0 else psF
                    pv_pending = None
                    last_pv = 0
                    for p in range(npair):
                        lo = (256, 384) if (p == npair - 1 and THIN) else (0, 0)
                        sbuf = (psA, psB)[p % 2]
                        W(TEN, "act", ST["scfree"][p % 2])
                        ev_sc1 = I(TEN.matmul(sbuf[:, lo[0]:512],
                                              k4h[:, li, (2 * p) * 128:(2 * p + 1) * 128],
                                              q4h[:, hl, j * 512 + lo[0]:(j + 1) * 512],
                                              start=True, stop=True), "pe", 1)
                        ev_sc = I(TEN.matmul(sbuf[:, 512 + lo[1]:1024],
                                             k4h[:, li, (2 * p + 1) * 128:(2 * p + 2) * 128],
                                             q4h[:, hl, j * 512 + lo[1]:(j + 1) * 512],
                                             start=True, stop=True), "pe", 1)
                        if p == 0 and norm_pending is not None:
                            emit_norm(*norm_pending)
                            norm_pending = None
                        if pv_pending is not None:
                            last_pv = emit_pv(*pv_pending)
                        pb = p % 2
                        W(SCA, "pe", ev_sc1)
                        W(SCA, "pe", ST["pbfree"][pb])
                        if p == 0 and hl == 0 and j == 0:
                            W(SCA, "dve", ev_negB)
                        ev_exp_a = I(SCA.activation(probs[:, pb, lo[0]:512], sbuf[:, lo[0]:512], AF.Exp,
                                                    bias=negB[:, :], scale=1.0), "act", 1)
                        W(SCA, "pe", ev_sc)
                        ev_exp = I(SCA.activation(probs[:, pb, 512 + lo[1]:1024], sbuf[:, 512 + lo[1]:1024],
                                                  AF.Exp, bias=negB[:, :], scale=1.0), "act", 1)
                        ST["scfree"][p % 2] = ev_exp
                        mask_ev_a = mask_ev = 0
                        if p >= npair - 2:
                            mi0 = (p - (npair - 2)) * 2
                            W(VEC, "act", ev_exp_a)
                            mask_ev_a = I(VEC.tensor_tensor(probs[:, pb, lo[0]:512], probs[:, pb, lo[0]:512],
                                                            cmask[:, mi0, lo[0]:512], OP.mult), "dve", 1)
                            W(VEC, "act", ev_exp)
                            mask_ev = I(VEC.tensor_tensor(probs[:, pb, 512 + lo[1]:1024],
                                                          probs[:, pb, 512 + lo[1]:1024],
                                                          cmask[:, mi0 + 1, lo[1]:512], OP.mult), "dve", 1)
                        pv_pending = (p, pb, ev_exp_a, ev_exp, mask_ev_a, mask_ev, hl, npair, ctxps, cb, lo)
                    last_pv = emit_pv(*pv_pending)
                    W(SCA, "pe", last_pv)
                    W(SCA, "pe", ST["recip_pe"])
                    with nc.allow_low_precision(reason="bf16 softmax rowsum row feeding broadcast matmul"):
                        ev_rc = I(SCA.activation(reciprow[:, :], ctxps[64:65, :], AF.Copy,
                                                 scale=1.0), "act", 1)
                    ST["rcread"][cb] = ev_rc
                    norm_pending = (cb, ctxps, pglob, half, j, ev_rc)
                    ST["u"] += 1
            if norm_pending is not None:
                emit_norm(*norm_pending)
                norm_pending = None
            ST["rcread"] = [ST["C"]["act"], ST["C"]["act"]]
            if arm == 0:
                final0 = snapshot()
            else:
                assert ST == final0, ("attn", grp)

    prev_norm = [0, 0, 0, 0]
    ev_ln1 = []
    ln_prev = (0, 0)
    for tc in range(8):
        buf = tc % 4
        W(SYN, f"g_x{tc}", ev_xc[tc])
        W(SYN, "dve", prev_norm[buf])
        evt = None
        for g in range(NG):
            evt = I(SYN.dma_start(xfm[:, buf, g, :],
                                  x16[tc * 512:(tc + 1) * 512, g * 128:(g + 1) * 128],
                                  transpose=True), f"s_xt{tc % 4}", 16)
        src = (lambda g, b=buf: xfm[:, b, g, :])
        pw = [(f"s_xt{tc % 4}", evt)] + ([("s_const", ev_consts), ("dve", ev_vtones)] if tc == 0 else [])
        env, evv = ln_chunk(onesc[:, :], src, src, src,
                            (lambda g, t=tc: xln1[:, g, t * 512:(t + 1) * 512]),
                            pre_mm_wait=pw, pre_sq_wait=[(f"s_xt{tc % 4}", evt)], prev=ln_prev,
                            mid_emit=(lambda t=tc: kv_proj_chunk(0, t - 1)) if tc >= 1 else None)
        ln_prev = (evv, env)
        prev_norm[buf] = env
        ev_ln1.append(env)
    if limit == "ln1":
        return _finish([xln1[:, :, :]])

    act_guard = 0
    kv_proj_chunk(0, 7)
    bmach_k_reduces(0)
    for grp in range(4):
        if grp > 0:
            kv_proj(grp)
        q_proj(grp)
        if grp < 3:
            W(GPS, "pe", ST["C"]["pe"])
            load_wqkv(grp + 1)
        if limit == "proj0":
            return _finish([k4h[:, :, :], q4h[:, :, :], vt4[:, :, :, :]])
        ev_negB = b_machinery(grp, act_guard)
        if limit == "bmach0":
            return _finish([negB[:, :]])
        attn_group(grp, ev_negB)
        act_guard = ST["C"]["act"]
        if limit == "attn0":
            return _finish([ctxsb[:, :, :]])

    if limit == "attn":
        return _finish([ctxsb[:, :, :]])
    ev_attn_pe_end = ST["C"]["pe"]
    ev_attn_act_end = ST["C"]["act"]
    ev_attn_dve_end = ST["C"]["dve"]
    ev_ctx_done = max(max(ST["ctxread"]), max(ST["bcread"]))

    W(SYN, "pe", ev_attn_pe_end)
    snap = snapshot()
    final0 = None
    for arm in nc.Switch(engines=[SYN], index=[val_sp], n=2):
        if arm == 1:
            restore(snap)
        loc = 0
        parts = []
        for (a, b) in OWN_RANGES[arm]:
            if b - a > 1024:
                m = (a + b) // 2
                parts += [(a, m), (m, b)]
            else:
                parts += [(a, b)]
        for (a, b) in parts:
            for g in range(NG):
                I(SYN.dma_start(xown[:, g, loc:loc + (b - a)],
                                x16[a:b, g * 128:(g + 1) * 128], transpose=True), "s_xo", 16)
            loc += b - a
        if arm == 0:
            final0 = snapshot()
        else:
            assert ST == final0, "xown"
    ev_xown = ST["C"]["s_xo"]

    ev_wo_tc = [None] * 4
    ev_ln2 = [None] * 4
    first_wo = [True]

    def wo_chunk(tc):
        ev_evac = None
        for dc in range(8):
            ps, bn = proj_bank()
            W(TEN, "dve", ST["ev_ps"][bn])
            mm = None
            for pg in range(8):
                if first_wo[0] and pg == 0:
                    W(TEN, "g_wo", ev_wo)
                    W(TEN, "dve", ev_ctx_done)
                    W(TEN, "act", ev_attn_act_end)
                    first_wo[0] = False
                mm = TEN.matmul(ps[:, :], wosb[:, pg, dc * 128:(dc + 1) * 128],
                                ctxsb[:, pg, tc * 512:(tc + 1) * 512],
                                start=(pg == 0), stop=(pg == 7))
            ev = I(mm, "pe", 1)
            W(VEC, "pe", ev)
            if tc == 0 and dc == 0:
                W(VEC, "s_xo", ev_xown)
            ev_evac = I(VEC.scalar_tensor_tensor(x1[:, dc, tc * 512:(tc + 1) * 512], ps[:, :],
                                                 botsb[:, dc:dc + 1],
                                                 xown[:, dc, tc * 512:(tc + 1) * 512],
                                                 OP.add, OP.add), "dve", 1)
            ST["ev_ps"][bn] = ev_evac
        ev_wo_tc[tc] = ev_evac

    def gating_logits(tjs):
        for tj in tjs:
            ps, bn = proj_bank()
            W(TEN, "dve", ST["ev_ps"][bn])
            if tj == 0:
                W(TEN, "g_small", ev_gsmall)
            W(TEN, "dve", ev_ln2[tj // 4])
            mm = None
            for g in range(NG):
                mm = TEN.matmul(ps[:, 0:8], xln2[:, g, tj * 128:(tj + 1) * 128], wgsb[:, g, :],
                                start=(g == 0), stop=(g == 7))
            ev = I(mm, "pe", 1)
            W(VEC, "pe", ev)
            ev_lg = I(VEC.tensor_copy(logits[:, tj, :], ps[:, 0:8]), "dve", 1)
            ST["ev_ps"][bn] = ev_lg

    wo_chunk(0)
    for tc in range(4):
        pw = [("dve", ev_wo_tc[tc])]
        if tc == 0:
            pw.append(("act", ev_attn_act_end))
        src_f32 = (lambda g, t=tc: x1[:, g, t * 512:(t + 1) * 512])
        mid = (lambda t=tc: wo_chunk(t + 1)) if tc < 3 else None
        env, evv = ln_chunk(onescf[:, :], src_f32, src_f32, src_f32,
                            (lambda g, t=tc: xln2[:, g, t * 512:(t + 1) * 512]),
                            pre_mm_wait=pw, pre_sq_wait=[("dve", ev_wo_tc[tc])], prev=ln_prev,
                            mid_emit=mid)
        ln_prev = (evv, env)
        ev_ln2[tc] = env

    if limit == "wo":
        return _finish([x1[:, :, :]])

    gating_logits(range(0, 16))
    W(VEC, "dve", max(ST["ev_ps"]["C"], ST["ev_ps"]["D"]))
    def stage_fence(instr):
        ev = I(instr, "dve", 1)
        VEC.wait_ge(sems["dve"], ev)
        return ev
    for tj in range(15):
        VEC.tensor_reduce(gcolM[:, tj:tj + 1], logits[:, tj, :], AXL, OP.max)
    stage_fence(VEC.tensor_reduce(gcolM[:, 15:16], logits[:, 15, :], AXL, OP.max))
    ev_m1 = stage_fence(VEC.tensor_scalar(gcolN[:, :], gcolM[:, :], -1.0, None, OP.mult))
    W(SCA, "dve", ev_m1)
    ev_e = None
    for tj in range(16):
        ev_e = I(SCA.activation(gtE[:, tj, :], logits[:, tj, :], AF.Exp, bias=gcolN[:, tj:tj + 1]), "act", 1)
    for tj in range(15):
        VEC.tensor_scalar(gtK[:, tj, :], logits[:, tj, :], gcolM[:, tj:tj + 1], None, OP.is_ge)
    stage_fence(VEC.tensor_scalar(gtK[:, 15, :], logits[:, 15, :], gcolM[:, 15:16], None, OP.is_ge))
    for tj in range(15):
        VEC.scalar_tensor_tensor(gtK[:, tj, :], gtK[:, tj, :], -1e9, logits[:, tj, :], OP.mult, OP.add)
    stage_fence(VEC.scalar_tensor_tensor(gtK[:, 15, :], gtK[:, 15, :], -1e9, logits[:, 15, :], OP.mult, OP.add))
    for tj in range(15):
        VEC.tensor_reduce(gcolM[:, tj:tj + 1], gtK[:, tj, :], AXL, OP.max)
    stage_fence(VEC.tensor_reduce(gcolM[:, 15:16], gtK[:, 15, :], AXL, OP.max))
    for tj in range(15):
        VEC.tensor_scalar(gtK[:, tj, :], logits[:, tj, :], gcolM[:, tj:tj + 1], None, OP.is_ge)
    stage_fence(VEC.tensor_scalar(gtK[:, 15, :], logits[:, 15, :], gcolM[:, 15:16], None, OP.is_ge))
    W(VEC, "act", ev_e)
    for tj in range(15):
        VEC.tensor_tensor(gtE[:, tj, :], gtE[:, tj, :], gtK[:, tj, :], OP.mult)
    stage_fence(VEC.tensor_tensor(gtE[:, 15, :], gtE[:, 15, :], gtK[:, 15, :], OP.mult))
    for tj in range(15):
        VEC.tensor_reduce(gcolD[:, tj:tj + 1], gtE[:, tj, :], AXL, OP.add)
    stage_fence(VEC.tensor_reduce(gcolD[:, 15:16], gtE[:, 15, :], AXL, OP.add))
    stage_fence(VEC.reciprocal(gcolN[:, :], gcolD[:, :]))
    ev_wtok = []
    with nc.allow_low_precision(reason="bf16 gating weights for broadcast matmul (top-2 selection done in fp32)"):
        for tj in range(16):
            ev_wtok.append(I(VEC.tensor_scalar(wtokb[:, tj, :], gtE[:, tj, :], gcolN[:, tj:tj + 1], None, OP.mult), "dve", 1))

    ev_wT = None
    for tj in range(16):
        ps, bn = proj_bank()
        W(TEN, "dve", ST["ev_ps"][bn])
        W(TEN, "dve", ev_wtok[tj])
        ev = I(TEN.matmul(ps[0:8, 0:128], wtokb[:, tj, :], i128b[:, :], start=True, stop=True), "pe", 1)
        W(VEC, "pe", ev)
        ev_wT = I(VEC.tensor_copy(w_T[:, tj * 128:(tj + 1) * 128], ps[0:8, 0:128]), "dve", 1)
        ST["ev_ps"][bn] = ev_wT

    if limit == "ln2":
        return _finish([xln2[:, :, :], w_T[:, :]])
    ev_wo_mm_done = ST["C"]["pe"]
    ev_w1l = {}
    ev_w2l = {}

    def load_w1(e, quarters=(0, 1, 2, 3), wait_pe=0):
        W(GPS, "pe", wait_pe)
        evs = []
        for q in quarters:
            evs.append(I(GPS.dma_start(
                w1sb[:, :, q * 512:(q + 1) * 512],
                w1_h[e * D:(e + 1) * D, q * 512:(q + 1) * 512].rearrange("(r p) n -> p r n", p=128)), f"g_w1q{q}", 16))
        ev_w1l[e] = evs

    def load_w2(e, wait_pe=0):
        W(GPS, "pe", wait_pe)
        evs = []
        for hh in range(2):
            evs.append(I(GPS.dma_start(
                w2sb[:, hh * 8:(hh + 1) * 8, :],
                w2_h[e * NI + hh * 1024:e * NI + (hh + 1) * 1024, :].rearrange("(r p) n -> p r n", p=128)), f"g_w2h{hh}", 16))
        ev_w2l[e] = evs

    load_w1(0, wait_pe=ev_wo_mm_done)
    load_w2(0, wait_pe=ev_wo_mm_done)

    H1B = [(psA, 0), (psA, 512), (psB, 0), (psB, 512)]
    h1_gelu_ev = [0, 0, 0, 0]
    h1s = 0
    ev_y_tc_prev = 0
    ev_pool = [0, 0]
    ev_wb = [0, 0]
    wb_par = 0
    ev_gelu_tc = 0
    ev_w1_lastmm = 0
    ev_w2_lastmm = 0
    ev_castout = [0, 0]
    ev_pool_tc = [0, 0, 0, 0]
    ln2_act_end = ST["C"]["act"]

    n_experts = 1 if limit == "ffn1" else NE
    for e in range(n_experts):
        for tc in range(4):
            first_gelu = True
            for ic in range(16):
                ps, colo = H1B[h1s % 4]
                W(TEN, "act", h1_gelu_ev[h1s % 4])
                if e == 0 and tc == 0 and ic == 0:
                    W(TEN, "act", ln2_act_end)
                    W(TEN, "dve", ev_ln2[3])
                mm = None
                for r in range(NG):
                    if ic in (0, 8) and r == 0:
                        W(TEN, f"g_w1q{0 if ic < 8 else 2}", ev_w1l[e][0 if ic < 8 else 2])
                    if ic in (4, 12) and r == 0:
                        W(TEN, f"g_w1q{1 if ic < 8 else 3}", ev_w1l[e][1 if ic < 8 else 3])
                    mm = TEN.matmul(ps[:, colo:colo + 512], w1sb[:, r, ic * 128:(ic + 1) * 128],
                                    xln2[:, r, tc * 512:(tc + 1) * 512],
                                    start=(r == 0), stop=(r == 7))
                ev_mm = I(mm, "pe", 1)
                W(SCA, "pe", ev_mm)
                if first_gelu:
                    W(SCA, "pe", ev_y_tc_prev)
                    first_gelu = False
                h1_gelu_ev[h1s % 4] = I(SCA.activation(h1[:, ic, :], ps[:, colo:colo + 512], AF.Gelu,
                                                       bias=b1tsb[:, e, ic:ic + 1]), "act", 1)
                h1s += 1
            ev_gelu_tc = h1_gelu_ev[(h1s - 1) % 4]
            if tc == 3:
                ev_w1_lastmm = ST["C"]["pe"]
                if e < NE - 1:
                    load_w1(e + 1, wait_pe=ev_w1_lastmm)
            wps = (psE, psF)[wb_par % 2]
            W(TEN, "dve", ev_wb[wb_par % 2])
            if e == 0 and tc == 0:
                W(TEN, "dve", max(ST["ctxread"][0], ST["ctxread"][1], ST["bcread"][0], ST["bcread"][1], ev_wT))
            else:
                W(TEN, "dve", ev_wT)
            ev = I(TEN.matmul(wps[:, :], eyeg[:, e, :], w_T[:, tc * 512:(tc + 1) * 512],
                              start=True, stop=True), "pe", 1)
            W(VEC, "pe", ev)
            ev_wb[wb_par % 2] = VF(VEC.tensor_copy(wbsb[:, tc % 2, :], wps[:, :]))
            ev_wb_cur = ev_wb[wb_par % 2]
            wb_par += 1
            for dc in range(8):
                ps, bn = proj_bank()
                W(TEN, "dve", ST["ev_ps"][bn])
                if dc == 0:
                    W(TEN, "act", ev_gelu_tc)
                    W(TEN, "g_w2h1", ev_w2l[e][1])
                    W(TEN, "g_w2h0", ev_w2l[e][0])
                mm = None
                for r in range(16):
                    mm = TEN.matmul(ps[:, :], w2sb[:, r, dc * 128:(dc + 1) * 128], h1[:, r, :],
                                    start=(r == 0), stop=(r == 15))
                ev_y = I(mm, "pe", 1)
                yb = dc % 2
                W(VEC, "pe", ev_y)
                W(VEC, "pool", ev_pool[yb])
                ev_stt = I(VEC.scalar_tensor_tensor(ytmp[:, yb, :], ps[:, :], b2tsb[:, e, dc:dc + 1],
                                                    wbsb[:, tc % 2, :], OP.add, OP.mult), "dve", 1)
                ST["ev_ps"][bn] = ev_stt
                W(GPS, "dve", ev_stt)
                ev_pool[yb] = I(GPS.tensor_tensor(x1[:, dc, tc * 512:(tc + 1) * 512],
                                                  x1[:, dc, tc * 512:(tc + 1) * 512],
                                                  ytmp[:, yb, :], OP.add), "pool", 1)
            ev_y_tc_prev = ST["C"]["pe"]
            if tc == 3:
                ev_w2_lastmm = ST["C"]["pe"]
                if e < NE - 1:
                    load_w2(e + 1, wait_pe=ev_w2_lastmm)
            if e == NE - 1:
                ev_pool_tc[tc] = max(ev_pool)

    if limit in ("ffn", "ffn1"):
        return _finish([x1[:, :, :], h1[:, :, :], wbsb[:, :, :]])

    for tc in range(4):
        W(VEC, "pool", ev_pool_tc[tc])
        if tc == 0:
            W(VEC, "pe", ev_w1_lastmm)
        ev_x2b = I(VEC.tensor_copy(x2b[:, :, tc * 512:(tc + 1) * 512],
                                   x1[:, :, tc * 512:(tc + 1) * 512]), "dve", 1)
        W(SYN, "dve", ev_x2b)
        if tc == 0:
            W(SYN, "dve", ev_attn_dve_end)
            W(SYN, "pe", ev_attn_pe_end)
        ev_sp1 = I(SYN.dma_start(
            x2f16.rearrange("g p t -> p g t")[:, :, tc * 512:(tc + 1) * 512],
            x2b[:, :, tc * 512:(tc + 1) * 512]), f"s_x2s{tc}", 16)
        for tj4 in range(4):
            tj = tc * 4 + tj4
            ob = tj % 2
            W(SYN, f"s_x2s{tc}", ev_sp1)
            W(SYN, f"g_out{ob}", ev_castout[ob])
            ev_tr = I(SYN.dma_start(
                x2tok[:, ob, :],
                x2f16.rearrange("g p t -> (g p) t")[:, tj * 128:(tj + 1) * 128],
                transpose=True), f"s_ot{ob}", 16)
            W(GPS, f"s_ot{ob}", ev_tr)
            ev_castout[ob] = I(GPS.dma_start(yout[tj * 128:(tj + 1) * 128, :],
                                             x2tok[:, ob, :]), f"g_out{ob}", 16)
    for sname, cnt in ST["C"].items():
        if sname.startswith("g_") and cnt:
            GPS.wait_ge(sems[sname], cnt)
        if sname.startswith("s_") and cnt:
            SYN.wait_ge(sems[sname], cnt)

    ctx.close()
    return nc


_NC_CACHE = {}


def _get_nc():
    if "nc" not in _NC_CACHE:
        _NC_CACHE["nc"] = build_nc()
    return _NC_CACHE["nc"]


def kernel(L0_attn_embeddings, L0_attn_Wq, L0_attn_bq, L0_attn_Wk, L0_attn_bk,
           L0_attn_Wv, L0_attn_bv, L0_attn_Wo, L0_attn_bo,
           L0_ffn_Wg, L0_ffn_W1, L0_ffn_b1, L0_ffn_W2, L0_ffn_b2):
    f32 = np.float32
    x = np.ascontiguousarray(np.asarray(L0_attn_embeddings, dtype=f32))
    Wq = np.asarray(L0_attn_Wq, dtype=f32)
    Wk = np.asarray(L0_attn_Wk, dtype=f32)
    Wv = np.asarray(L0_attn_Wv, dtype=f32)
    bq = np.asarray(L0_attn_bq, dtype=f32)
    bk = np.asarray(L0_attn_bk, dtype=f32)
    bv = np.asarray(L0_attn_bv, dtype=f32)
    Wo = np.ascontiguousarray(np.asarray(L0_attn_Wo, dtype=f32))
    bo = np.asarray(L0_attn_bo, dtype=f32)
    Wg = np.ascontiguousarray(np.asarray(L0_ffn_Wg, dtype=f32))
    W1 = np.asarray(L0_ffn_W1, dtype=f32)
    W2 = np.asarray(L0_ffn_W2, dtype=f32)
    b1 = np.asarray(L0_ffn_b1, dtype=f32)
    b2 = np.asarray(L0_ffn_b2, dtype=f32)

    wq2 = np.ascontiguousarray(Wq.transpose(1, 0, 2).reshape(D, H * DK))
    wk2 = np.ascontiguousarray(Wk.transpose(1, 0, 2).reshape(D, H * DK))
    wv2 = np.ascontiguousarray(Wv.transpose(1, 0, 2).reshape(D, H * DK))
    w1f = np.ascontiguousarray(W1.reshape(NE * D, NI))
    w2f = np.ascontiguousarray(W2.reshape(NE * NI, D))
    bqr = np.ascontiguousarray(bq.reshape(1, 8, 128))
    bkr = np.ascontiguousarray(bk.reshape(1, 8, 128))
    bvr = np.ascontiguousarray(bv.reshape(1, 4, 256))
    bot = np.ascontiguousarray(bo.reshape(8, 128).T)
    b1t = np.ascontiguousarray(b1.reshape(8, 16, 128).transpose(2, 0, 1))
    b2t = np.ascontiguousarray(b2.reshape(8, 8, 128).transpose(2, 0, 1))

    s_idx = np.arange(128)[:, None]
    q_idx = np.arange(512)[None, :]
    cmask = np.stack([(128 * i + s_idx <= q_idx) for i in range(4)], axis=1).astype(ml_dtypes.bfloat16)
    cmask = np.ascontiguousarray(cmask)
    i128 = np.eye(128, dtype=ml_dtypes.bfloat16)
    eyeg = np.zeros((8, 8, 128), dtype=ml_dtypes.bfloat16)
    for e in range(8):
        eyeg[e, e, :] = 1
    shared = dict(wq2=wq2, wk2=wk2, wv2=wv2, wo=Wo, w1=w1f, w2=w2f, wg=Wg,
                  bqr=bqr, bkr=bkr, bvr=bvr, bot=bot, b1t=b1t, b2t=b2t,
                  cmask=cmask, i128=i128, eyeg=eyeg)

    nc = _get_nc()
    in_maps = []
    for c in range(8):
        m = dict(shared)
        m["xb"] = np.ascontiguousarray(x[c // 2])
        m["role"] = np.array([[c % 2]], dtype=np.int32)
        in_maps.append(m)
    res = run_bass_kernel_spmd(nc, in_maps, core_ids=list(range(8)))

    out = np.empty((B, S, D), dtype=f32)
    for c in range(8):
        y = res.results[c]["yout"]
        b, role = c // 2, c % 2
        loc = 0
        for (a, e2) in OWN_RANGES[role]:
            out[b, a:e2, :] = y[loc:loc + (e2 - a), :]
            loc += e2 - a
    return out


# version 22 (final)
# speedup vs baseline: 1.2190x; 1.0107x over previous
import copy
import numpy as np
import ml_dtypes
import concourse.bass as bass
import concourse.mybir as mybir
from concourse.bass_utils import run_bass_kernel_spmd
from concourse.alu_op_type import AluOpType as OP
from contextlib import ExitStack

F32 = mybir.dt.float32
BF16 = mybir.dt.bfloat16
I32 = mybir.dt.int32
AF = mybir.ActivationFunctionType
AXL = mybir.AxisListType.X
ENG = mybir.EngineType

B, S, D, H, DK = 4, 4096, 1024, 16, 64
NE, NI = 8, 2048
NOWN = 2048
NG = 8
QCS = [[0, 512, 3072, 3584], [1024, 1536, 2048, 2560]]
OWN_RANGES = [[(0, 1024), (3072, 4096)], [(1024, 3072)]]
ARENA_SZ = 207872


THIN = True


def build_nc(limit=None, debug=False):
    nc = bass.Bass("TRN2", target_bir_lowering=False)

    xb = nc.dram_tensor("xb", [S, D], F32, kind="ExternalInput").ap()
    role_t = nc.dram_tensor("role", [1, 1], I32, kind="ExternalInput").ap()
    wq2 = nc.dram_tensor("wq2", [D, H * DK], F32, kind="ExternalInput").ap()
    wk2 = nc.dram_tensor("wk2", [D, H * DK], F32, kind="ExternalInput").ap()
    wv2 = nc.dram_tensor("wv2", [D, H * DK], F32, kind="ExternalInput").ap()
    wo_h = nc.dram_tensor("wo", [D, D], F32, kind="ExternalInput").ap()
    w1_h = nc.dram_tensor("w1", [NE * D, NI], F32, kind="ExternalInput").ap()
    w2_h = nc.dram_tensor("w2", [NE * NI, D], F32, kind="ExternalInput").ap()
    wg_h = nc.dram_tensor("wg", [D, NE], F32, kind="ExternalInput").ap()
    bqr_h = nc.dram_tensor("bqr", [1, 8, 128], F32, kind="ExternalInput").ap()
    bkr_h = nc.dram_tensor("bkr", [1, 8, 128], F32, kind="ExternalInput").ap()
    bvr_h = nc.dram_tensor("bvr", [1, 4, 256], F32, kind="ExternalInput").ap()
    bot_h = nc.dram_tensor("bot", [128, 8], F32, kind="ExternalInput").ap()
    b1t_h = nc.dram_tensor("b1t", [128, 8, 16], F32, kind="ExternalInput").ap()
    b2t_h = nc.dram_tensor("b2t", [128, 8, 8], F32, kind="ExternalInput").ap()
    cmask_h = nc.dram_tensor("cmask", [128, 4, 512], BF16, kind="ExternalInput").ap()
    i128_h = nc.dram_tensor("i128", [128, 128], BF16, kind="ExternalInput").ap()
    eyeg_h = nc.dram_tensor("eyeg", [8, 8, 128], BF16, kind="ExternalInput").ap()
    yout = nc.dram_tensor("yout", [NOWN, D], F32, kind="ExternalOutput").ap()
    dbg = nc.dram_tensor("dbg", [128, 40960], F32, kind="ExternalOutput").ap() if debug else None
    x16 = nc.dram_tensor("x16", [S, D], BF16).ap()
    x2f16 = nc.dram_tensor("x2f16", [NG, 128, NOWN], BF16).ap()

    ctx = ExitStack()
    E = ctx.enter_context
    arena = E(nc.sbuf_tensor("arena", [128, ARENA_SZ], mybir.dt.uint8))
    base_addr = nc.lookup_mloc(arena).addr

    def place(name, shape, dtype, off):
        nbytes = int(np.prod(shape[1:])) * mybir.dt.size(dtype)
        assert off % 32 == 0 and off + nbytes <= ARENA_SZ, (name, off, nbytes)
        h = nc.virtual_sbuf_tensor(name, shape, dtype)
        nc._tensor(h.name, shape, dtype, type="SB")
        mloc = nc.lookup_mloc(h)
        mloc.allocated = True
        mloc.addr = base_addr + off
        return h

    R_XLN1, R_BIG, R_CTX, R_WOW2, R_MISC = 0, 65536, 114944, 147712, 180480
    xln1 = place("xln1", [128, NG, S], BF16, R_XLN1)
    x1 = place("x1", [128, NG, NOWN], F32, R_XLN1)
    xfm = place("xfm", [128, 4, NG, 512], BF16, R_CTX)
    k4h = place("k4h", [128, 2, S], BF16, R_BIG)
    xown = place("xown", [128, NG, NOWN], BF16, R_BIG)
    xln2 = place("xln2", [128, NG, NOWN], BF16, R_BIG)
    vt4 = place("vt4", [128, 32, 4, 65], BF16, R_BIG + 16384)
    q4h = place("q4h", [128, 4, NOWN], BF16, R_BIG + 33024)
    h1 = place("h1", [128, 16, 512], BF16, R_BIG + 32768)
    probs = None
    ctxsb = place("ctxsb", [128, 8, NOWN], BF16, R_CTX)
    w1sb = place("w1sb", [128, 8, NI], BF16, R_CTX)
    x2b = place("x2b", [128, NG, NOWN], BF16, R_CTX)
    wosb = place("wosb", [128, 8, 1024], BF16, R_WOW2)
    wqkv = place("wqkv", [128, 3, 8, 256], BF16, R_WOW2 + 16384)
    w2sb = place("w2sb", [128, 16, 1024], BF16, R_WOW2)

    off = [R_MISC]

    def P(name, shape, dtype, at=None):
        o = off[0] if at is None else at
        t = place(name, shape, dtype, o)
        if at is None:
            nbytes = int(np.prod(shape[1:])) * mybir.dt.size(dtype)
            off[0] += (nbytes + 31) // 32 * 32
        return t

    cmask = place("cmask", [128, 4, 512], BF16, R_WOW2 + 28672)
    x2tok = P("x2tok", [128, 2, 1024], BF16)
    probs = place("probs", [128, 2, 1024], BF16, R_MISC)
    abuf = off[0]
    bqr = P("bqr", [1, 8, 128], BF16, at=abuf)
    bkr = P("bkr", [1, 8, 128], BF16, at=abuf + 2048)
    bvr = P("bvr", [1, 4, 256], BF16, at=abuf + 4096)
    reciprow = P("reciprow", [1, 512], BF16, at=abuf + 6144)
    recipb = P("recipb", [64, 512], BF16, at=abuf + 7168)
    colmax = P("colmax", [128, 4], BF16, at=abuf + 8192)
    ytmp = P("ytmp", [128, 2, 512], F32, at=abuf)
    wbsb = P("wbsb", [128, 2, 512], F32, at=abuf + 4096)
    off[0] = abuf + 8224
    i128b = P("i128b", [128, 128], BF16)
    eyeg = P("eyeg", [8, 8, 128], BF16)
    rbuf = off[0]
    rowsf = P("rowsf", [1, 3, 512], F32, at=rbuf)
    rowsb = P("rowsb", [1, 2, 512], BF16, at=rbuf + 6144)
    w_T = P("w_T", [8, NOWN], BF16, at=rbuf)
    off[0] = rbuf + 8192
    ones_row = P("ones_row", [1, 512], BF16)
    ones1 = P("ones1", [1, 128], BF16)
    onesc = P("onesc", [128, 1], BF16)
    onescf = P("onescf", [128, 1], F32)
    negB = P("negB", [128, 1], F32)
    sc1 = P("sc1", [1, 8], F32)
    negbrow = P("negbrow", [1, 1], BF16)
    epsr = P("epsr", [1, 1], F32)
    wgsb = P("wgsb", [128, 8, 8], BF16)
    sbuf0 = off[0]
    sqtmp = P("sqtmp", [128, 2, 512], BF16, at=sbuf0)
    logits = P("logits", [128, 16, 8], F32, at=sbuf0)
    gtE = P("gtE", [128, 16, 8], F32, at=sbuf0 + 512)
    gtK = P("gtK", [128, 16, 8], F32, at=sbuf0 + 1024)
    gcolM = P("gcolM", [128, 16], F32, at=sbuf0 + 1536)
    gcolN = P("gcolN", [128, 16], F32, at=sbuf0 + 1600)
    gcolD = P("gcolD", [128, 16], F32, at=sbuf0 + 1664)
    wtokb = P("wtokb", [128, 16, 8], BF16, at=sbuf0 + 1728)
    off[0] = sbuf0 + 2048
    botsb = P("botsb", [128, 8], F32)
    b1tsb = P("b1tsb", [128, 8, 16], F32)
    b2tsb = P("b2tsb", [128, 8, 8], F32)
    rsb = P("rsb", [1, 1], I32)
    assert off[0] <= ARENA_SZ, off[0]

    psA = E(nc.psum_tensor("psA", [128, 1024], F32, unsafe=True))
    psB = E(nc.psum_tensor("psB", [128, 1024], F32, unsafe=True))
    psC = E(nc.psum_tensor("psC", [128, 512], F32))
    psD = E(nc.psum_tensor("psD", [128, 512], F32))
    psE = E(nc.psum_tensor("psE", [128, 512], F32))
    psF = E(nc.psum_tensor("psF", [128, 512], F32))

    sems = {s: E(nc.semaphore(s)) for s in ["pe", "act", "dve", "pool",
            "g_role", "g_small", "g_wo", "g_wqkv",
            "s_const", "s_xo"]
        + [f"g_x{i}" for i in range(8)] + ["s_xt0", "s_xt1", "s_xt2", "s_xt3"]
        + [f"g_w1q{i}" for i in range(4)] + ["g_w2h0", "g_w2h1"]
        + [f"s_x2s{i}" for i in range(4)] + ["s_ot0", "s_ot1", "g_out0", "g_out1"]}
    TEN, SCA, VEC, GPS, SYN = nc.tensor, nc.scalar, nc.vector, nc.gpsimd, nc.sync
    blk = E(nc.Block())

    ST = {
        "C": {s: 0 for s in sems},
        "ev_ps": {"C": 0, "D": 0},
        "ps_par": 0,
        "scfree": [0, 0],
        "pbfree": [0, 0],
        "u": 0,
        "ctxread": [0, 0],
        "bcread": [0, 0],
        "rcread": [0, 0],
        "recip_pe": 0,
        "u_norm": 0,
        "bmach_pe": 0,
        "bmach_ev1": 0,
    }


    def _finish(dumps=()):
        if debug and dumps:
            col = 0
            for ap in dumps:
                n = int(np.prod(ap.shape[1:]))
                GPS.wait_ge(sems["pe"], ST["C"]["pe"])
                GPS.wait_ge(sems["dve"], ST["C"]["dve"])
                GPS.wait_ge(sems["act"], ST["C"]["act"])
                pat = {2: None, 3: "p a b -> p (a b)", 4: "p a b c -> p (a b c)"}[len(ap.shape)]
                flat = ap if pat is None else ap.rearrange(pat)
                I(GPS.dma_start(dbg[0:ap.shape[0], col:col + n], flat), "g_out0", 16)
                col += n
        for sname, cnt in ST["C"].items():
            if sname.startswith("g_") and cnt:
                GPS.wait_ge(sems[sname], cnt)
            if sname.startswith("s_") and cnt:
                SYN.wait_ge(sems[sname], cnt)
        ctx.close()
        return nc
    def I(instr, s, n=1):
        ST["C"][s] += n
        instr.then_inc(sems[s], n)
        return ST["C"][s]

    def W(eng, s, v):
        if v > 0:
            eng.wait_ge(sems[s], v)

    def VF(instr):
        ev = I(instr, "dve", 1)
        VEC.wait_ge(sems["dve"], ev)
        return ev

    def proj_bank():
        b = "C" if ST["ps_par"] % 2 == 0 else "D"
        ST["ps_par"] += 1
        return (psC if b == "C" else psD), b

    def snapshot():
        return copy.deepcopy(ST)

    def restore(s):
        st2 = copy.deepcopy(s)
        ST.clear()
        ST.update(st2)

    ev_role = I(GPS.dma_start(rsb[:, :], role_t), "g_role", 16)
    I(GPS.dma_start(bqr[:, :, :], bqr_h), "g_small", 16)
    I(GPS.dma_start(bkr[:, :, :], bkr_h), "g_small", 16)
    I(GPS.dma_start(bvr[:, :, :], bvr_h), "g_small", 16)
    ev_gsmall = I(GPS.dma_start(wgsb[:, :, :], wg_h.rearrange("(g p) e -> p g e", p=128)), "g_small", 16)

    ev_wqkv = {}

    def load_wqkv(grp):
        cs, ce = grp * 256, (grp + 1) * 256
        I(GPS.dma_start(wqkv[:, 0, :, :], wq2[:, cs:ce].rearrange("(g p) n -> p g n", p=128)), "g_wqkv", 16)
        I(GPS.dma_start(wqkv[:, 1, :, :], wk2[:, cs:ce].rearrange("(g p) n -> p g n", p=128)), "g_wqkv", 16)
        ev_wqkv[grp] = I(GPS.dma_start(wqkv[:, 2, :, :], wv2[:, cs:ce].rearrange("(g p) n -> p g n", p=128)), "g_wqkv", 16)

    load_wqkv(0)
    ev_xc = []
    for tc in range(8):
        ev_xc.append(I(GPS.dma_start(x16[tc * 512:(tc + 1) * 512, :], xb[tc * 512:(tc + 1) * 512, :]), f"g_x{tc}", 16))
    ev_wo = I(GPS.dma_start(wosb[:, :, :], wo_h.rearrange("(P p) n -> p P n", p=128)), "g_wo", 16)

    I(SYN.dma_start(cmask[:, :, :], cmask_h), "s_const", 16)
    I(SYN.dma_start(i128b[:, :], i128_h), "s_const", 16)
    I(SYN.dma_start(eyeg[:, :, :], eyeg_h), "s_const", 16)
    I(SYN.dma_start(botsb[:, :], bot_h), "s_const", 16)
    I(SYN.dma_start(b1tsb[:, :, :], b1t_h), "s_const", 16)
    ev_consts = I(SYN.dma_start(b2tsb[:, :, :], b2t_h), "s_const", 16)

    VEC.memset(ones_row[:, :], 1.0)
    VEC.memset(ones1[:, :], 1.0)
    VEC.memset(onesc[:, :], 1.0)
    VEC.memset(onescf[:, :], 1.0)
    VEC.memset(epsr[:, :], 1e-5)
    VEC.memset(q4h[:, :, :], 0.0)
    ev_vtones = I(VEC.memset(vt4[:, :, :, 64:65], 1.0), "dve", 1)

    for eng in (TEN, SCA, VEC, SYN):
        W(eng, "g_role", ev_role)
    val_t = nc.values_load(rsb[:1, :1], engines=[ENG.PE], skip_runtime_bounds_check=True)
    val_s = nc.values_load(rsb[:1, :1], engines=[ENG.Activation], skip_runtime_bounds_check=True)
    val_v = nc.values_load(rsb[:1, :1], engines=[ENG.DVE], skip_runtime_bounds_check=True)
    val_sp = nc.values_load(rsb[:1, :1], engines=[ENG.SP], skip_runtime_bounds_check=True)

    def ln_chunk(sum_lhsT, sum_rhs_g, sq_src_g, norm_src_g, out_g,
                 pre_mm_wait=(), pre_sq_wait=(), prev=(0, 0), mid_emit=None):
        pre_mm_wait = list(pre_mm_wait) + [("dve", prev[0])]
        pre_sq_wait = list(pre_sq_wait) + [("dve", prev[1])]
        mm_evs = []
        for g in range(NG):
            if g == 0:
                for s, v in pre_sq_wait:
                    W(SCA, s, v)
            if g >= 2:
                W(SCA, "pe", mm_evs[g - 2])
            sq_ev = I(SCA.activation(sqtmp[:, g % 2, :], sq_src_g(g), AF.Square), "act", 1)
            if g == 0:
                for s, v in pre_mm_wait:
                    W(TEN, s, v)
            TEN.matmul(psA[0:1, 0:512], sum_lhsT, sum_rhs_g(g), start=(g == 0), stop=(g == NG - 1))
            W(TEN, "act", sq_ev)
            mm_evs.append(I(TEN.matmul(psA[0:1, 512:1024], onesc[:, :], sqtmp[:, g % 2, :],
                                       start=(g == 0), stop=(g == NG - 1)), "pe", 1))
        ev_mm = mm_evs[-1]
        if mid_emit is not None:
            mid_emit()
        W(VEC, "pe", ev_mm)
        VF(VEC.tensor_scalar(rowsf[:, 0, :], psA[0:1, 0:512], 1.0 / 1024.0, None, OP.mult))
        VF(VEC.tensor_tensor(rowsf[:, 1, :], rowsf[:, 0, :], rowsf[:, 0, :], OP.mult))
        ev_var = VF(VEC.scalar_tensor_tensor(rowsf[:, 2, :], psA[0:1, 512:1024], 1.0 / 1024.0,
                                             rowsf[:, 1, :], OP.mult, OP.subtract))
        W(SCA, "dve", ev_var)
        ev_sd = I(SCA.activation(rowsf[:, 1, :], rowsf[:, 2, :], AF.Sqrt, bias=epsr[:, :]), "act", 1)
        W(VEC, "act", ev_sd)
        VF(VEC.reciprocal(rowsf[:, 2, :], rowsf[:, 1, :]))
        with nc.allow_low_precision(reason="bf16 LN mean/rstd rows feeding broadcast matmul (values computed in fp32)"):
            VEC.tensor_copy(rowsb[:, 0, :], rowsf[:, 0, :])
            ev_rows = I(VEC.tensor_copy(rowsb[:, 1, :], rowsf[:, 2, :]), "dve", 1)
        W(TEN, "dve", ev_rows)
        TEN.matmul(psB[:, 0:512], ones1[:, :], rowsb[:, 0, :], start=True, stop=True)
        ev_bc = I(TEN.matmul(psB[:, 512:1024], ones1[:, :], rowsb[:, 1, :], start=True, stop=True), "pe", 1)
        W(VEC, "pe", ev_bc)
        ev_nrm = None
        for g in range(NG):
            VF(VEC.tensor_tensor(sqtmp[:, g % 2, :], norm_src_g(g), psB[:, 0:512], OP.subtract))
            ev_nrm = I(VEC.tensor_tensor(out_g(g), sqtmp[:, g % 2, :], psB[:, 512:1024], OP.mult), "dve", 1)
        return ev_nrm, ev_var

    def kv_proj_chunk(grp, sc):
        for li in range(2):
            pglob = grp * 2 + li
            ps, bn = proj_bank()
            W(TEN, "dve", ST["ev_ps"][bn])
            if sc == 0 and li == 0:
                W(TEN, "g_wqkv", ev_wqkv[grp])
                W(TEN, "g_small", ev_gsmall)
            for g in range(NG):
                if g == 0 and grp == 0:
                    W(TEN, "dve", ev_ln1[sc])
                TEN.matmul(ps[:, :], wqkv[:, 1, g, li * 128:(li + 1) * 128],
                           xln1[:, g, sc * 512:(sc + 1) * 512], start=(g == 0), stop=False)
            ev = I(TEN.matmul(ps[:, :], bkr[:, pglob, :], ones_row[:, :], start=False, stop=True), "pe", 1)
            W(VEC, "pe", ev)
            ST["ev_ps"][bn] = I(VEC.tensor_copy(k4h[:, li, sc * 512:(sc + 1) * 512], ps[:, :]), "dve", 1)
        for s4 in range(4):
            sc128 = 4 * sc + s4
            ps, bn = proj_bank()
            W(TEN, "dve", ST["ev_ps"][bn])
            for g in range(NG):
                TEN.matmul(ps[:, 0:256], xln1[:, g, sc128 * 128:(sc128 + 1) * 128], wqkv[:, 2, g, :],
                           start=(g == 0), stop=False)
            ev = I(TEN.matmul(ps[:, 0:256], ones1[:, :], bvr[:, grp, :], start=False, stop=True), "pe", 1)
            W(VEC, "pe", ev)
            if sc128 == 0 and grp == 0:
                W(VEC, "dve", ev_vtones)
            ST["ev_ps"][bn] = I(VEC.tensor_copy(vt4[:, sc128, :, 0:64],
                                                ps[:, 0:256].rearrange("p (a b) -> p a b", a=4)), "dve", 1)

    def kv_proj(grp):
        for sc in range(8):
            kv_proj_chunk(grp, sc)
        bmach_k_reduces(grp)

    def q_proj(grp):
        snap = snapshot()
        final0 = None
        for arm in nc.Switch(engines=[TEN, VEC], index=[val_t, val_v], n=2):
            if arm == 1:
                restore(snap)
            qcs = QCS[arm]
            for li in range(2):
                pglob = grp * 2 + li
                for j in range(4):
                    ps, bn = proj_bank()
                    W(TEN, "dve", ST["ev_ps"][bn])
                    for g in range(NG):
                        TEN.matmul(ps[:, :], wqkv[:, 0, g, li * 128:(li + 1) * 128],
                                   xln1[:, g, qcs[j]:qcs[j] + 512], start=(g == 0), stop=False)
                    ev = I(TEN.matmul(ps[:, :], bqr[:, pglob, :], ones_row[:, :],
                                      start=False, stop=True), "pe", 1)
                    W(VEC, "pe", ev)
                    VEC.tensor_scalar(q4h[0:64, 2 * li, j * 512:(j + 1) * 512],
                                      ps[0:64, :], 0.125, None, OP.mult)
                    ST["ev_ps"][bn] = I(VEC.tensor_scalar(q4h[64:128, 2 * li + 1, j * 512:(j + 1) * 512],
                                                          ps[64:128, :], 0.125, None, OP.mult), "dve", 1)
                    bmach_q_step(li, j, li == 0 and j == 0)
            if arm == 0:
                final0 = snapshot()
            else:
                assert ST == final0, ("qproj", grp)

    def bmach_k_reduces(grp):
        W(VEC, "pe", ST.get("bmach_pe", 0))
        VEC.tensor_reduce(colmax[:, 0:1], k4h[:, 0, :], AXL, OP.max, apply_absolute_value=True)
        VF(VEC.tensor_reduce(colmax[:, 1:2], k4h[:, 1, :], AXL, OP.max, apply_absolute_value=True))
        VF(VEC.tensor_tensor(colmax[:, 0:1], colmax[:, 0:1], colmax[:, 1:2], OP.max))

    def bmach_q_step(li, j, first):
        W(VEC, "dve", ev_vtones)
        W(VEC, "dve", max(ST["ev_ps"]["C"], ST["ev_ps"]["D"]))
        VEC.tensor_reduce(colmax[:, 1:2], q4h[:, 2 * li, j * 512:(j + 1) * 512], AXL, OP.max, apply_absolute_value=True)
        VF(VEC.tensor_reduce(colmax[:, 3:4], q4h[:, 2 * li + 1, j * 512:(j + 1) * 512], AXL, OP.max, apply_absolute_value=True))
        VF(VEC.tensor_tensor(colmax[:, 3:4], colmax[:, 3:4], colmax[:, 1:2], OP.max))
        if first:
            ST["bmach_ev1"] = VF(VEC.tensor_copy(colmax[:, 2:3], colmax[:, 3:4]))
        else:
            ST["bmach_ev1"] = VF(VEC.tensor_tensor(colmax[:, 2:3], colmax[:, 2:3], colmax[:, 3:4], OP.max))

    def b_machinery(grp, act_guard):
        ev1 = ST["bmach_ev1"]
        W(TEN, "dve", ev1)
        W(TEN, "dve", max(ST["ctxread"][0], ST["bcread"][0]))
        W(TEN, "act", ST["rcread"][0])
        TEN.matmul(psE[0:1, 0:128], colmax[:, 0:1], i128b[:, :], start=True, stop=True)
        ev2 = I(TEN.matmul(psE[0:1, 128:256], colmax[:, 2:3], i128b[:, :], start=True, stop=True), "pe", 1)
        ST["bmach_pe"] = ev2
        W(VEC, "pe", ev2)
        VEC.tensor_reduce(sc1[:, 0:1], psE[0:1, 0:128], AXL, OP.max)
        VF(VEC.tensor_reduce(sc1[:, 1:2], psE[0:1, 128:256], AXL, OP.max))
        VF(VEC.tensor_tensor(sc1[:, 2:3], sc1[:, 0:1], sc1[:, 1:2], OP.mult))
        with nc.allow_low_precision(reason="bf16 stability-shift scalar feeding broadcast matmul"):
            ev3 = VF(VEC.tensor_scalar(negbrow[:, :], sc1[:, 2:3], -64.0, 50.0, OP.mult, OP.add))
        W(TEN, "dve", ev3)
        ev4 = I(TEN.matmul(psE[:, 256:257], ones1[:, :], negbrow[:, :], start=True, stop=True), "pe", 1)
        W(VEC, "pe", ev4)
        W(VEC, "act", act_guard)
        ev5 = I(VEC.tensor_copy(negB[:, :], psE[:, 256:257]), "dve", 1)
        ST["ctxread"][0] = max(ST["ctxread"][0], ev5)
        ST["bcread"][0] = max(ST["bcread"][0], ev5)
        return ev5

    def emit_pv(p, pb, ev_exp_a, ev_exp, mask_ev_a, mask_ev, hl, npair, ctxps, cb, lo=(0, 0)):
        W(TEN, "act", ev_exp_a)
        if mask_ev_a:
            W(TEN, "dve", mask_ev_a)
        if p == 0:
            W(TEN, "dve", ST["bcread"][cb])
            W(TEN, "act", ST["rcread"][cb])
        TEN.matmul(ctxps[0:65, lo[0]:512], vt4[:, 2 * p, hl, :], probs[:, pb, lo[0]:512],
                   start=(p == 0), stop=False)
        W(TEN, "act", ev_exp)
        if mask_ev:
            W(TEN, "dve", mask_ev)
        ev_pv = I(TEN.matmul(ctxps[0:65, lo[1]:512], vt4[:, 2 * p + 1, hl, :],
                             probs[:, pb, 512 + lo[1]:1024],
                             start=False, stop=(p == npair - 1)), "pe", 1)
        ST["pbfree"][pb] = ev_pv
        return ev_pv

    def emit_norm(cb, ctxps, pglob, half, j, ev_rc):
        bname = "C" if ST["u_norm"] % 2 == 0 else "D"
        bcps = psC if bname == "C" else psD
        ST["u_norm"] += 1
        W(TEN, "act", ev_rc)
        W(TEN, "dve", ST["ev_ps"][bname])
        ev_bcmm = I(TEN.matmul(bcps[0:64, :], ones1[0:1, 0:64], reciprow[:, :],
                               start=True, stop=True), "pe", 1)
        ST["recip_pe"] = ev_bcmm
        W(VEC, "pe", ev_bcmm)
        with nc.allow_low_precision(reason="bf16 softmax 1/sum feeding ctx normalization"):
            ev_bcv = VF(VEC.reciprocal(recipb[:, :], bcps[0:64, :]))
        ST["ev_ps"][bname] = ev_bcv
        ev_cx = I(VEC.tensor_tensor(ctxsb[64 * half:64 * half + 64, pglob, j * 512:(j + 1) * 512],
                                    ctxps[0:64, :], recipb[:, :], OP.mult), "dve", 1)
        ST["ctxread"][cb] = ev_cx
        ST["bcread"][cb] = ev_cx

    def attn_group(grp, ev_negB):
        snap = snapshot()
        final0 = None
        for arm in nc.Switch(engines=[TEN, SCA, VEC], index=[val_t, val_s, val_v], n=2):
            if arm == 1:
                restore(snap)
            qcs = QCS[arm]
            norm_pending = None
            for hl in range(4):
                li, half = hl // 2, hl % 2
                pglob = grp * 2 + li
                for j in range(4):
                    qb = qcs[j]
                    npair = (qb // 128 + 4) // 2
                    cb = ST["u"] % 2
                    ctxps = psE if cb == 0 else psF
                    pv_pending = None
                    last_pv = 0
                    for p in range(npair):
                        lo = (256, 384) if (p == npair - 1 and THIN) else (0, 0)
                        sbuf = (psA, psB)[p % 2]
                        W(TEN, "act", ST["scfree"][p % 2])
                        ev_sc1 = I(TEN.matmul(sbuf[:, lo[0]:512],
                                              k4h[:, li, (2 * p) * 128:(2 * p + 1) * 128],
                                              q4h[:, hl, j * 512 + lo[0]:(j + 1) * 512],
                                              start=True, stop=True), "pe", 1)
                        ev_sc = I(TEN.matmul(sbuf[:, 512 + lo[1]:1024],
                                             k4h[:, li, (2 * p + 1) * 128:(2 * p + 2) * 128],
                                             q4h[:, hl, j * 512 + lo[1]:(j + 1) * 512],
                                             start=True, stop=True), "pe", 1)
                        if p == 0 and norm_pending is not None:
                            emit_norm(*norm_pending)
                            norm_pending = None
                        if pv_pending is not None:
                            last_pv = emit_pv(*pv_pending)
                        pb = p % 2
                        W(SCA, "pe", ev_sc1)
                        W(SCA, "pe", ST["pbfree"][pb])
                        if p == 0 and hl == 0 and j == 0:
                            W(SCA, "dve", ev_negB)
                        ev_exp_a = I(SCA.activation(probs[:, pb, lo[0]:512], sbuf[:, lo[0]:512], AF.Exp,
                                                    bias=negB[:, :], scale=1.0), "act", 1)
                        W(SCA, "pe", ev_sc)
                        ev_exp = I(SCA.activation(probs[:, pb, 512 + lo[1]:1024], sbuf[:, 512 + lo[1]:1024],
                                                  AF.Exp, bias=negB[:, :], scale=1.0), "act", 1)
                        ST["scfree"][p % 2] = ev_exp
                        mask_ev_a = mask_ev = 0
                        if p >= npair - 2:
                            mi0 = (p - (npair - 2)) * 2
                            W(VEC, "act", ev_exp_a)
                            mask_ev_a = I(VEC.tensor_tensor(probs[:, pb, lo[0]:512], probs[:, pb, lo[0]:512],
                                                            cmask[:, mi0, lo[0]:512], OP.mult), "dve", 1)
                            W(VEC, "act", ev_exp)
                            mask_ev = I(VEC.tensor_tensor(probs[:, pb, 512 + lo[1]:1024],
                                                          probs[:, pb, 512 + lo[1]:1024],
                                                          cmask[:, mi0 + 1, lo[1]:512], OP.mult), "dve", 1)
                        pv_pending = (p, pb, ev_exp_a, ev_exp, mask_ev_a, mask_ev, hl, npair, ctxps, cb, lo)
                    last_pv = emit_pv(*pv_pending)
                    W(SCA, "pe", last_pv)
                    W(SCA, "pe", ST["recip_pe"])
                    with nc.allow_low_precision(reason="bf16 softmax rowsum row feeding broadcast matmul"):
                        ev_rc = I(SCA.activation(reciprow[:, :], ctxps[64:65, :], AF.Copy,
                                                 scale=1.0), "act", 1)
                    ST["rcread"][cb] = ev_rc
                    norm_pending = (cb, ctxps, pglob, half, j, ev_rc)
                    ST["u"] += 1
            if norm_pending is not None:
                emit_norm(*norm_pending)
                norm_pending = None
            ST["rcread"] = [ST["C"]["act"], ST["C"]["act"]]
            if arm == 0:
                final0 = snapshot()
            else:
                assert ST == final0, ("attn", grp)

    prev_norm = [0, 0, 0, 0]
    ev_ln1 = []
    ln_prev = (0, 0)
    for tc in range(8):
        buf = tc % 4
        W(SYN, f"g_x{tc}", ev_xc[tc])
        W(SYN, "dve", prev_norm[buf])
        evt = None
        for g in range(NG):
            evt = I(SYN.dma_start(xfm[:, buf, g, :],
                                  x16[tc * 512:(tc + 1) * 512, g * 128:(g + 1) * 128],
                                  transpose=True), f"s_xt{tc % 4}", 16)
        src = (lambda g, b=buf: xfm[:, b, g, :])
        pw = [(f"s_xt{tc % 4}", evt)] + ([("s_const", ev_consts), ("dve", ev_vtones)] if tc == 0 else [])
        env, evv = ln_chunk(onesc[:, :], src, src, src,
                            (lambda g, t=tc: xln1[:, g, t * 512:(t + 1) * 512]),
                            pre_mm_wait=pw, pre_sq_wait=[(f"s_xt{tc % 4}", evt)], prev=ln_prev,
                            mid_emit=(lambda t=tc: kv_proj_chunk(0, t - 1)) if tc >= 1 else None)
        ln_prev = (evv, env)
        prev_norm[buf] = env
        ev_ln1.append(env)
    if limit == "ln1":
        return _finish([xln1[:, :, :]])

    act_guard = 0
    kv_proj_chunk(0, 7)
    bmach_k_reduces(0)
    for grp in range(4):
        if grp > 0:
            kv_proj(grp)
        q_proj(grp)
        if grp < 3:
            W(GPS, "pe", ST["C"]["pe"])
            load_wqkv(grp + 1)
        if limit == "proj0":
            return _finish([k4h[:, :, :], q4h[:, :, :], vt4[:, :, :, :]])
        ev_negB = b_machinery(grp, act_guard)
        if limit == "bmach0":
            return _finish([negB[:, :]])
        attn_group(grp, ev_negB)
        act_guard = ST["C"]["act"]
        if limit == "attn0":
            return _finish([ctxsb[:, :, :]])

    if limit == "attn":
        return _finish([ctxsb[:, :, :]])
    ev_attn_pe_end = ST["C"]["pe"]
    ev_attn_act_end = ST["C"]["act"]
    ev_attn_dve_end = ST["C"]["dve"]
    ev_ctx_done = max(max(ST["ctxread"]), max(ST["bcread"]))

    W(SYN, "pe", ev_attn_pe_end)
    snap = snapshot()
    final0 = None
    for arm in nc.Switch(engines=[SYN], index=[val_sp], n=2):
        if arm == 1:
            restore(snap)
        loc = 0
        parts = []
        for (a, b) in OWN_RANGES[arm]:
            if b - a > 1024:
                m = (a + b) // 2
                parts += [(a, m), (m, b)]
            else:
                parts += [(a, b)]
        for (a, b) in parts:
            for g in range(NG):
                I(SYN.dma_start(xown[:, g, loc:loc + (b - a)],
                                x16[a:b, g * 128:(g + 1) * 128], transpose=True), "s_xo", 16)
            loc += b - a
        if arm == 0:
            final0 = snapshot()
        else:
            assert ST == final0, "xown"
    ev_xown = ST["C"]["s_xo"]

    ev_wo_tc = [None] * 4
    ev_ln2 = [None] * 4
    first_wo = [True]

    def wo_chunk(tc):
        ev_evac = None
        for dc in range(8):
            ps, bn = proj_bank()
            W(TEN, "dve", ST["ev_ps"][bn])
            mm = None
            for pg in range(8):
                if first_wo[0] and pg == 0:
                    W(TEN, "g_wo", ev_wo)
                    W(TEN, "dve", ev_ctx_done)
                    W(TEN, "act", ev_attn_act_end)
                    first_wo[0] = False
                mm = TEN.matmul(ps[:, :], wosb[:, pg, dc * 128:(dc + 1) * 128],
                                ctxsb[:, pg, tc * 512:(tc + 1) * 512],
                                start=(pg == 0), stop=(pg == 7))
            ev = I(mm, "pe", 1)
            W(VEC, "pe", ev)
            if tc == 0 and dc == 0:
                W(VEC, "s_xo", ev_xown)
            ev_evac = I(VEC.scalar_tensor_tensor(x1[:, dc, tc * 512:(tc + 1) * 512], ps[:, :],
                                                 botsb[:, dc:dc + 1],
                                                 xown[:, dc, tc * 512:(tc + 1) * 512],
                                                 OP.add, OP.add), "dve", 1)
            ST["ev_ps"][bn] = ev_evac
        ev_wo_tc[tc] = ev_evac

    def gating_logits(tjs):
        for tj in tjs:
            ps, bn = proj_bank()
            W(TEN, "dve", ST["ev_ps"][bn])
            if tj == 0:
                W(TEN, "g_small", ev_gsmall)
            W(TEN, "dve", ev_ln2[tj // 4])
            mm = None
            for g in range(NG):
                mm = TEN.matmul(ps[:, 0:8], xln2[:, g, tj * 128:(tj + 1) * 128], wgsb[:, g, :],
                                start=(g == 0), stop=(g == 7))
            ev = I(mm, "pe", 1)
            W(VEC, "pe", ev)
            ev_lg = I(VEC.tensor_copy(logits[:, tj, :], ps[:, 0:8]), "dve", 1)
            ST["ev_ps"][bn] = ev_lg

    wo_chunk(0)
    for tc in range(4):
        pw = [("dve", ev_wo_tc[tc])]
        if tc == 0:
            pw.append(("act", ev_attn_act_end))
        src_f32 = (lambda g, t=tc: x1[:, g, t * 512:(t + 1) * 512])
        mid = (lambda t=tc: wo_chunk(t + 1)) if tc < 3 else None
        env, evv = ln_chunk(onescf[:, :], src_f32, src_f32, src_f32,
                            (lambda g, t=tc: xln2[:, g, t * 512:(t + 1) * 512]),
                            pre_mm_wait=pw, pre_sq_wait=[("dve", ev_wo_tc[tc])], prev=ln_prev,
                            mid_emit=mid)
        ln_prev = (evv, env)
        ev_ln2[tc] = env

    if limit == "wo":
        return _finish([x1[:, :, :]])

    gating_logits(range(0, 16))
    W(VEC, "dve", max(ST["ev_ps"]["C"], ST["ev_ps"]["D"]))
    def stage_fence(instr):
        ev = I(instr, "dve", 1)
        VEC.wait_ge(sems["dve"], ev)
        return ev
    for tj in range(15):
        VEC.tensor_reduce(gcolM[:, tj:tj + 1], logits[:, tj, :], AXL, OP.max)
    stage_fence(VEC.tensor_reduce(gcolM[:, 15:16], logits[:, 15, :], AXL, OP.max))
    ev_m1 = stage_fence(VEC.tensor_scalar(gcolN[:, :], gcolM[:, :], -1.0, None, OP.mult))
    W(SCA, "dve", ev_m1)
    ev_e = None
    for tj in range(16):
        ev_e = I(SCA.activation(gtE[:, tj, :], logits[:, tj, :], AF.Exp, bias=gcolN[:, tj:tj + 1]), "act", 1)
    for tj in range(15):
        VEC.tensor_scalar(gtK[:, tj, :], logits[:, tj, :], gcolM[:, tj:tj + 1], None, OP.is_ge)
    stage_fence(VEC.tensor_scalar(gtK[:, 15, :], logits[:, 15, :], gcolM[:, 15:16], None, OP.is_ge))
    for tj in range(15):
        VEC.scalar_tensor_tensor(gtK[:, tj, :], gtK[:, tj, :], -1e9, logits[:, tj, :], OP.mult, OP.add)
    stage_fence(VEC.scalar_tensor_tensor(gtK[:, 15, :], gtK[:, 15, :], -1e9, logits[:, 15, :], OP.mult, OP.add))
    for tj in range(15):
        VEC.tensor_reduce(gcolM[:, tj:tj + 1], gtK[:, tj, :], AXL, OP.max)
    stage_fence(VEC.tensor_reduce(gcolM[:, 15:16], gtK[:, 15, :], AXL, OP.max))
    for tj in range(15):
        VEC.tensor_scalar(gtK[:, tj, :], logits[:, tj, :], gcolM[:, tj:tj + 1], None, OP.is_ge)
    stage_fence(VEC.tensor_scalar(gtK[:, 15, :], logits[:, 15, :], gcolM[:, 15:16], None, OP.is_ge))
    W(VEC, "act", ev_e)
    for tj in range(15):
        VEC.tensor_tensor(gtE[:, tj, :], gtE[:, tj, :], gtK[:, tj, :], OP.mult)
    stage_fence(VEC.tensor_tensor(gtE[:, 15, :], gtE[:, 15, :], gtK[:, 15, :], OP.mult))
    for tj in range(15):
        VEC.tensor_reduce(gcolD[:, tj:tj + 1], gtE[:, tj, :], AXL, OP.add)
    stage_fence(VEC.tensor_reduce(gcolD[:, 15:16], gtE[:, 15, :], AXL, OP.add))
    stage_fence(VEC.reciprocal(gcolN[:, :], gcolD[:, :]))
    ev_wtok = []
    with nc.allow_low_precision(reason="bf16 gating weights for broadcast matmul (top-2 selection done in fp32)"):
        for tj in range(16):
            ev_wtok.append(I(VEC.tensor_scalar(wtokb[:, tj, :], gtE[:, tj, :], gcolN[:, tj:tj + 1], None, OP.mult), "dve", 1))

    ev_wT = None
    for tj in range(16):
        ps, bn = proj_bank()
        W(TEN, "dve", ST["ev_ps"][bn])
        W(TEN, "dve", ev_wtok[tj])
        ev = I(TEN.matmul(ps[0:8, 0:128], wtokb[:, tj, :], i128b[:, :], start=True, stop=True), "pe", 1)
        W(VEC, "pe", ev)
        ev_wT = I(VEC.tensor_copy(w_T[:, tj * 128:(tj + 1) * 128], ps[0:8, 0:128]), "dve", 1)
        ST["ev_ps"][bn] = ev_wT

    if limit == "ln2":
        return _finish([xln2[:, :, :], w_T[:, :]])
    ev_wo_mm_done = ST["C"]["pe"]
    ev_w1l = {}
    ev_w2l = {}

    def load_w1(e, quarters=(0, 1, 2, 3), wait_pe=0):
        W(GPS, "pe", wait_pe)
        evs = []
        for q in quarters:
            evs.append(I(GPS.dma_start(
                w1sb[:, :, q * 512:(q + 1) * 512],
                w1_h[e * D:(e + 1) * D, q * 512:(q + 1) * 512].rearrange("(r p) n -> p r n", p=128)), f"g_w1q{q}", 16))
        ev_w1l[e] = evs

    def load_w2(e, wait_pe=0):
        W(GPS, "pe", wait_pe)
        evs = []
        for hh in range(2):
            evs.append(I(GPS.dma_start(
                w2sb[:, hh * 8:(hh + 1) * 8, :],
                w2_h[e * NI + hh * 1024:e * NI + (hh + 1) * 1024, :].rearrange("(r p) n -> p r n", p=128)), f"g_w2h{hh}", 16))
        ev_w2l[e] = evs

    load_w1(0, wait_pe=ev_wo_mm_done)
    load_w2(0, wait_pe=ev_wo_mm_done)

    H1B = [(psA, 0), (psA, 512), (psB, 0), (psB, 512)]
    h1_gelu_ev = [0, 0, 0, 0]
    h1s = 0
    ev_y_tc_prev = 0
    ev_pool = [0, 0]
    ev_wb = [0, 0]
    wb_par = 0
    ev_gelu_tc = 0
    ev_w1_lastmm = 0
    ev_w2_lastmm = 0
    ev_castout = [0, 0]
    ev_pool_tc = [0, 0, 0, 0]
    ln2_act_end = ST["C"]["act"]

    n_experts = 1 if limit == "ffn1" else NE
    for e in range(n_experts):
        for tc in range(4):
            first_gelu = True
            for ic in range(16):
                ps, colo = H1B[h1s % 4]
                W(TEN, "act", h1_gelu_ev[h1s % 4])
                if e == 0 and tc == 0 and ic == 0:
                    W(TEN, "act", ln2_act_end)
                    W(TEN, "dve", ev_ln2[3])
                mm = None
                for r in range(NG):
                    if ic in (0, 8) and r == 0:
                        W(TEN, f"g_w1q{0 if ic < 8 else 2}", ev_w1l[e][0 if ic < 8 else 2])
                    if ic in (4, 12) and r == 0:
                        W(TEN, f"g_w1q{1 if ic < 8 else 3}", ev_w1l[e][1 if ic < 8 else 3])
                    mm = TEN.matmul(ps[:, colo:colo + 512], w1sb[:, r, ic * 128:(ic + 1) * 128],
                                    xln2[:, r, tc * 512:(tc + 1) * 512],
                                    start=(r == 0), stop=(r == 7))
                ev_mm = I(mm, "pe", 1)
                W(SCA, "pe", ev_mm)
                if first_gelu:
                    W(SCA, "pe", ev_y_tc_prev)
                    first_gelu = False
                h1_gelu_ev[h1s % 4] = I(SCA.activation(h1[:, ic, :], ps[:, colo:colo + 512], AF.Gelu,
                                                       bias=b1tsb[:, e, ic:ic + 1]), "act", 1)
                h1s += 1
            ev_gelu_tc = h1_gelu_ev[(h1s - 1) % 4]
            if tc == 3:
                ev_w1_lastmm = ST["C"]["pe"]
                if e < NE - 1:
                    load_w1(e + 1, wait_pe=ev_w1_lastmm)
            wps = (psE, psF)[wb_par % 2]
            W(TEN, "dve", ev_wb[wb_par % 2])
            if e == 0 and tc == 0:
                W(TEN, "dve", max(ST["ctxread"][0], ST["ctxread"][1], ST["bcread"][0], ST["bcread"][1], ev_wT))
            else:
                W(TEN, "dve", ev_wT)
            ev = I(TEN.matmul(wps[:, :], eyeg[:, e, :], w_T[:, tc * 512:(tc + 1) * 512],
                              start=True, stop=True), "pe", 1)
            W(VEC, "pe", ev)
            ev_wb[wb_par % 2] = VF(VEC.tensor_copy(wbsb[:, tc % 2, :], wps[:, :]))
            ev_wb_cur = ev_wb[wb_par % 2]
            wb_par += 1
            for dc in range(8):
                ps, bn = proj_bank()
                W(TEN, "dve", ST["ev_ps"][bn])
                if dc == 0:
                    W(TEN, "act", ev_gelu_tc)
                    W(TEN, "g_w2h1", ev_w2l[e][1])
                    W(TEN, "g_w2h0", ev_w2l[e][0])
                mm = None
                for r in range(16):
                    mm = TEN.matmul(ps[:, :], w2sb[:, r, dc * 128:(dc + 1) * 128], h1[:, r, :],
                                    start=(r == 0), stop=(r == 15))
                ev_y = I(mm, "pe", 1)
                yb = dc % 2
                W(VEC, "pe", ev_y)
                W(VEC, "pool", ev_pool[yb])
                ev_stt = I(VEC.scalar_tensor_tensor(ytmp[:, yb, :], ps[:, :], b2tsb[:, e, dc:dc + 1],
                                                    wbsb[:, tc % 2, :], OP.add, OP.mult), "dve", 1)
                ST["ev_ps"][bn] = ev_stt
                W(GPS, "dve", ev_stt)
                ev_pool[yb] = I(GPS.tensor_tensor(x1[:, dc, tc * 512:(tc + 1) * 512],
                                                  x1[:, dc, tc * 512:(tc + 1) * 512],
                                                  ytmp[:, yb, :], OP.add), "pool", 1)
            ev_y_tc_prev = ST["C"]["pe"]
            if tc == 3:
                ev_w2_lastmm = ST["C"]["pe"]
                if e < NE - 1:
                    load_w2(e + 1, wait_pe=ev_w2_lastmm)
            if e == NE - 1:
                ev_pool_tc[tc] = max(ev_pool)

    if limit in ("ffn", "ffn1"):
        return _finish([x1[:, :, :], h1[:, :, :], wbsb[:, :, :]])

    for tc in range(4):
        W(VEC, "pool", ev_pool_tc[tc])
        if tc == 0:
            W(VEC, "pe", ev_w1_lastmm)
        ev_x2b = I(VEC.tensor_copy(x2b[:, :, tc * 512:(tc + 1) * 512],
                                   x1[:, :, tc * 512:(tc + 1) * 512]), "dve", 1)
        W(SYN, "dve", ev_x2b)
        if tc == 0:
            W(SYN, "dve", ev_attn_dve_end)
            W(SYN, "pe", ev_attn_pe_end)
        ev_sp1 = I(SYN.dma_start(
            x2f16.rearrange("g p t -> p g t")[:, :, tc * 512:(tc + 1) * 512],
            x2b[:, :, tc * 512:(tc + 1) * 512]), f"s_x2s{tc}", 16)
        for tj4 in range(4):
            tj = tc * 4 + tj4
            ob = tj % 2
            W(SYN, f"s_x2s{tc}", ev_sp1)
            W(SYN, f"g_out{ob}", ev_castout[ob])
            ev_tr = I(SYN.dma_start(
                x2tok[:, ob, :],
                x2f16.rearrange("g p t -> (g p) t")[:, tj * 128:(tj + 1) * 128],
                transpose=True), f"s_ot{ob}", 16)
            W(GPS, f"s_ot{ob}", ev_tr)
            ev_castout[ob] = I(GPS.dma_start(yout[tj * 128:(tj + 1) * 128, :],
                                             x2tok[:, ob, :]), f"g_out{ob}", 16)
    for sname, cnt in ST["C"].items():
        if sname.startswith("g_") and cnt:
            GPS.wait_ge(sems[sname], cnt)
        if sname.startswith("s_") and cnt:
            SYN.wait_ge(sems[sname], cnt)

    ctx.close()
    return nc


_NC_CACHE = {}


def _get_nc():
    if "nc" not in _NC_CACHE:
        _NC_CACHE["nc"] = build_nc()
    return _NC_CACHE["nc"]


def kernel(L0_attn_embeddings, L0_attn_Wq, L0_attn_bq, L0_attn_Wk, L0_attn_bk,
           L0_attn_Wv, L0_attn_bv, L0_attn_Wo, L0_attn_bo,
           L0_ffn_Wg, L0_ffn_W1, L0_ffn_b1, L0_ffn_W2, L0_ffn_b2):
    f32 = np.float32
    x = np.ascontiguousarray(np.asarray(L0_attn_embeddings, dtype=f32))
    Wq = np.asarray(L0_attn_Wq, dtype=f32)
    Wk = np.asarray(L0_attn_Wk, dtype=f32)
    Wv = np.asarray(L0_attn_Wv, dtype=f32)
    bq = np.asarray(L0_attn_bq, dtype=f32)
    bk = np.asarray(L0_attn_bk, dtype=f32)
    bv = np.asarray(L0_attn_bv, dtype=f32)
    Wo = np.ascontiguousarray(np.asarray(L0_attn_Wo, dtype=f32))
    bo = np.asarray(L0_attn_bo, dtype=f32)
    Wg = np.ascontiguousarray(np.asarray(L0_ffn_Wg, dtype=f32))
    W1 = np.asarray(L0_ffn_W1, dtype=f32)
    W2 = np.asarray(L0_ffn_W2, dtype=f32)
    b1 = np.asarray(L0_ffn_b1, dtype=f32)
    b2 = np.asarray(L0_ffn_b2, dtype=f32)

    wq2 = np.ascontiguousarray(Wq.transpose(1, 0, 2).reshape(D, H * DK))
    wk2 = np.ascontiguousarray(Wk.transpose(1, 0, 2).reshape(D, H * DK))
    wv2 = np.ascontiguousarray(Wv.transpose(1, 0, 2).reshape(D, H * DK))
    w1f = np.ascontiguousarray(W1.reshape(NE * D, NI))
    w2f = np.ascontiguousarray(W2.reshape(NE * NI, D))
    bqr = np.ascontiguousarray(bq.reshape(1, 8, 128))
    bkr = np.ascontiguousarray(bk.reshape(1, 8, 128))
    bvr = np.ascontiguousarray(bv.reshape(1, 4, 256))
    bot = np.ascontiguousarray(bo.reshape(8, 128).T)
    b1t = np.ascontiguousarray(b1.reshape(8, 16, 128).transpose(2, 0, 1))
    b2t = np.ascontiguousarray(b2.reshape(8, 8, 128).transpose(2, 0, 1))

    s_idx = np.arange(128)[:, None]
    q_idx = np.arange(512)[None, :]
    cmask = np.stack([(128 * i + s_idx <= q_idx) for i in range(4)], axis=1).astype(ml_dtypes.bfloat16)
    cmask = np.ascontiguousarray(cmask)
    i128 = np.eye(128, dtype=ml_dtypes.bfloat16)
    eyeg = np.zeros((8, 8, 128), dtype=ml_dtypes.bfloat16)
    for e in range(8):
        eyeg[e, e, :] = 1
    shared = dict(wq2=wq2, wk2=wk2, wv2=wv2, wo=Wo, w1=w1f, w2=w2f, wg=Wg,
                  bqr=bqr, bkr=bkr, bvr=bvr, bot=bot, b1t=b1t, b2t=b2t,
                  cmask=cmask, i128=i128, eyeg=eyeg)

    nc = _get_nc()
    in_maps = []
    for c in range(8):
        m = dict(shared)
        m["xb"] = np.ascontiguousarray(x[c // 2])
        m["role"] = np.array([[c % 2]], dtype=np.int32)
        in_maps.append(m)
    res = run_bass_kernel_spmd(nc, in_maps, core_ids=list(range(8)))

    out = np.empty((B, S, D), dtype=f32)
    for c in range(8):
        y = res.results[c]["yout"]
        b, role = c // 2, c % 2
        loc = 0
        for (a, e2) in OWN_RANGES[role]:
            out[b, a:e2, :] = y[loc:loc + (e2 - a), :]
            loc += e2 - a
    return out
